# version 03
# baseline (speedup 1.0000x reference)
import contextlib
import dataclasses
import numpy as np
import concourse.bass as bass
import concourse.mybir as mybir
from concourse.bass import IndirectOffsetOnAxis
from concourse.bass_utils import run_bass_kernel_spmd

F32 = mybir.dt.float32
BF16 = mybir.dt.bfloat16
I32 = mybir.dt.int32
U32 = mybir.dt.uint32
AF = mybir.ActivationFunctionType
ALU = mybir.AluOpType
AX = mybir.AxisListType

ENGINES = ("sync", "gpsimd", "tensor", "vector", "scalar")

N_CORES = 8
B_PER_CORE = 2
S = 2048
NTOK = B_PER_CORE * S
D = 1024
NF = D // 128
NT = NTOK // 128
H0 = 16
DH = 64
CAPACITY = 10240
SLOT = 512
NSLOT = 24


class KB:

    _QUEUES = ("sync", "gq")
    _STREAM = {"gq": "gpsimd"}

    def __init__(self, nc):
        self.nc = nc
        self.ops = []
        self.cnt = {e: 0 for e in ENGINES + ("gq",)}
        self.sems = {}
        self.nblk = 0
        self.epoch = {q: 0 for q in self._QUEUES}
        self.ckpt = {q: [] for q in self._QUEUES}
        self._gq_unthrottled = 0
        self.bregs = {}
        self._xwait = {e: {} for e in self._SELF_EDGE}

    def _sem(self, e):
        if e not in self.sems:
            self.sems[e] = self.nc.alloc_semaphore(name=f"prog_{e}")
        return self.sems[e]

    def _esem(self, q):
        key = "ep_" + q
        if key not in self.sems:
            self.sems[key] = self.nc.alloc_semaphore(name=key)
        return self.sems[key]

    def _queue_epoch_wait(self, q, v):
        for cv, ep in reversed(self.ckpt[q]):
            if cv >= v:
                return ("ep_" + q, ep)
        cv = self.cnt[q]
        assert cv >= v, (q, cv, v)
        self.epoch[q] += 1
        ep = self.epoch[q]
        esem = self._esem(q)
        qsem = self._sem(q)

        def clo(obj, cv=cv, ep=ep):
            if cv > 0:
                obj.wait_ge(qsem, cv)
            obj.sem_inc(esem, 1)
        self.ops.append((self._STREAM.get(q, q), clo))
        self.ckpt[q].append((cv, ep))
        return ("ep_" + q, ep)

    _SELF_EDGE = ("vector", "scalar", "gpsimd")

    def emit(self, eng, builder, waits=(), inc=0):
        if builder is not None and eng in self._SELF_EDGE:
            if inc == 0:
                inc = 1
            if self.cnt[eng] > 0:
                waits = ((eng, self.cnt[eng]),) + tuple(waits)
            extra = []
            for e2 in self._SELF_EDGE:
                if e2 != eng and self.cnt[e2] > self._xwait[eng].get(e2, 0):
                    extra.append((e2, self.cnt[e2]))
                    self._xwait[eng][e2] = self.cnt[e2]
            waits = tuple(waits) + tuple(extra)
        fixed = []
        for we, wv in waits:
            if wv <= 0:
                continue
            if we in self._QUEUES and we != eng:
                fixed.append(self._queue_epoch_wait(we, wv))
            elif we == eng and we in self._QUEUES:
                fixed.append((we, self.cnt[we]))
            else:
                fixed.append((we, wv))
        waits = tuple(fixed)
        sem = self._sem(eng) if inc else None
        for we, _ in waits:
            self._sem(we)

        def clo(obj, waits=waits, inc=inc, sem=sem, builder=builder):
            for we, wv in waits:
                obj.wait_ge(self.sems[we], wv)
            if builder is None:
                return
            ins = builder(obj)
            if inc:
                ins.then_inc(sem, inc)
        self.ops.append((self._STREAM.get(eng, eng), clo))
        if inc:
            self.cnt[eng] += inc
        if eng == "gq" and inc:
            self._gq_unthrottled += 1
            if self._gq_unthrottled >= 16:
                self._gq_unthrottled = 0
                lag = self.cnt["gq"] - 16 * 16
                if lag > 0:
                    qsem = self._sem("gq")

                    def thr(obj, lag=lag, qsem=qsem):
                        obj.wait_ge(qsem, lag)
                    self.ops.append(("gpsimd", thr))
        return self.cnt[eng]

    def wait(self, eng, waits):
        self.emit(eng, None, waits=waits, inc=0)

    def breg(self, value, eng="gpsimd"):
        key = (eng, value)
        if key not in self.bregs:
            holder = [None]

            def mk(obj, holder=holder, value=value):
                holder[0] = obj.to_reg(value)
                return obj.nop()
            self.emit(eng, mk, inc=1)
            self.bregs[key] = holder
        return self.bregs[key]

    def val(self, eng):
        return self.cnt[eng]

    def barrier(self, engs=ENGINES):
        engs = tuple(engs)
        targets = {}
        for e in engs:
            v = self.cnt[e]
            sem = self._sem(e)
            extra = []
            if e == "gpsimd" and self.cnt["gq"] > 0:
                extra.append((self._sem("gq"), self.cnt["gq"]))

            def clo(obj, v=v, sem=sem, extra=tuple(extra)):
                if v > 0:
                    obj.wait_ge(sem, v)
                for s2, v2 in extra:
                    obj.wait_ge(s2, v2)
                obj.sem_inc(sem, 1)
            self.ops.append((self._STREAM.get(e, e), clo))
            self.cnt[e] += 1
            targets[e] = self.cnt[e]
        for e in engs:
            self.wait(e, [(e2, targets[e2]) for e2 in engs if e2 != e])

    def drain(self):
        self.wait("sync", [(e, self.cnt[e]) for e in ENGINES])

    def fence(self, writer="tensor", waiters=("vector", "scalar", "gpsimd")):
        v = self.cnt[writer]
        for e in waiters:
            self.wait(e, [(writer, v)])

    def flush(self, barrier=True):
        if barrier:
            self.barrier()
        if not self.ops:
            return
        ops, self.ops = self.ops, []
        self.nblk += 1
        with self.nc.Block(f"blk{self.nblk}") as block:
            for eng in ENGINES:
                mine = [c for (e, c) in ops if e == eng]
                if not mine:
                    continue

                def run(obj, mine=mine):
                    for c in mine:
                        c(obj)
                getattr(block, eng)(run)




def _mk(ctx, nc, kind, name, shape, dt):
    f = nc.sbuf_tensor if kind == "s" else nc.psum_tensor
    return ctx.enter_context(f(name, shape, dt))

def cdiv(a, b):
    return (a + b - 1) // b


def fview(dram_ap, f, c0, c1):
    return dram_ap[f * 128:(f + 1) * 128, c0:c1]



def make_const_arrays():
    import ml_dtypes
    c = {}
    c["c_ident"] = np.eye(128, dtype=np.float32)
    c["c_ones1"] = np.ones((1, 128), dtype=np.float32)
    c["c_ones128"] = np.ones((128, 128), dtype=np.float32)
    c["c_identb"] = np.eye(128).astype(ml_dtypes.bfloat16)
    c["c_iota8"] = np.tile(np.arange(8, dtype=np.float32), (128, 1))
    c["c_iota24"] = np.arange(NSLOT, dtype=np.float32).reshape(1, NSLOT)
    c["c_iota128"] = np.arange(128, dtype=np.float32).reshape(128, 1)
    c["c_iota128r"] = np.arange(128, dtype=np.float32).reshape(1, 128)
    return c


class Consts:

    def __init__(self, kb, nc):
        self.kb = kb
        self.nc = nc
        self.tiles = {}
        v = 0
        for name, shape, dt in (("c_ident", [128, 128], F32), ("c_ones1", [1, 128], F32),
                                ("c_ones128", [128, 128], F32), ("c_identb", [128, 128], BF16),
                                ("c_iota8", [128, 8], F32), ("c_iota24", [1, NSLOT], F32),
                                ("c_iota128", [128, 1], F32), ("c_iota128r", [1, 128], F32)):
            ap = nc.dram_tensor(name, shape, dt, kind="ExternalInput").ap()
            t = nc.alloc_sbuf_tensor("s" + name, shape, dt)
            v = kb.emit("sync", lambda o, t=t, ap=ap: o.dma_start(out=t.ap(), in_=ap), inc=16)
            self.tiles[name] = t
        for e in ("tensor", "vector", "scalar", "gpsimd"):
            kb.wait(e, [("sync", v)])

    @property
    def ident(self):
        return self.tiles["c_ident"].ap()

    @property
    def ones1(self):
        return self.tiles["c_ones1"].ap()

    @property
    def ones128(self):
        return self.tiles["c_ones128"].ap()

    @property
    def ident_bf(self):
        return self.tiles["c_identb"].ap()

    @property
    def iota8(self):
        return self.tiles["c_iota8"].ap()

    @property
    def iota24(self):
        return self.tiles["c_iota24"].ap()

    @property
    def iota128(self):
        return self.tiles["c_iota128"].ap()

    @property
    def iota128r(self):
        return self.tiles["c_iota128r"].ap()



def ph_a(kb, nc, cst, x_nat, gamma_row, beta_row, xT, xnT, xnT_bf, xn2T_bf):
    with (
        nc.sbuf_tensor("pa_xt0", [128, D], F32) as xt0,
        nc.sbuf_tensor("pa_xt1", [128, D], F32) as xt1,
        nc.sbuf_tensor("pa_xn", [128, D], F32) as xn_t,
        nc.sbuf_tensor("pa_xn2", [128, D], F32) as xn2_t,
        nc.sbuf_tensor("pa_sq", [128, D], F32) as sq_t,
        nc.sbuf_tensor("pa_stats", [128, 8], F32) as st,
        nc.sbuf_tensor("pa_gb", [128, 2 * D], F32) as gb,
        nc.sbuf_tensor("pa_grow", [1, 2 * D], F32) as grow,
        nc.sbuf_tensor("pa_tp32", [128, D], F32) as tp_f32,
        nc.sbuf_tensor("pa_tpb", [128, D], BF16) as tp_bf,
        nc.psum_tensor("pa_ps_t", [128, 1024], F32) as ps_t,
        nc.psum_tensor("pa_ps_b", [128, 1024], F32) as ps_b,
    ):
        ident, ones1 = cst.ident, cst.ones1
        ld = kb.emit("sync", lambda o: o.dma_start(out=grow[0:1, 0:D], in_=gamma_row), inc=16)
        ld = kb.emit("sync", lambda o: o.dma_start(out=grow[0:1, D:2 * D], in_=beta_row), inc=16)
        tv = 0
        for h in range(2):
            tv = kb.emit("tensor", lambda o, h=h: o.matmul(
                ps_b[:, h * 512:(h + 1) * 512], ones1, grow[0:1, h * 512:(h + 1) * 512],
                start=True, stop=True),
                waits=[("sync", ld)] if h == 0 else [], inc=1)
        vv = kb.emit("vector", lambda o: o.tensor_copy(gb[:, 0:D], ps_b[:, 0:D]),
                     waits=[("tensor", tv)], inc=1)
        for h in range(2):
            tv = kb.emit("tensor", lambda o, h=h: o.matmul(
                ps_b[:, h * 512:(h + 1) * 512], ones1, grow[0:1, D + h * 512:D + (h + 1) * 512],
                start=True, stop=True),
                waits=[("vector", vv)] if h == 0 else [], inc=1)
        vv = kb.emit("vector", lambda o: o.tensor_copy(gb[:, D:2 * D], ps_b[:, 0:D]),
                     waits=[("tensor", tv)], inc=1)

        bufs = [xt0, xt1]
        last_use = [(0, 0), (0, 0)]
        for i in range(NT):
            bi = i % 2
            xt = bufs[bi]
            vprev, tprev = last_use[bi]
            ld = kb.emit("sync", lambda o, i=i, xt=xt: o.dma_start(
                out=xt[:], in_=x_nat[i * 128:(i + 1) * 128, :]),
                waits=[("vector", vprev), ("tensor", tprev)], inc=16)
            vv = kb.emit("vector", lambda o, xt=xt: o.reduce_sum(st[:, 0:1], xt[:], axis=AX.X),
                         waits=[("sync", ld)], inc=1)
            vv = kb.emit("vector", lambda o: o.tensor_scalar(
                st[:, 1:2], st[:, 0:1], 1.0 / D, None, op0=ALU.mult), inc=1)
            vv = kb.emit("vector", lambda o, xt=xt: o.tensor_scalar(
                xn_t[:], xt[:], st[:, 1:2], None, op0=ALU.subtract), inc=1)
            sv = kb.emit("scalar", lambda o: o.activation(
                sq_t[:], xn_t[:], AF.Square, accum_out=st[:, 2:3]),
                waits=[("vector", vv)], inc=1)
            vv = kb.emit("vector", lambda o: o.tensor_scalar(
                st[:, 3:4], st[:, 2:3], 1.0 / D, 1e-5, op0=ALU.mult, op1=ALU.add),
                waits=[("scalar", sv)], inc=1)
            sv = kb.emit("scalar", lambda o: o.activation(
                st[:, 5:6], st[:, 3:4], AF.Sqrt),
                waits=[("vector", vv)], inc=1)
            vv = kb.emit("vector", lambda o: o.reciprocal(st[:, 4:5], st[:, 5:6]),
                         waits=[("scalar", sv)], inc=1)
            vv = kb.emit("vector", lambda o: o.tensor_scalar(
                xn_t[:], xn_t[:], st[:, 4:5], None, op0=ALU.mult), inc=1)
            vxn = vv
            vv = kb.emit("vector", lambda o: o.reduce_sum(st[:, 0:1], xn_t[:], axis=AX.X), inc=1)
            vv = kb.emit("vector", lambda o: o.tensor_scalar(
                st[:, 1:2], st[:, 0:1], 1.0 / D, None, op0=ALU.mult), inc=1)
            vv = kb.emit("vector", lambda o: o.tensor_scalar(
                xn2_t[:], xn_t[:], st[:, 1:2], None, op0=ALU.subtract), inc=1)
            sv = kb.emit("scalar", lambda o: o.activation(
                sq_t[:], xn2_t[:], AF.Square, accum_out=st[:, 2:3]),
                waits=[("vector", vv)], inc=1)
            vv = kb.emit("vector", lambda o: o.tensor_scalar(
                st[:, 3:4], st[:, 2:3], 1.0 / D, 1e-5, op0=ALU.mult, op1=ALU.add),
                waits=[("scalar", sv)], inc=1)
            sv = kb.emit("scalar", lambda o: o.activation(
                st[:, 5:6], st[:, 3:4], AF.Sqrt),
                waits=[("vector", vv)], inc=1)
            vv = kb.emit("vector", lambda o: o.reciprocal(st[:, 4:5], st[:, 5:6]),
                         waits=[("scalar", sv)], inc=1)
            vv = kb.emit("vector", lambda o: o.tensor_scalar(
                xn2_t[:], xn2_t[:], st[:, 4:5], None, op0=ALU.mult), inc=1)
            vv = kb.emit("vector", lambda o: o.scalar_tensor_tensor(
                xn2_t[:], xn2_t[:], 1.0, gb[:, 0:D], op0=ALU.mult, op1=ALU.mult), inc=1)
            vv = kb.emit("vector", lambda o: o.tensor_add(xn2_t[:], xn2_t[:], gb[:, D:2 * D]), inc=1)
            vxn2 = vv

            tv0 = 0
            for f in range(NF):
                tv0 = kb.emit("tensor", lambda o, f=f, xt=xt: o.transpose(
                    ps_t[:, f * 128:(f + 1) * 128], xt[:, f * 128:(f + 1) * 128], ident),
                    waits=[("sync", ld)] if f == 0 else [], inc=1)
            vv = kb.emit("vector", lambda o: o.tensor_copy(tp_f32[:], ps_t[:]),
                         waits=[("tensor", tv0)], inc=1)
            s_x = 0
            for f in range(NF):
                s_x = kb.emit("sync", lambda o, f=f, i=i: o.dma_start(
                    out=fview(xT, f, i * 128, (i + 1) * 128),
                    in_=tp_f32[:, f * 128:(f + 1) * 128]),
                    waits=[("vector", vv)] if f == 0 else [], inc=16)
            tv1 = 0
            for f in range(NF):
                tv1 = kb.emit("tensor", lambda o, f=f: o.transpose(
                    ps_t[:, f * 128:(f + 1) * 128], xn_t[:, f * 128:(f + 1) * 128], ident),
                    waits=[("vector", vxn2), ("sync", s_x)] if f == 0 else [], inc=1)
            vv2 = kb.emit("vector", lambda o: o.tensor_copy(tp_f32[:], ps_t[:]),
                          waits=[("tensor", tv1)], inc=1)
            s_xn = 0
            g_xn = 0
            for f in range(NF):
                s_xn = kb.emit("sync", lambda o, f=f, i=i: o.dma_start(
                    out=fview(xnT, f, i * 128, (i + 1) * 128),
                    in_=tp_f32[:, f * 128:(f + 1) * 128]),
                    waits=[("vector", vv2)] if f == 0 else [], inc=16)
                g_xn = kb.emit("gq", lambda o, f=f, i=i: o.dma_start(
                    out=fview(xnT_bf, f, i * 128, (i + 1) * 128),
                    in_=tp_f32[:, f * 128:(f + 1) * 128]),
                    waits=[("vector", vv2)] if f == 0 else [], inc=16)
            tv2 = 0
            for f in range(NF):
                tv2 = kb.emit("tensor", lambda o, f=f: o.transpose(
                    ps_t[:, f * 128:(f + 1) * 128], xn2_t[:, f * 128:(f + 1) * 128], ident),
                    waits=[("vector", vv2)] if f == 0 else [], inc=1)
            vv3 = kb.emit("vector", lambda o: o.tensor_copy(tp_bf[:], ps_t[:]),
                          waits=[("tensor", tv2), ("sync", s_xn), ("gq", g_xn)], inc=1)
            s_xn2 = 0
            for f in range(NF):
                s_xn2 = kb.emit("sync", lambda o, f=f, i=i: o.dma_start(
                    out=fview(xn2T_bf, f, i * 128, (i + 1) * 128),
                    in_=tp_bf[:, f * 128:(f + 1) * 128]),
                    waits=[("vector", vv3)] if f == 0 else [], inc=16)
            last_use[bi] = (vxn, tv0)
            kb.wait("vector", [("sync", s_xn2)])
        kb.flush()



def col_bias(kb, nc, cst, row_sb, off, out_tile, ocol0, ps128, scale=None):
    for f in range(NF):
        tv = kb.emit("tensor", lambda o, f=f: o.transpose(
            ps128[:, 0:1], row_sb[0:1, off + f * 128:off + (f + 1) * 128],
            cst.ident[0:1, 0:1]), inc=1)
        vv = kb.emit("vector", lambda o, f=f: o.tensor_copy(
            out_tile[:, ocol0 + f:ocol0 + f + 1], ps128[:, 0:1]),
            waits=[("tensor", tv)], inc=1)
        kb.wait("tensor", [("vector", vv)])
    if scale is not None:
        kb.emit("vector", lambda o: o.tensor_scalar(
            out_tile[:, ocol0:ocol0 + NF], out_tile[:, ocol0:ocol0 + NF],
            scale, None, op0=ALU.mult), inc=1)


def row_bcast(kb, nc, cst, row_sb, off, width, ps, pbase=0, prows=128):
    tv = 0
    for c0 in range(0, width, 512):
        w = min(512, width - c0)
        tv = kb.emit("tensor", lambda o, c0=c0, w=w: o.matmul(
            ps[pbase:pbase + prows, c0:c0 + w], cst.ones1[0:1, 0:prows],
            row_sb[0:1, off + c0:off + c0 + w], start=True, stop=True), inc=1)
    return tv



def ph_b(kb, nc, cst, xn2T_bf, wq, wk, wv, wo, bq, bk, bv, bo,
         relb, relbT, mt, mask1, mask2, s0T, otmp, rowtmp):
    ones1 = cst.ones1
    with (
        nc.sbuf_tensor("pb_q", [128, 8 * 2048], BF16) as q_sb,
        nc.sbuf_tensor("pb_k", [128, 8 * 2048], BF16) as k_sb,
        nc.sbuf_tensor("pb_v", [128, 16 * 1024], BF16) as v_sb,
    ):
        for b in range(B_PER_CORE):
            tb = b * S
            kb.wait("tensor", [("vector", kb.val("vector")), ("scalar", kb.val("scalar"))])
            with (
                nc.sbuf_tensor(f"pb_w3{b}", [128, 3 * 8 * 1024], BF16) as w3,
                nc.sbuf_tensor(f"pb_xn{b}", [128, 8 * 2048], BF16) as xn_r,
                nc.sbuf_tensor(f"pb_brow{b}", [1, 3 * D], F32) as brow,
                nc.sbuf_tensor(f"pb_bcol{b}", [128, 2 * NF], F32) as bcol,
                nc.sbuf_tensor(f"pb_bvbc{b}", [128, D], F32) as bvbc,
                nc.psum_tensor(f"pb_ps1{b}", [128, 512], F32) as ps1,
                nc.psum_tensor(f"pb_ps2{b}", [128, 512], F32) as ps2,
            ):
                gw = 0
                for wi, w in enumerate((wq, wk, wv)):
                    for kc in range(NF):
                        gw = kb.emit("gq", lambda o, wi=wi, w=w, kc=kc: o.dma_start(
                            out=w3[:, (wi * NF + kc) * 1024:(wi * NF + kc + 1) * 1024],
                            in_=w[kc * 128:(kc + 1) * 128, :]), inc=16)
                ld = kb.emit("sync", lambda o: o.dma_start(out=brow[0:1, 0:D], in_=bq), inc=16)
                ld = kb.emit("sync", lambda o: o.dma_start(out=brow[0:1, D:2 * D], in_=bk), inc=16)
                ld = kb.emit("sync", lambda o: o.dma_start(out=brow[0:1, 2 * D:3 * D], in_=bv), inc=16)
                kb.wait("tensor", [("sync", ld)])
                kb.wait("vector", [("sync", ld)])
                col_bias(kb, nc, cst, brow, 0, bcol, 0, ps1)
                col_bias(kb, nc, cst, brow, D, bcol, NF, ps1)
                for oc in range(2):
                    tv = kb.emit("tensor", lambda o, oc=oc: o.matmul(
                        ps2[:, 0:512], ones1, brow[0:1, 2 * D + oc * 512:2 * D + (oc + 1) * 512],
                        start=True, stop=True),
                        waits=[("vector", kb.val("vector"))], inc=1)
                    vv = kb.emit("vector", lambda o, oc=oc: o.tensor_copy(
                        bvbc[:, oc * 512:(oc + 1) * 512], ps2[:, 0:512]),
                        waits=[("tensor", tv)], inc=1)
                ldx = 0
                for kc in range(NF):
                    ldx = kb.emit("sync", lambda o, kc=kc: o.dma_start(
                        out=xn_r[:, kc * 2048:(kc + 1) * 2048],
                        in_=xn2T_bf[kc * 128:(kc + 1) * 128, tb:tb + S]), inc=16)
                kb.wait("tensor", [("sync", ldx), ("gq", gw)])
                pv0 = kb.val("vector")
                plast = {0: pv0, 1: pv0}
                for u, (ft, ch, which) in enumerate(
                        [(ft, ch, w_) for ft in range(NF) for ch in range(4) for w_ in (0, 1)]):
                    ps = (ps1, ps2)[u % 2]
                    dst = (q_sb, k_sb)[which]
                    bias_off = (0, NF)[which]
                    tv = 0
                    for kc in range(NF):
                        tv = kb.emit("tensor", lambda o, which=which, ft=ft, ch=ch, kc=kc, ps=ps: o.matmul(
                            ps[:, 0:512],
                            w3[:, (which * NF + kc) * 1024 + ft * 128:
                               (which * NF + kc) * 1024 + (ft + 1) * 128],
                            xn_r[:, kc * 2048 + ch * 512:kc * 2048 + (ch + 1) * 512],
                            start=(kc == 0), stop=(kc == NF - 1)),
                            waits=[("vector", plast[u % 2])] if kc == 0 else [], inc=1)
                    scl = 0.125 if which == 0 else 1.0
                    sv = kb.emit("vector", lambda o, which=which, ft=ft, ch=ch, dst=dst, bias_off=bias_off, ps=ps, scl=scl: o.tensor_scalar(
                        dst[:, ft * 2048 + ch * 512:ft * 2048 + (ch + 1) * 512],
                        ps[:, 0:512], bcol[:, bias_off + ft:bias_off + ft + 1], scl,
                        op0=ALU.add, op1=ALU.mult), waits=[("tensor", tv)], inc=1)
                    plast[u % 2] = sv
                vlast = {0: plast[0], 1: plast[1]}
                for u, (tt, oc) in enumerate([(tt, oc) for tt in range(16) for oc in (0, 1)]):
                    ps = (ps1, ps2)[u % 2]
                    tv = 0
                    for kc in range(NF):
                        tv = kb.emit("tensor", lambda o, tt=tt, oc=oc, kc=kc, ps=ps: o.matmul(
                            ps[:, 0:512],
                            xn_r[:, kc * 2048 + tt * 128:kc * 2048 + (tt + 1) * 128],
                            w3[:, (2 * NF + kc) * 1024 + oc * 512:
                               (2 * NF + kc) * 1024 + (oc + 1) * 512],
                            start=(kc == 0), stop=(kc == NF - 1)),
                            waits=([("vector", max(plast[u % 2], vlast[u % 2]))]
                                   if kc == 0 else []), inc=1)
                    vv = kb.emit("vector", lambda o, tt=tt, oc=oc, ps=ps: o.tensor_tensor(
                        v_sb[:, tt * 1024 + oc * 512:tt * 1024 + (oc + 1) * 512],
                        ps[:, 0:512], bvbc[:, oc * 512:(oc + 1) * 512], op=ALU.add),
                        waits=[("tensor", tv)], inc=1)
                    vlast[u % 2] = vv
                kb.flush()
            with contextlib.ExitStack() as ctx:
                g_sb = _mk(ctx, nc, "s", f"pb_g{b}", [32, 2 * 2048], BF16)
                mt_sb = _mk(ctx, nc, "s", f"pb_mt{b}", [32, 2048], BF16)
                rb_sb = _mk(ctx, nc, "s", f"pb_rb{b}", [32, 64], BF16)
                m1_sb = _mk(ctx, nc, "s", f"pb_m1{b}", [128, 4 * 512], F32)
                m2_sb = _mk(ctx, nc, "s", f"pb_m2{b}", [128, 4 * 512], F32)
                mx_sb = _mk(ctx, nc, "s", f"pb_mx{b}", [128, 4], F32)
                nma = _mk(ctx, nc, "s", f"pb_nma{b}", [128, 16], F32)
                nm16 = _mk(ctx, nc, "s", f"pb_nm16{b}", [16, 128], F32)
                nmrow = _mk(ctx, nc, "s", f"pb_nmrow{b}", [1, 2048], F32)
                qaug = _mk(ctx, nc, "s", f"pb_qaug{b}", [65, 2048], BF16)
                kaug = _mk(ctx, nc, "s", f"pb_kaug{b}", [65, 128], BF16)
                vaug = _mk(ctx, nc, "s", f"pb_vaug{b}", [128, 65], BF16)
                onecol = _mk(ctx, nc, "s", f"pb_onecol{b}", [128, 1], BF16)
                pt_sb = _mk(ctx, nc, "s", f"pb_pt{b}", [128, 2 * 512], BF16)
                ot_sb = _mk(ctx, nc, "s", f"pb_ot{b}", [64, 512], BF16)
                rcp_sb = _mk(ctx, nc, "s", f"pb_rcp{b}", [1, 512], F32)
                ps_sA = _mk(ctx, nc, "p", f"pb_ps_sA{b}", [128, 512], F32)
                ps_sB = _mk(ctx, nc, "p", f"pb_ps_sB{b}", [128, 512], F32)
                ps_o = _mk(ctx, nc, "p", f"pb_ps_o{b}", [65, 2048], F32)
                ps_g = _mk(ctx, nc, "p", f"pb_ps_g{b}", [64, 512], F32)
                ps_bc = _mk(ctx, nc, "p", f"pb_ps_bc{b}", [64, 512], F32)
                ld = kb.emit("gq", lambda o: o.dma_start(out=mt_sb[:], in_=mt), inc=16)
                ldm = 0
                for a in range(4):
                    ldm = kb.emit("sync", lambda o, a=a: o.dma_start(
                        out=m1_sb[:, a * 512:(a + 1) * 512], in_=mask1[a]), inc=16)
                    ldm = kb.emit("sync", lambda o, a=a: o.dma_start(
                        out=m2_sb[:, a * 512:(a + 1) * 512], in_=mask2[a]), inc=16)
                vv = kb.emit("vector", lambda o: o.memset(onecol[:], 1.0), inc=1)
                vv = kb.emit("vector", lambda o: o.memset(kaug[64:65, :], 1.0), inc=1)
                vv = kb.emit("vector", lambda o: o.memset(vaug[:, 64:65], 1.0), inc=1)
                kb.wait("tensor", [("sync", ldm), ("gq", ld), ("vector", vv)])
                kb.wait("scalar", [("vector", vv)])
                for h in range(H0):
                    hb = (h % 2) * 64
                    hf = h // 2
                    qsl = q_sb[hb:hb + 64, hf * 2048:(hf + 1) * 2048]
                    ksl = k_sb[hb:hb + 64, hf * 2048:(hf + 1) * 2048]
                    ldr = kb.emit("gq", lambda o, h=h: o.dma_start(
                        out=rb_sb[:, 0:32], in_=relbT[h]), inc=16)
                    ldr = kb.emit("gq", lambda o, h=h: o.dma_start(
                        out=rb_sb[:, 32:64], in_=relb[h]), inc=16)
                    gvv = 0
                    for gi in range(2):
                        for ch in range(4):
                            tv = kb.emit("tensor", lambda o, gi=gi, ch=ch: o.matmul(
                                ps_g[0:32, 0:512], rb_sb[:, gi * 32:(gi + 1) * 32],
                                mt_sb[:, ch * 512:(ch + 1) * 512], start=True, stop=True),
                                waits=[("gq", ldr), ("vector", gvv)], inc=1)
                            gvv = kb.emit("vector", lambda o, gi=gi, ch=ch: o.tensor_copy(
                                g_sb[:, gi * 2048 + ch * 512:gi * 2048 + (ch + 1) * 512],
                                ps_g[0:32, 0:512]), waits=[("tensor", tv)], inc=1)
                            kb.wait("tensor", [("vector", gvv)])
                    vred = {0: 0, 1: 0}
                    unit = 0
                    for qt in range(16):
                        nch = qt // 4 + 1
                        for c in range(nch):
                            par1 = unit % 2
                            unit += 1
                            ps = (ps_sA, ps_sB)[par1]
                            tv = kb.emit("tensor", lambda o, qt=qt, c=c, ps=ps, qsl=qsl, ksl=ksl: o.matmul(
                                ps[:, 0:512], qsl[:, qt * 128:(qt + 1) * 128],
                                ksl[:, c * 512:(c + 1) * 512], start=True, stop=False),
                                waits=[("vector", vred[par1])], inc=1)
                            tv = kb.emit("tensor", lambda o, qt=qt, c=c, ps=ps: o.matmul(
                                ps[:, 0:512], mt_sb[:, qt * 128:(qt + 1) * 128],
                                g_sb[:, c * 512:(c + 1) * 512],
                                start=False, stop=True), inc=1)
                            vv = 0
                            if c == nch - 1:
                                vv = kb.emit("vector", lambda o, qt=qt, ps=ps: o.tensor_add(
                                    ps[:, 0:512], ps[:, 0:512],
                                    m1_sb[:, (qt % 4) * 512:(qt % 4 + 1) * 512]),
                                    waits=[("tensor", tv)], inc=1)
                            col = 0 if c == 0 else 1
                            vv = kb.emit("vector", lambda o, ps=ps, col=col: o.reduce_max(
                                mx_sb[:, col:col + 1], ps[:, 0:512], axis=AX.X),
                                waits=[("tensor", tv)], inc=1)
                            if c > 0:
                                vv = kb.emit("vector", lambda o: o.tensor_max(
                                    mx_sb[:, 0:1], mx_sb[:, 0:1], mx_sb[:, 1:2]), inc=1)
                            vred[par1] = vv
                        vv = kb.emit("vector", lambda o, qt=qt: o.tensor_scalar(
                            nma[:, qt:qt + 1], mx_sb[:, 0:1], -1.0, None, op0=ALU.mult), inc=1)
                    tv = kb.emit("tensor", lambda o: o.transpose(
                        ps_g[0:16, 0:128], nma[:], cst.ident),
                        waits=[("vector", vv)], inc=1)
                    vv = kb.emit("vector", lambda o: o.tensor_copy(nm16[:], ps_g[0:16, 0:128]),
                                 waits=[("tensor", tv)], inc=1)
                    sd1 = kb.emit("sync", lambda o: o.dma_start(out=rowtmp, in_=nm16[:]),
                                  waits=[("vector", vv)], inc=16)
                    sd = kb.emit("sync", lambda o: o.dma_start(
                        out=nmrow[:], in_=rowtmp.rearrange("a b -> (a b)").unsqueeze(0)),
                        waits=[("sync", sd1)], inc=16)
                    vv = kb.emit("vector", lambda o: o.tensor_copy(qaug[64:65, :], nmrow[:]),
                                 waits=[("sync", sd)], inc=1)
                    sv_exp = {0: 0, 1: 0}
                    tv_pv = {0: 0, 1: 0}
                    vv_kaug = 0
                    tv_last_o = 0
                    for kt in range(16):
                        vv_kaug = kb.emit("vector", lambda o, kt=kt, ksl=ksl: o.tensor_copy(
                            kaug[0:64, :], ksl[:, kt * 128:(kt + 1) * 128]),
                            waits=[("tensor", tv_pv[0]), ("tensor", tv_pv[1])], inc=1)
                        vv_kaug = kb.emit("vector", lambda o, kt=kt, h=h: o.tensor_copy(
                            vaug[:, 0:64], v_sb[:, kt * 1024 + h * 64:kt * 1024 + h * 64 + 64]),
                            inc=1)
                        for qc in range(kt // 4, 4):
                            par = (kt + qc) % 2
                            ps = (ps_sA, ps_sB)[par]
                            tv = kb.emit("tensor", lambda o, kt=kt, qc=qc, ps=ps: o.matmul(
                                ps[:, 0:512], kaug[:, 0:128],
                                qaug[:, qc * 512:(qc + 1) * 512], start=True, stop=False),
                                waits=[("vector", vv_kaug), ("scalar", sv_exp[par])], inc=1)
                            tv = kb.emit("tensor", lambda o, kt=kt, qc=qc, ps=ps: o.matmul(
                                ps[:, 0:512], mt_sb[:, kt * 128:(kt + 1) * 128],
                                g_sb[:, 2048 + qc * 512:2048 + (qc + 1) * 512],
                                start=False, stop=True), inc=1)
                            vv = 0
                            if kt // 4 == qc:
                                vv = kb.emit("vector", lambda o, kt=kt, ps=ps: o.tensor_add(
                                    ps[:, 0:512], ps[:, 0:512],
                                    m2_sb[:, (kt % 4) * 512:(kt % 4 + 1) * 512]),
                                    waits=[("tensor", tv)], inc=1)
                            sv = kb.emit("scalar", lambda o, ps=ps, par=par: o.activation(
                                pt_sb[:, par * 512:(par + 1) * 512], ps[:, 0:512], AF.Exp),
                                waits=[("tensor", tv), ("vector", vv)], inc=1)
                            sv_exp[par] = sv
                            first = (kt == 0)
                            last = (kt == 4 * qc + 3)
                            tv = kb.emit("tensor", lambda o, qc=qc, par=par, first=first, last=last: o.matmul(
                                ps_o[0:65, qc * 512:(qc + 1) * 512], vaug[:],
                                pt_sb[:, par * 512:(par + 1) * 512],
                                start=first, stop=last),
                                waits=[("scalar", sv)], inc=1)
                            tv_pv[par] = tv
                            tv_last_o = tv
                    for qc in range(4):
                        vv = kb.emit("vector", lambda o, qc=qc: o.reciprocal(
                            rcp_sb[:], ps_o[64:65, qc * 512:(qc + 1) * 512]),
                            waits=[("tensor", tv_last_o)], inc=1)
                        tvb = kb.emit("tensor", lambda o: o.matmul(
                            ps_bc[0:64, 0:512], ones1[0:1, 0:64], rcp_sb[:],
                            start=True, stop=True), waits=[("vector", vv)], inc=1)
                        sv = kb.emit("scalar", lambda o, qc=qc: o.activation(
                            ot_sb[:], ps_o[0:64, qc * 512:(qc + 1) * 512], AF.Copy),
                            waits=[("tensor", tvb)], inc=1)
                        vv = kb.emit("vector", lambda o: o.tensor_tensor(
                            ot_sb[:], ot_sb[:], ps_bc[0:64, 0:512], op=ALU.mult),
                            waits=[("scalar", sv), ("tensor", tvb)], inc=1)
                        sd = kb.emit("sync", lambda o, h=h, qc=qc: o.dma_start(
                            out=otmp[h * 64:(h + 1) * 64, qc * 512:(qc + 1) * 512],
                            in_=ot_sb[:]), waits=[("vector", vv)], inc=16)
                        kb.wait("tensor", [("sync", sd)])
                    kb.wait("tensor", [("vector", kb.val("vector")), ("scalar", kb.val("scalar"))])
                    kb.fence()
                kb.flush()
            with (
                nc.sbuf_tensor(f"pb_wo{b}", [128, 8 * 1024], BF16) as wo_sb,
                nc.sbuf_tensor(f"pb_oin{b}", [128, 8 * 512], BF16) as oin,
                nc.sbuf_tensor(f"pb_borow{b}", [1, D], F32) as borow,
                nc.sbuf_tensor(f"pb_bocol{b}", [128, NF], F32) as bocol,
                nc.sbuf_tensor(f"pb_sev{b}", [128, 512], F32) as sev,
                nc.psum_tensor(f"pb_ps4{b}", [128, 512], F32) as ps4,
                nc.psum_tensor(f"pb_ps5{b}", [128, 512], F32) as ps5,
            ):
                gw = 0
                for kc in range(NF):
                    gw = kb.emit("gq", lambda o, kc=kc: o.dma_start(
                        out=wo_sb[:, kc * 1024:(kc + 1) * 1024],
                        in_=wo[kc * 128:(kc + 1) * 128, :]), inc=16)
                ld = kb.emit("sync", lambda o: o.dma_start(out=borow[0:1, :], in_=bo), inc=16)
                kb.wait("tensor", [("sync", ld)])
                kb.wait("vector", [("sync", ld)])
                col_bias(kb, nc, cst, borow, 0, bocol, 0, ps4)
                slast = {0: 0, 1: 0}
                for qc in range(4):
                    ldo = 0
                    for kc in range(NF):
                        ldo = kb.emit("sync", lambda o, kc=kc, qc=qc: o.dma_start(
                            out=oin[:, kc * 512:(kc + 1) * 512],
                            in_=otmp[kc * 128:(kc + 1) * 128, qc * 512:(qc + 1) * 512]),
                            waits=[("tensor", kb.val("tensor"))] if kc == 0 else [], inc=16)
                    for ft in range(NF):
                        ps = (ps4, ps5)[ft % 2]
                        tv = 0
                        for kc in range(NF):
                            tv = kb.emit("tensor", lambda o, ft=ft, kc=kc, ps=ps: o.matmul(
                                ps[:, 0:512],
                                wo_sb[:, kc * 1024 + ft * 128:kc * 1024 + (ft + 1) * 128],
                                oin[:, kc * 512:(kc + 1) * 512],
                                start=(kc == 0), stop=(kc == NF - 1)),
                                waits=([("sync", ldo), ("gq", gw), ("vector", slast[ft % 2])]
                                       if kc == 0 else []), inc=1)
                        sv = kb.emit("vector", lambda o, ft=ft, ps=ps: o.tensor_scalar(
                            sev[:], ps[:, 0:512], bocol[:, ft:ft + 1], None, op0=ALU.add),
                            waits=[("tensor", tv)], inc=1)
                        sd = kb.emit("sync", lambda o, ft=ft, qc=qc, b=b: o.dma_start(
                            out=s0T[ft * 128:(ft + 1) * 128, tb + qc * 512:tb + (qc + 1) * 512],
                            in_=sev[:]), waits=[("vector", sv)], inc=16)
                        slast[ft % 2] = sv
                        kb.wait("vector", [("sync", sd)])
                kb.flush()


def ph_c(kb, nc, cst, xnT_bf, xnT_f32, qw, kw, vw, ow, qb, kbias, vb, ob, n1w,
         src1T, src1T_bf, src1nat, otmp):
    ones1 = cst.ones1
    KV = 4
    with nc.sbuf_tensor("pc_zero", [128, D], BF16) as zt:
        vv = kb.emit("vector", lambda o: o.memset(zt[:], 0.0), inc=1)
        kb.emit("sync", lambda o: o.dma_start(out=src1nat[NTOK:NTOK + 128, :], in_=zt[:]),
                waits=[("vector", vv)], inc=16)
        kb.flush()
    with (
        nc.sbuf_tensor("pc_q", [128, 8 * 2048], BF16) as q_sb,
        nc.sbuf_tensor("pc_k", [128, 2 * 2048], BF16) as k_sb,
        nc.sbuf_tensor("pc_v", [128, 16 * 256], BF16) as v_sb,
    ):
        for b in range(B_PER_CORE):
            tb = b * S
            kb.wait("tensor", [("vector", kb.val("vector")), ("scalar", kb.val("scalar"))])
            with contextlib.ExitStack() as ctx:
                w3 = _mk(ctx, nc, "s", f"pc_w3{b}", [128, 8 * 1024 + 2 * 8 * 256], BF16)
                xn_r = _mk(ctx, nc, "s", f"pc_xn{b}", [128, 8 * 2048], BF16)
                brow = _mk(ctx, nc, "s", f"pc_brow{b}", [1, D + 2 * 256], F32)
                bcol = _mk(ctx, nc, "s", f"pc_bcol{b}", [128, NF], F32)
                bvbc = _mk(ctx, nc, "s", f"pc_bvbc{b}", [128, 256], F32)
                ps1 = _mk(ctx, nc, "p", f"pc_ps1{b}", [128, 512], F32)
                ps2 = _mk(ctx, nc, "p", f"pc_ps2{b}", [128, 512], F32)
                gw = 0
                for kc in range(NF):
                    gw = kb.emit("gq", lambda o, kc=kc: o.dma_start(
                        out=w3[:, kc * 1024:(kc + 1) * 1024],
                        in_=qw[kc * 128:(kc + 1) * 128, :]), inc=16)
                for wi, w in enumerate((kw, vw)):
                    for kc in range(NF):
                        gw = kb.emit("gq", lambda o, wi=wi, w=w, kc=kc: o.dma_start(
                            out=w3[:, 8192 + (wi * NF + kc) * 256:8192 + (wi * NF + kc + 1) * 256],
                            in_=w[kc * 128:(kc + 1) * 128, :]), inc=16)
                ld = kb.emit("sync", lambda o: o.dma_start(out=brow[0:1, 0:D], in_=qb), inc=16)
                ld = kb.emit("sync", lambda o: o.dma_start(out=brow[0:1, D:D + 256], in_=kbias), inc=16)
                ld = kb.emit("sync", lambda o: o.dma_start(out=brow[0:1, D + 256:D + 512], in_=vb), inc=16)
                kb.wait("tensor", [("sync", ld)])
                kb.wait("vector", [("sync", ld)])
                col_bias(kb, nc, cst, brow, 0, bcol, 0, ps1)
                bkcol = _mk(ctx, nc, "s", f"pc_bkcol{b}", [128, 2], F32)
                for f in range(2):
                    tvb = kb.emit("tensor", lambda o, f=f: o.transpose(
                        ps1[:, 0:1], brow[0:1, D + f * 128:D + (f + 1) * 128],
                        cst.ident[0:1, 0:1]), inc=1)
                    vvb = kb.emit("vector", lambda o, f=f: o.tensor_copy(
                        bkcol[:, f:f + 1], ps1[:, 0:1]), waits=[("tensor", tvb)], inc=1)
                    kb.wait("tensor", [("vector", vvb)])
                tvv = kb.emit("tensor", lambda o: o.matmul(
                    ps2[:, 0:256], ones1, brow[0:1, D + 256:D + 512],
                    start=True, stop=True), waits=[("vector", kb.val("vector"))], inc=1)
                vv = kb.emit("vector", lambda o: o.tensor_copy(bvbc[:, 0:256], ps2[:, 0:256]),
                             waits=[("tensor", tvv)], inc=1)
                ldx = 0
                for kc in range(NF):
                    ldx = kb.emit("sync", lambda o, kc=kc: o.dma_start(
                        out=xn_r[:, kc * 2048:(kc + 1) * 2048],
                        in_=xnT_bf[kc * 128:(kc + 1) * 128, tb:tb + S]), inc=16)
                kb.wait("tensor", [("sync", ldx), ("gq", gw)])
                pv0 = kb.val("vector")
                plast = {0: pv0, 1: pv0}
                for u, (ft, ch) in enumerate([(ft, ch) for ft in range(NF) for ch in range(4)]):
                    ps = (ps1, ps2)[u % 2]
                    tv = 0
                    for kc in range(NF):
                        tv = kb.emit("tensor", lambda o, ft=ft, ch=ch, kc=kc, ps=ps: o.matmul(
                            ps[:, 0:512],
                            w3[:, kc * 1024 + ft * 128:kc * 1024 + (ft + 1) * 128],
                            xn_r[:, kc * 2048 + ch * 512:kc * 2048 + (ch + 1) * 512],
                            start=(kc == 0), stop=(kc == NF - 1)),
                            waits=[("vector", plast[u % 2])] if kc == 0 else [], inc=1)
                    sv = kb.emit("vector", lambda o, ft=ft, ch=ch, ps=ps: o.tensor_scalar(
                        q_sb[:, ft * 2048 + ch * 512:ft * 2048 + (ch + 1) * 512],
                        ps[:, 0:512], bcol[:, ft:ft + 1], 0.125,
                        op0=ALU.add, op1=ALU.mult), waits=[("tensor", tv)], inc=1)
                    plast[u % 2] = sv
                for u, (ft, ch) in enumerate([(ft, ch) for ft in range(2) for ch in range(4)]):
                    ps = (ps1, ps2)[u % 2]
                    tv = 0
                    for kc in range(NF):
                        tv = kb.emit("tensor", lambda o, ft=ft, ch=ch, kc=kc, ps=ps: o.matmul(
                            ps[:, 0:512],
                            w3[:, 8192 + kc * 256 + ft * 128:8192 + kc * 256 + (ft + 1) * 128],
                            xn_r[:, kc * 2048 + ch * 512:kc * 2048 + (ch + 1) * 512],
                            start=(kc == 0), stop=(kc == NF - 1)),
                            waits=[("vector", plast[u % 2])] if kc == 0 else [], inc=1)
                    sv = kb.emit("vector", lambda o, ft=ft, ch=ch, ps=ps: o.tensor_scalar(
                        k_sb[:, ft * 2048 + ch * 512:ft * 2048 + (ch + 1) * 512],
                        ps[:, 0:512], bkcol[:, ft:ft + 1], None,
                        op0=ALU.add), waits=[("tensor", tv)], inc=1)
                    plast[u % 2] = sv
                for u, tt in enumerate(range(16)):
                    ps = (ps1, ps2)[u % 2]
                    tv = 0
                    for kc in range(NF):
                        tv = kb.emit("tensor", lambda o, tt=tt, kc=kc, ps=ps: o.matmul(
                            ps[:, 0:256],
                            xn_r[:, kc * 2048 + tt * 128:kc * 2048 + (tt + 1) * 128],
                            w3[:, 8192 + NF * 256 + kc * 256:8192 + NF * 256 + (kc + 1) * 256],
                            start=(kc == 0), stop=(kc == NF - 1)),
                            waits=[("vector", plast[u % 2])] if kc == 0 else [], inc=1)
                    vv = kb.emit("vector", lambda o, tt=tt, ps=ps: o.tensor_tensor(
                        v_sb[:, tt * 256:(tt + 1) * 256],
                        ps[:, 0:256], bvbc[:, 0:256], op=ALU.add),
                        waits=[("tensor", tv)], inc=1)
                    plast[u % 2] = vv
                kb.flush()
            with contextlib.ExitStack() as ctx:
                mx_sb = _mk(ctx, nc, "s", f"pc_mx{b}", [128, 4], F32)
                nma = _mk(ctx, nc, "s", f"pc_nma{b}", [128, 16], F32)
                nm16 = _mk(ctx, nc, "s", f"pc_nm16{b}", [16, 128], F32)
                nmrow = _mk(ctx, nc, "s", f"pc_nmrow{b}", [1, 2048], F32)
                qaug = _mk(ctx, nc, "s", f"pc_qaug{b}", [65, 2048], BF16)
                kaug = _mk(ctx, nc, "s", f"pc_kaug{b}", [65, 128], BF16)
                vaug = _mk(ctx, nc, "s", f"pc_vaug{b}", [128, 65], BF16)
                kstage = _mk(ctx, nc, "s", f"pc_kst{b}", [64, 2048], BF16)
                pt_sb = _mk(ctx, nc, "s", f"pc_pt{b}", [128, 2 * 512], BF16)
                ot_sb = _mk(ctx, nc, "s", f"pc_ot{b}", [64, 512], BF16)
                rcp_sb = _mk(ctx, nc, "s", f"pc_rcp{b}", [1, 512], F32)
                ps_sA = _mk(ctx, nc, "p", f"pc_ps_sA{b}", [128, 512], F32)
                ps_sB = _mk(ctx, nc, "p", f"pc_ps_sB{b}", [128, 512], F32)
                ps_o = _mk(ctx, nc, "p", f"pc_ps_o{b}", [65, 2048], F32)
                ps_bc = _mk(ctx, nc, "p", f"pc_ps_bc{b}", [64, 512], F32)
                rowtmp = nc.dram_tensor(f"pc_rowtmp{b}", [16, 128], F32, kind="Internal").ap()
                vv = kb.emit("vector", lambda o: o.memset(kaug[64:65, :], 1.0), inc=1)
                vv = kb.emit("vector", lambda o: o.memset(vaug[:, 64:65], 1.0), inc=1)
                kb.wait("tensor", [("vector", vv)])
                kb.wait("scalar", [("vector", vv)])
                for h in range(H0):
                    hb = (h % 2) * 64
                    hf = h // 2
                    g = h // 4
                    gb2 = (g % 2) * 64
                    gf = g // 2
                    qsl = q_sb[hb:hb + 64, hf * 2048:(hf + 1) * 2048]
                    ksl = k_sb[gb2:gb2 + 64, gf * 2048:(gf + 1) * 2048]
                    vst = kb.emit("vector", lambda o, qsl=qsl: o.tensor_copy(qaug[0:64, :], qsl),
                                  waits=[("tensor", kb.val("tensor"))], inc=1)
                    if h % 4 == 0:
                        vst = kb.emit("vector", lambda o, ksl=ksl: o.tensor_copy(kstage[:], ksl),
                                      inc=1)
                    vred = {0: vst, 1: vst}
                    unit = 0
                    for qt in range(16):
                        for c in range(4):
                            par1 = unit % 2
                            unit += 1
                            ps = (ps_sA, ps_sB)[par1]
                            tv = kb.emit("tensor", lambda o, qt=qt, c=c, ps=ps: o.matmul(
                                ps[:, 0:512], qaug[0:64, qt * 128:(qt + 1) * 128],
                                kstage[:, c * 512:(c + 1) * 512], start=True, stop=True),
                                waits=[("vector", vred[par1])], inc=1)
                            col = 0 if c == 0 else 1
                            vv = kb.emit("vector", lambda o, ps=ps, col=col: o.reduce_max(
                                mx_sb[:, col:col + 1], ps[:, 0:512], axis=AX.X),
                                waits=[("tensor", tv)], inc=1)
                            if c > 0:
                                vv = kb.emit("vector", lambda o: o.tensor_max(
                                    mx_sb[:, 0:1], mx_sb[:, 0:1], mx_sb[:, 1:2]), inc=1)
                            vred[par1] = vv
                        vv = kb.emit("vector", lambda o, qt=qt: o.tensor_scalar(
                            nma[:, qt:qt + 1], mx_sb[:, 0:1], -1.0, None, op0=ALU.mult), inc=1)
                    tv = kb.emit("tensor", lambda o: o.transpose(
                        ps_bc[0:16, 0:128] if False else ps_sA[0:16, 0:128], nma[:], cst.ident),
                        waits=[("vector", vv)], inc=1)
                    vv = kb.emit("vector", lambda o: o.tensor_copy(nm16[:], ps_sA[0:16, 0:128]),
                                 waits=[("tensor", tv)], inc=1)
                    sd1 = kb.emit("sync", lambda o: o.dma_start(out=rowtmp, in_=nm16[:]),
                                  waits=[("vector", vv)], inc=16)
                    sd = kb.emit("sync", lambda o: o.dma_start(
                        out=nmrow[:], in_=rowtmp.rearrange("a b -> (a b)").unsqueeze(0)),
                        waits=[("sync", sd1)], inc=16)
                    vv = kb.emit("vector", lambda o: o.tensor_copy(qaug[64:65, :], nmrow[:]),
                                 waits=[("sync", sd)], inc=1)
                    sv_exp = {0: 0, 1: 0}
                    tv_pv = {0: 0, 1: 0}
                    tv_last_o = 0
                    for kt in range(16):
                        vv_kaug = kb.emit("vector", lambda o, kt=kt: o.tensor_copy(
                            kaug[0:64, :], kstage[:, kt * 128:(kt + 1) * 128]),
                            waits=[("tensor", tv_pv[0]), ("tensor", tv_pv[1])], inc=1)
                        vv_kaug = kb.emit("vector", lambda o, kt=kt, g=g: o.tensor_copy(
                            vaug[:, 0:64], v_sb[:, kt * 256 + g * 64:kt * 256 + g * 64 + 64]),
                            inc=1)
                        for qc in range(4):
                            par = (kt + qc) % 2
                            ps = (ps_sA, ps_sB)[par]
                            tv = kb.emit("tensor", lambda o, qc=qc, ps=ps: o.matmul(
                                ps[:, 0:512], kaug[:, 0:128],
                                qaug[:, qc * 512:(qc + 1) * 512], start=True, stop=True),
                                waits=[("vector", vv_kaug), ("scalar", sv_exp[par])], inc=1)
                            sv = kb.emit("scalar", lambda o, ps=ps, par=par: o.activation(
                                pt_sb[:, par * 512:(par + 1) * 512], ps[:, 0:512], AF.Exp),
                                waits=[("tensor", tv)], inc=1)
                            sv_exp[par] = sv
                            first = (kt == 0)
                            last = (kt == 15)
                            tv = kb.emit("tensor", lambda o, qc=qc, par=par, first=first, last=last: o.matmul(
                                ps_o[0:65, qc * 512:(qc + 1) * 512], vaug[:],
                                pt_sb[:, par * 512:(par + 1) * 512],
                                start=first, stop=last),
                                waits=[("scalar", sv)], inc=1)
                            tv_pv[par] = tv
                            tv_last_o = tv
                    for qc in range(4):
                        vv = kb.emit("vector", lambda o, qc=qc: o.reciprocal(
                            rcp_sb[:], ps_o[64:65, qc * 512:(qc + 1) * 512]),
                            waits=[("tensor", tv_last_o)], inc=1)
                        tvb = kb.emit("tensor", lambda o: o.matmul(
                            ps_bc[0:64, 0:512], ones1[0:1, 0:64], rcp_sb[:],
                            start=True, stop=True), waits=[("vector", vv)], inc=1)
                        sv = kb.emit("scalar", lambda o, qc=qc: o.activation(
                            ot_sb[:], ps_o[0:64, qc * 512:(qc + 1) * 512], AF.Copy),
                            waits=[("tensor", tvb)], inc=1)
                        vv = kb.emit("vector", lambda o: o.tensor_tensor(
                            ot_sb[:], ot_sb[:], ps_bc[0:64, 0:512], op=ALU.mult),
                            waits=[("scalar", sv)], inc=1)
                        sd = kb.emit("sync", lambda o, h=h, qc=qc: o.dma_start(
                            out=otmp[h * 64:(h + 1) * 64, qc * 512:(qc + 1) * 512],
                            in_=ot_sb[:]), waits=[("vector", vv)], inc=16)
                        kb.wait("tensor", [("sync", sd)])
                    kb.wait("tensor", [("vector", kb.val("vector")), ("scalar", kb.val("scalar"))])
                    kb.fence()
                kb.flush()
            with contextlib.ExitStack() as ctx:
                wo_sb = _mk(ctx, nc, "s", f"pc_wo{b}", [128, 8 * 1024], BF16)
                oin = _mk(ctx, nc, "s", f"pc_oin{b}", [128, 8 * 512], BF16)
                xn32 = _mk(ctx, nc, "s", f"pc_xn32{b}", [128, 8 * 512], F32)
                src_sb = _mk(ctx, nc, "s", f"pc_src{b}", [128, 8 * 512], F32)
                sq_sb = _mk(ctx, nc, "s", f"pc_sq{b}", [128, 512], F32)
                rstd_sb = _mk(ctx, nc, "s", f"pc_rstd{b}", [128, 512], F32)
                borow = _mk(ctx, nc, "s", f"pc_borow{b}", [1, 2 * D], F32)
                bocol = _mk(ctx, nc, "s", f"pc_bocol{b}", [128, 2 * NF], F32)
                sev = _mk(ctx, nc, "s", f"pc_sev{b}", [128, 512], F32)
                sevb = _mk(ctx, nc, "s", f"pc_sevb{b}", [128, 512], BF16)
                ps4 = _mk(ctx, nc, "p", f"pc_ps4{b}", [128, 512], F32)
                ps5 = _mk(ctx, nc, "p", f"pc_ps5{b}", [128, 512], F32)
                ps6 = _mk(ctx, nc, "p", f"pc_ps6{b}", [128, 512], F32)
                gw = 0
                for kc in range(NF):
                    gw = kb.emit("gq", lambda o, kc=kc: o.dma_start(
                        out=wo_sb[:, kc * 1024:(kc + 1) * 1024],
                        in_=ow[kc * 128:(kc + 1) * 128, :]), inc=16)
                ld = kb.emit("sync", lambda o: o.dma_start(out=borow[0:1, 0:D], in_=ob), inc=16)
                ld = kb.emit("sync", lambda o: o.dma_start(out=borow[0:1, D:2 * D], in_=n1w), inc=16)
                kb.wait("tensor", [("sync", ld)])
                kb.wait("vector", [("sync", ld)])
                col_bias(kb, nc, cst, borow, 0, bocol, 0, ps4)
                col_bias(kb, nc, cst, borow, D, bocol, NF, ps4)
                vlast = kb.val("vector")
                for qc in range(4):
                    ldo = 0
                    ldx = 0
                    for kc in range(NF):
                        ldo = kb.emit("sync", lambda o, kc=kc, qc=qc: o.dma_start(
                            out=oin[:, kc * 512:(kc + 1) * 512],
                            in_=otmp[kc * 128:(kc + 1) * 128, qc * 512:(qc + 1) * 512]),
                            waits=[("tensor", kb.val("tensor")), ("vector", vlast)]
                            if kc == 0 else [], inc=16)
                        ldx = kb.emit("sync", lambda o, kc=kc, qc=qc: o.dma_start(
                            out=xn32[:, kc * 512:(kc + 1) * 512],
                            in_=xnT_f32[kc * 128:(kc + 1) * 128, tb + qc * 512:tb + (qc + 1) * 512]),
                            inc=16)
                    for ft in range(NF):
                        ps = (ps4, ps5)[ft % 2]
                        tv = 0
                        for kc in range(NF):
                            tv = kb.emit("tensor", lambda o, ft=ft, kc=kc, ps=ps: o.matmul(
                                ps[:, 0:512],
                                wo_sb[:, kc * 1024 + ft * 128:kc * 1024 + (ft + 1) * 128],
                                oin[:, kc * 512:(kc + 1) * 512],
                                start=(kc == 0), stop=(kc == NF - 1)),
                                waits=([("sync", ldx), ("gq", gw), ("vector", vlast)]
                                       if kc == 0 else []), inc=1)
                        vv = kb.emit("vector", lambda o, ft=ft, ps=ps: o.tensor_scalar(
                            sev[:], ps[:, 0:512], bocol[:, ft:ft + 1], None, op0=ALU.add),
                            waits=[("tensor", tv)], inc=1)
                        vv = kb.emit("vector", lambda o, ft=ft: o.tensor_tensor(
                            src_sb[:, ft * 512:(ft + 1) * 512], sev[:],
                            xn32[:, ft * 512:(ft + 1) * 512], op=ALU.add), inc=1)
                        vlast = vv
                    tvs = 0
                    for ft in range(NF):
                        sv = kb.emit("scalar", lambda o, ft=ft: o.activation(
                            sq_sb[:], src_sb[:, ft * 512:(ft + 1) * 512], AF.Square),
                            waits=[("vector", vlast), ("tensor", tvs)], inc=1)
                        tvs = kb.emit("tensor", lambda o, ft=ft: o.matmul(
                            ps6[:, 0:512], cst.ones128, sq_sb[:],
                            start=(ft == 0), stop=(ft == NF - 1)),
                            waits=[("scalar", sv)], inc=1)
                    vv = kb.emit("vector", lambda o: o.tensor_scalar(
                        rstd_sb[:], ps6[:, 0:512], 1.0 / D, 1e-6, op0=ALU.mult, op1=ALU.add),
                        waits=[("tensor", tvs)], inc=1)
                    sv = kb.emit("scalar", lambda o: o.activation(
                        sq_sb[:], rstd_sb[:], AF.Ln), waits=[("vector", vv)], inc=1)
                    sv = kb.emit("scalar", lambda o: o.activation(
                        rstd_sb[:], sq_sb[:], AF.Exp, scale=-0.5), inc=1)
                    for ft in range(NF):
                        vv = kb.emit("vector", lambda o, ft=ft: o.scalar_tensor_tensor(
                            sev[:], src_sb[:, ft * 512:(ft + 1) * 512],
                            1.0, rstd_sb[:], op0=ALU.mult, op1=ALU.mult),
                            waits=[("scalar", sv)], inc=1)
                        vv = kb.emit("vector", lambda o, ft=ft: o.tensor_scalar(
                            sev[:], sev[:], bocol[:, NF + ft:NF + ft + 1], None, op0=ALU.mult),
                            inc=1)
                        vv = kb.emit("vector", lambda o, ft=ft: o.tensor_copy(sevb[:], sev[:]), inc=1)
                        sd = kb.emit("sync", lambda o, ft=ft, qc=qc: o.dma_start(
                            out=src1T[ft * 128:(ft + 1) * 128, tb + qc * 512:tb + (qc + 1) * 512],
                            in_=sev[:]), waits=[("vector", vv)], inc=16)
                        sd = kb.emit("sync", lambda o, ft=ft, qc=qc: o.dma_start(
                            out=src1T_bf[ft * 128:(ft + 1) * 128, tb + qc * 512:tb + (qc + 1) * 512],
                            in_=sevb[:]), inc=16)
                        for tt4 in range(4):
                            tvt = kb.emit("tensor", lambda o, tt4=tt4: o.transpose(
                                ps4[:, 0:128].bitcast(BF16)[:, 0:128], sevb[:, tt4 * 128:(tt4 + 1) * 128],
                                cst.ident_bf), waits=[("vector", vv)], inc=1)
                            vv2 = kb.emit("vector", lambda o, tt4=tt4: o.tensor_copy(
                                sevb[:, 0:128] if False else sq_sb.bitcast(BF16)[:, 0:128],
                                ps4[:, 0:128].bitcast(BF16)[:, 0:128]),
                                waits=[("tensor", tvt)], inc=1)
                            sd = kb.emit("sync", lambda o, tt4=tt4, qc=qc, ft=ft, b=b: o.dma_start(
                                out=src1nat[tb + qc * 512 + tt4 * 128:tb + qc * 512 + (tt4 + 1) * 128,
                                            ft * 128:(ft + 1) * 128],
                                in_=sq_sb.bitcast(BF16)[:, 0:128]),
                                waits=[("vector", vv2)], inc=16)
                            kb.wait("tensor", [("sync", sd)])
                            kb.wait("vector", [("sync", sd)])
                        vlast = kb.val("vector")
                kb.flush()


def ph_d_count(kb, nc, cst, src1T_f32, gate_w, cnt_o, sel12_o, wts12_o, cincl_o):
    with contextlib.ExitStack() as ctx:
        gw_sb = _mk(ctx, nc, "s", "pd_gw", [128, 8 * 8], F32)
        src_r = _mk(ctx, nc, "s", "pd_src", [128, 8 * 512], F32)
        probs = _mk(ctx, nc, "s", "pd_probs", [8, NTOK], F32)
        ptok = _mk(ctx, nc, "s", "pd_ptok", [128, 8], F32)
        m8 = _mk(ctx, nc, "s", "pd_m8", [128, 8], F32)
        i8 = _mk(ctx, nc, "s", "pd_i8", [128, 8], U32)
        i8f = _mk(ctx, nc, "s", "pd_i8f", [128, 2], F32)
        wcol = _mk(ctx, nc, "s", "pd_w", [128, 4], F32)
        etok = _mk(ctx, nc, "s", "pd_etok", [128, 8], F32)
        esum = _mk(ctx, nc, "s", "pd_e", [8, NTOK], F32)
        cnt8 = _mk(ctx, nc, "s", "pd_cnt8", [8, 1], F32)
        cntrow = _mk(ctx, nc, "s", "pd_cntrow", [1, 8], F32)
        ps1 = _mk(ctx, nc, "p", "pd_ps1", [128, 512], F32)
        ps2 = _mk(ctx, nc, "p", "pd_ps2", [128, 512], F32)
        g = 0
        for kc in range(NF):
            g = kb.emit("sync", lambda o, kc=kc: o.dma_start(
                out=gw_sb[:, kc * 8:(kc + 1) * 8], in_=gate_w[kc * 128:(kc + 1) * 128, :]),
                inc=16)
        for ch in range(NTOK // 512):
            ldx = 0
            for kc in range(NF):
                ldx = kb.emit("sync", lambda o, kc=kc, ch=ch: o.dma_start(
                    out=src_r[:, kc * 512:(kc + 1) * 512],
                    in_=src1T_f32[kc * 128:(kc + 1) * 128, ch * 512:(ch + 1) * 512]),
                    waits=[("tensor", kb.val("tensor"))] if kc == 0 else [], inc=16)
            tv = 0
            for kc in range(NF):
                tv = kb.emit("tensor", lambda o, kc=kc, ch=ch: o.matmul(
                    ps1[0:8, 0:512], gw_sb[:, kc * 8:(kc + 1) * 8],
                    src_r[:, kc * 512:(kc + 1) * 512],
                    start=(kc == 0), stop=(kc == NF - 1)),
                    waits=[("sync", ldx)] if kc == 0 else [], inc=1)
            sv = kb.emit("scalar", lambda o, ch=ch: o.activation(
                probs[:, ch * 512:(ch + 1) * 512], ps1[0:8, 0:512], AF.Sigmoid),
                waits=[("tensor", tv)], inc=1)
            kb.wait("tensor", [("scalar", sv)])
        for i in range(NT):
            tv = kb.emit("tensor", lambda o, i=i: o.transpose(
                ps2[0:128, 0:8], probs[:, i * 128:(i + 1) * 128], cst.ident[0:8, 0:8]),
                waits=[("vector", kb.val("vector")), ("scalar", kb.val("scalar"))], inc=1)
            vv = kb.emit("vector", lambda o: o.tensor_copy(ptok[:], ps2[0:128, 0:8]),
                         waits=[("tensor", tv)], inc=1)
            vv = kb.emit("vector", lambda o: o.max(m8[:], ptok[:]), inc=1)
            vv = kb.emit("vector", lambda o: o.max_index(i8[:], m8[:], ptok[:]), inc=1)
            vv = kb.emit("vector", lambda o: o.tensor_copy(i8f[:, 0:2], i8[:, 0:2]), inc=1)
            vv = kb.emit("vector", lambda o: o.tensor_tensor(
                wcol[:, 0:1], m8[:, 0:1], m8[:, 1:2], op=ALU.add), inc=1)
            vv = kb.emit("vector", lambda o: o.tensor_scalar(
                wcol[:, 0:1], wcol[:, 0:1], 1e-6, None, op0=ALU.add), inc=1)
            vv = kb.emit("vector", lambda o: o.reciprocal(wcol[:, 1:2], wcol[:, 0:1]), inc=1)
            vv = kb.emit("vector", lambda o: o.tensor_scalar(
                wcol[:, 2:3], m8[:, 0:1], wcol[:, 1:2], None, op0=ALU.mult), inc=1)
            vv = kb.emit("vector", lambda o: o.tensor_scalar(
                wcol[:, 3:4], m8[:, 1:2], wcol[:, 1:2], None, op0=ALU.mult), inc=1)
            vv = kb.emit("vector", lambda o: o.tensor_scalar(
                etok[:], cst.iota8, i8f[:, 0:1], None, op0=ALU.is_equal), inc=1)
            vv = kb.emit("vector", lambda o: o.scalar_tensor_tensor(
                etok[:], cst.iota8, i8f[:, 1:2], etok[:], op0=ALU.is_equal, op1=ALU.add), inc=1)
            sd = kb.emit("sync", lambda o, i=i: o.dma_start(
                out=sel12_o[i * 128:(i + 1) * 128, :], in_=i8f[:, 0:2]),
                waits=[("vector", vv)], inc=16)
            sd = kb.emit("sync", lambda o, i=i: o.dma_start(
                out=wts12_o[i * 128:(i + 1) * 128, :], in_=wcol[:, 2:4]), inc=16)
            tv = kb.emit("tensor", lambda o, i=i: o.transpose(
                ps2[0:8, 0:128], etok[:], cst.ident),
                waits=[("vector", vv)], inc=1)
            vv = kb.emit("vector", lambda o, i=i: o.tensor_copy(
                esum[:, i * 128:(i + 1) * 128], ps2[0:8, 0:128]),
                waits=[("tensor", tv)], inc=1)
            kb.wait("tensor", [("vector", vv), ("sync", sd)])
        zs = _mk(ctx, nc, "s", "pd_zs", [8, NTOK], F32)
        vv = kb.emit("vector", lambda o: o.memset(zs[:], 0.0), inc=1)
        vv = kb.emit("vector", lambda o: o.tensor_tensor_scan(
            esum[:], esum[:], zs[:], 0.0, op0=ALU.add, op1=ALU.add), inc=1)
        sd = kb.emit("sync", lambda o: o.dma_start(out=cincl_o, in_=esum[:]),
                     waits=[("vector", vv)], inc=16)
        vv = kb.emit("vector", lambda o: o.tensor_copy(cnt8[:], esum[:, NTOK - 1:NTOK]), inc=1)
        tv = kb.emit("tensor", lambda o: o.transpose(
            ps2[0:1, 0:8], cnt8[:], cst.ident[0:8, 0:8]), waits=[("vector", vv)], inc=1)
        vv = kb.emit("vector", lambda o: o.tensor_copy(cntrow[:], ps2[0:1, 0:8]),
                     waits=[("tensor", tv)], inc=1)
        sd = kb.emit("sync", lambda o: o.dma_start(out=cnt_o, in_=cntrow[:]),
                     waits=[("vector", vv)], inc=16)
        kb.flush()


def ph_d_route(kb, nc, cst, cincl_i, sel12_i, wts12_i, allcnt_i, maskcol_i,
               ilist2, wlist2, idxw1_o, idxw2_o):
    with contextlib.ExitStack() as ctx:
        allc = _mk(ctx, nc, "s", "pr_allc", [8, 8], F32)
        mcol = _mk(ctx, nc, "s", "pr_mcol", [8, 1], F32)
        rows = _mk(ctx, nc, "s", "pr_rows", [1, 8 * 8], F32)
        srow = _mk(ctx, nc, "s", "pr_srow", [1, NSLOT], F32)
        iw = _mk(ctx, nc, "s", "pr_iw", [NSLOT, 2 + 2 * 128], F32)
        iwi = _mk(ctx, nc, "s", "pr_iwi", [NSLOT, 2 * 128], I32)
        prebuf = _mk(ctx, nc, "s", "pr_pre", [128, NSLOT * 4], I32)
        prew = _mk(ctx, nc, "s", "pr_prew", [128, NSLOT * 4], I32)
        cinc = _mk(ctx, nc, "s", "pr_cinc", [8, 128], F32)
        ctok = _mk(ctx, nc, "s", "pr_ctok", [128, 8], F32)
        etok = _mk(ctx, nc, "s", "pr_etok", [128, 8], F32)
        self12 = _mk(ctx, nc, "s", "pr_sel", [128, 2], F32)
        wt12 = _mk(ctx, nc, "s", "pr_wt", [128, 2], F32)
        m1t = _mk(ctx, nc, "s", "pr_m1", [128, 8], F32)
        m2t = _mk(ctx, nc, "s", "pr_m2", [128, 8], F32)
        colv = _mk(ctx, nc, "s", "pr_colv", [128, 12], F32)
        coli = _mk(ctx, nc, "s", "pr_coli", [128, 2], I32)
        sci = _mk(ctx, nc, "s", "pr_sci", [128, 4], I32)
        sct = _mk(ctx, nc, "s", "pr_sct", [128, 4], I32)
        scw = _mk(ctx, nc, "s", "pr_scw", [128, 4], I32)
        obc = _mk(ctx, nc, "s", "pr_obc", [128, 8], F32)
        cbc = _mk(ctx, nc, "s", "pr_cbc", [128, 8], F32)
        ps1 = _mk(ctx, nc, "p", "pr_ps1", [128, 128], F32)
        ld = kb.emit("sync", lambda o: o.dma_start(out=allc[:], in_=allcnt_i), inc=16)
        ld = kb.emit("sync", lambda o: o.dma_start(out=mcol[:], in_=maskcol_i), inc=16)
        tv = kb.emit("tensor", lambda o: o.matmul(ps1[0:1, 0:8], mcol[:], allc[:],
                                                  start=True, stop=True),
                     waits=[("sync", ld)], inc=1)
        vv = kb.emit("vector", lambda o: o.tensor_copy(rows[0:1, 0:8], ps1[0:1, 0:8]),
                     waits=[("tensor", tv)], inc=1)
        ld2 = kb.emit("sync", lambda o: o.dma_start(
            out=cinc[:, 0:1], in_=cincl_i[:, NTOK - 1:NTOK],
            allow_slow_non_contiguous=True), inc=16)
        tv = kb.emit("tensor", lambda o: o.transpose(ps1[0:1, 0:8], cinc[0:8, 0:1],
                                                     cst.ident[0:8, 0:8]),
                     waits=[("sync", ld2), ("vector", vv)], inc=1)
        vv = kb.emit("vector", lambda o: o.tensor_copy(rows[0:1, 8:16], ps1[0:1, 0:8]),
                     waits=[("tensor", tv)], inc=1)
        vv = kb.emit("vector", lambda o: o.tensor_scalar(
            rows[0:1, 16:24], rows[0:1, 0:8], -1.0, float(CAPACITY),
            op0=ALU.mult, op1=ALU.add), inc=1)
        vv = kb.emit("vector", lambda o: o.tensor_scalar(
            rows[0:1, 16:24], rows[0:1, 16:24], 0.0, None, op0=ALU.max), inc=1)
        vv = kb.emit("vector", lambda o: o.tensor_tensor(
            rows[0:1, 16:24], rows[0:1, 16:24], rows[0:1, 8:16], op=ALU.min), inc=1)
        vv = kb.emit("vector", lambda o: o.memset(rows[0:1, 24:32], 0.0), inc=1)
        for k in range(16):
            vv = kb.emit("vector", lambda o, k=k: o.scalar_tensor_tensor(
                rows[0:1, 24:32], rows[0:1, 16:24], float(k * 512), rows[0:1, 24:32],
                op0=ALU.is_gt, op1=ALU.add), inc=1)
        vv = kb.emit("vector", lambda o: o.memset(rows[0:1, 32:40], 0.0), inc=1)
        vv = kb.emit("vector", lambda o: o.tensor_tensor_scan(
            rows[0:1, 40:48], rows[0:1, 24:32], rows[0:1, 32:40], 0.0,
            op0=ALU.add, op1=ALU.add), inc=1)
        vv = kb.emit("vector", lambda o: o.tensor_tensor(
            rows[0:1, 48:56], rows[0:1, 40:48], rows[0:1, 24:32], op=ALU.subtract), inc=1)
        vv = kb.emit("vector", lambda o: o.memset(srow[:], 0.0), inc=1)
        for e in range(8):
            vv = kb.emit("vector", lambda o, e=e: o.scalar_tensor_tensor(
                srow[:], cst.iota24, rows[0:1, 40 + e:41 + e], srow[:],
                op0=ALU.is_ge, op1=ALU.add), inc=1)
        tv = kb.emit("tensor", lambda o: o.transpose(
            ps1[0:NSLOT, 0:1], srow[:], cst.ident[0:1, 0:1]),
            waits=[("vector", vv)], inc=1)
        vv = kb.emit("vector", lambda o: o.tensor_copy(iw[0:NSLOT, 0:1], ps1[0:NSLOT, 0:1]),
                     waits=[("tensor", tv)], inc=1)
        tv = kb.emit("tensor", lambda o: o.matmul(
            ps1[0:NSLOT, 0:128], cst.ones1[0:1, 0:NSLOT], cst.iota128r,
            start=True, stop=True), waits=[("vector", vv)], inc=1)
        for wi, scale in ((0, 1024.0), (1, 2048.0)):
            vv = kb.emit("vector", lambda o, wi=wi, scale=scale: o.tensor_scalar(
                iw[0:NSLOT, 2 + wi * 128:2 + (wi + 1) * 128], ps1[0:NSLOT, 0:128],
                iw[0:NSLOT, 0:1], None, op0=ALU.bypass)
                if False else o.tensor_scalar(
                iw[0:NSLOT, 2 + wi * 128:2 + (wi + 1) * 128], ps1[0:NSLOT, 0:128],
                iw[0:NSLOT, 0:1], None, op0=ALU.add),
                waits=[("tensor", tv)] if wi == 0 else [], inc=1)
        for wi, scale in ((0, 1024.0), (1, 2048.0)):
            vv = kb.emit("vector", lambda o, wi=wi, scale=scale: o.tensor_scalar(
                iw[0:NSLOT, 1:2], iw[0:NSLOT, 0:1], scale, None, op0=ALU.mult), inc=1)
            vv = kb.emit("vector", lambda o, wi=wi: o.tensor_scalar(
                iw[0:NSLOT, 2 + wi * 128:2 + (wi + 1) * 128], ps1[0:NSLOT, 0:128],
                iw[0:NSLOT, 1:2], None, op0=ALU.add), inc=1)
        vv = kb.emit("vector", lambda o: o.tensor_copy(
            iwi[0:NSLOT, 0:256], iw[0:NSLOT, 2:258]), inc=1)
        sd = kb.emit("sync", lambda o: o.dma_start(out=idxw1_o, in_=iwi[0:NSLOT, 0:128]),
                     waits=[("vector", vv)], inc=16)
        sd = kb.emit("sync", lambda o: o.dma_start(out=idxw2_o, in_=iwi[0:NSLOT, 128:256]),
                     inc=16)
        vv = kb.emit("vector", lambda o: o.memset(prebuf[:], NTOK), inc=1)
        vv = kb.emit("vector", lambda o: o.memset(prew[:], 0), inc=1)
        for qstart in range(0, NSLOT * 512, 128 * NSLOT * 4 // 8):
            pass
        sd = kb.emit("sync", lambda o: o.dma_start(
            out=ilist2.rearrange("(a b) -> a b", b=NSLOT * 4), in_=prebuf[:]),
            waits=[("vector", vv)], inc=16)
        sd = kb.emit("sync", lambda o: o.dma_start(
            out=wlist2.rearrange("(a b) -> a b", b=NSLOT * 4), in_=prew[:]), inc=16)
        kb.wait("gq", [("sync", sd)])
        vv = kb.emit("vector", lambda o: o.tensor_scalar(
            rows[0:1, 56:64], rows[0:1, 48:56], 512.0, None, op0=ALU.mult), inc=1)
        tv = kb.emit("tensor", lambda o: o.matmul(
            ps1[0:128, 0:8], cst.ones1, rows[0:1, 0:8], start=True, stop=True),
            waits=[("vector", vv)], inc=1)
        vv = kb.emit("vector", lambda o: o.tensor_copy(obc[:], ps1[0:128, 0:8]),
                     waits=[("tensor", tv)], inc=1)
        tv = kb.emit("tensor", lambda o: o.matmul(
            ps1[0:128, 0:8], cst.ones1, rows[0:1, 56:64], start=True, stop=True),
            waits=[("vector", vv)], inc=1)
        vv = kb.emit("vector", lambda o: o.tensor_copy(cbc[:], ps1[0:128, 0:8]),
                     waits=[("tensor", tv)], inc=1)
        for i in range(NT):
            ld = kb.emit("sync", lambda o, i=i: o.dma_start(
                out=cinc[:], in_=cincl_i[:, i * 128:(i + 1) * 128]),
                waits=[("tensor", kb.val("tensor")), ("vector", kb.val("vector"))], inc=16)
            ld = kb.emit("sync", lambda o, i=i: o.dma_start(
                out=self12[:], in_=sel12_i[i * 128:(i + 1) * 128, :]), inc=16)
            ld = kb.emit("sync", lambda o, i=i: o.dma_start(
                out=wt12[:], in_=wts12_i[i * 128:(i + 1) * 128, :]), inc=16)
            tv = kb.emit("tensor", lambda o: o.transpose(
                ps1[0:128, 0:8], cinc[:], cst.ident[0:8, 0:8]),
                waits=[("sync", ld)], inc=1)
            vv = kb.emit("vector", lambda o: o.tensor_copy(ctok[:], ps1[0:128, 0:8]),
                         waits=[("tensor", tv)], inc=1)
            vv = kb.emit("vector", lambda o: o.tensor_scalar(
                etok[:], cst.iota8, self12[:, 0:1], None, op0=ALU.is_equal), inc=1)
            vv = kb.emit("vector", lambda o: o.scalar_tensor_tensor(
                etok[:], cst.iota8, self12[:, 1:2], etok[:], op0=ALU.is_equal, op1=ALU.add),
                inc=1)
            vv = kb.emit("vector", lambda o: o.tensor_tensor(
                ctok[:], ctok[:], etok[:], op=ALU.subtract), inc=1)
            for j in range(2):
                rb_ = (i * 2 + j) % 4
                vv = kb.emit("vector", lambda o, j=j: o.tensor_scalar(
                    m1t[:], cst.iota8, self12[:, j:j + 1], None, op0=ALU.is_equal), inc=1)
                vv = kb.emit("vector", lambda o: o.tensor_tensor(
                    m2t[:], m1t[:], ctok[:], op=ALU.mult), inc=1)
                vv = kb.emit("vector", lambda o, j=j: o.reduce_sum(
                    colv[:, 0 + j:1 + j], m2t[:], axis=AX.X), inc=1)
                vv = kb.emit("vector", lambda o: o.tensor_tensor(
                    m2t[:], m1t[:], obc[:], op=ALU.mult), inc=1)
                vv = kb.emit("vector", lambda o, j=j: o.reduce_sum(
                    colv[:, 2 + j:3 + j], m2t[:], axis=AX.X), inc=1)
                vv = kb.emit("vector", lambda o: o.tensor_tensor(
                    m2t[:], m1t[:], cbc[:], op=ALU.mult), inc=1)
                vv = kb.emit("vector", lambda o, j=j: o.reduce_sum(
                    colv[:, 4 + j:5 + j], m2t[:], axis=AX.X), inc=1)
                vv = kb.emit("vector", lambda o, j=j: o.tensor_tensor(
                    colv[:, 6 + j:7 + j], colv[:, 0 + j:1 + j], colv[:, 2 + j:3 + j],
                    op=ALU.add), inc=1)
                vv = kb.emit("vector", lambda o, j=j: o.tensor_scalar(
                    colv[:, 8 + j:9 + j], colv[:, 6 + j:7 + j], float(CAPACITY), None,
                    op0=ALU.is_lt), inc=1)
                vv = kb.emit("vector", lambda o, j=j: o.tensor_tensor(
                    colv[:, 10 + j:11 + j], colv[:, 4 + j:5 + j], colv[:, 0 + j:1 + j],
                    op=ALU.add), inc=1)
                vv = kb.emit("vector", lambda o, j=j: o.scalar_tensor_tensor(
                    m2t[:, 0:1], colv[:, 8 + j:9 + j], -1e6, colv[:, 10 + j:11 + j],
                    op0=ALU.mult, op1=ALU.add), inc=1)
                vv = kb.emit("vector", lambda o, j=j: o.tensor_scalar(
                    m2t[:, 1:2], m2t[:, 0:1], 1e6, None, op0=ALU.add), inc=1)
                vv = kb.emit("vector", lambda o, j=j, rb_=rb_: o.tensor_copy(
                    sci[:, rb_:rb_ + 1], m2t[:, 1:2]), inc=1)
                vv = kb.emit("vector", lambda o, i=i: o.tensor_scalar(
                    m2t[:, 2:3], cst.iota128, float(i * 128), None, op0=ALU.add), inc=1)
                vv = kb.emit("vector", lambda o, rb_=rb_: o.tensor_copy(sct[:, rb_:rb_ + 1], m2t[:, 2:3]), inc=1)
                vv = kb.emit("vector", lambda o, j=j, rb_=rb_: o.scalar_tensor_tensor(
                    scw.ap().bitcast(F32)[:, rb_:rb_ + 1], wt12[:, j:j + 1], 1.0, colv[:, 8 + j:9 + j],
                    op0=ALU.mult, op1=ALU.mult), inc=1)
                bc = kb.breg(NSLOT * 512 - 1)
                gsc = kb.emit("gq", lambda o, i=i, j=j, rb_=rb_, bc=bc: o.indirect_dma_start(
                    wlist2.unsqueeze(1), IndirectOffsetOnAxis(sci[:, rb_:rb_ + 1], 0),
                    scw[:, rb_:rb_ + 1], None,
                    bounds_check=bc[0], oob_is_err=False),
                    waits=[("vector", vv)], inc=16)
                gsc = kb.emit("gq", lambda o, i=i, j=j, rb_=rb_, bc=bc: o.indirect_dma_start(
                    ilist2.unsqueeze(1), IndirectOffsetOnAxis(sci[:, rb_:rb_ + 1], 0),
                    sct[:, rb_:rb_ + 1], None,
                    bounds_check=bc[0], oob_is_err=False), inc=16)
                if (i * 2 + j) % 4 == 3:
                    kb.wait("vector", [("gq", gsc)])
        kb.flush()


def ph_e_prep(kb, nc, cst, w1f, w2f, w1b, w2b, ysum):
    g = 0
    for e in range(8):
        g = kb.emit("gq", lambda o, e=e: o.dma_start(
            out=w1b[e * 1024:(e + 1) * 1024, :], in_=w1f[e * 1024:(e + 1) * 1024, :]), inc=16)
        g = kb.emit("gq", lambda o, e=e: o.dma_start(
            out=w2b[e * 2048:(e + 1) * 2048, :], in_=w2f[e * 2048:(e + 1) * 2048, :]), inc=16)
    with contextlib.ExitStack() as ctx:
        zt = _mk(ctx, nc, "s", "pe_zero", [128, D], F32)
        vv = kb.emit("vector", lambda o: o.memset(zt[:], 0.0), inc=1)
        sz = 0
        for i in range(NT + 1):
            sz = kb.emit("sync", lambda o, i=i: o.dma_start(
                out=ysum[i * 128:(i + 1) * 128, :], in_=zt[:]),
                waits=[("vector", vv)] if i == 0 else [], inc=16)
        kb.flush()


def ph_e(kb, nc, cst, src1nat, w1b, w2b, ilist2, wlist2, idxw1, idxw2, ysum,
         slots=range(NSLOT)):
    with contextlib.ExitStack() as ctx:
        w1_sb = _mk(ctx, nc, "s", "pe_w1", [128, 8 * 2048], BF16)
        w2_sb = _mk(ctx, nc, "s", "pe_w2", [128, 16 * 1024], BF16)
        idx_sb = _mk(ctx, nc, "s", "pe_idx", [128, 16], I32)
        wrow_sb = _mk(ctx, nc, "s", "pe_wr", [128, 4], I32)
        xe_sb = _mk(ctx, nc, "s", "pe_xe", [128, 4 * 1024], BF16)
        xeT_sb = _mk(ctx, nc, "s", "pe_xeT", [128, 8 * 512], BF16)
        h_sb = _mk(ctx, nc, "s", "pe_h", [128, 16 * 512], BF16)
        y_sb = _mk(ctx, nc, "s", "pe_y", [128, 8 * 512], F32)
        yn_sb = _mk(ctx, nc, "s", "pe_yn", [128, 1024], F32)
        ps1 = _mk(ctx, nc, "p", "pe_ps1", [128, 512], F32)
        ps2 = _mk(ctx, nc, "p", "pe_ps2", [128, 512], F32)
        psT = _mk(ctx, nc, "p", "pe_psT", [128, 512], BF16)
        for s in slots:
            tprev = kb.val("tensor")
            ld = 0
            for k in range(4):
                ld = kb.emit("sync", lambda o, s=s, k=k: o.dma_start(
                    out=idx_sb[:, k:k + 1],
                    in_=ilist2[s * 512 + k * 128:s * 512 + (k + 1) * 128].unsqueeze(1)),
                    waits=[("gq", kb.val("gq")), ("tensor", tprev),
                           ("vector", kb.val("vector"))] if k == 0 else [], inc=16)
                ld = kb.emit("sync", lambda o, s=s, k=k: o.dma_start(
                    out=wrow_sb[:, k:k + 1],
                    in_=wlist2[s * 512 + k * 128:s * 512 + (k + 1) * 128].unsqueeze(1)),
                    inc=16)
            ld = kb.emit("sync", lambda o, s=s: o.dma_start(
                out=idx_sb[:, 4:5], in_=idxw1[s].unsqueeze(1)), inc=16)
            ld = kb.emit("sync", lambda o, s=s: o.dma_start(
                out=idx_sb[:, 5:6], in_=idxw2[s].unsqueeze(1)), inc=16)
            gg = 0
            for k in range(4):
                bct = kb.breg(NTOK + 127)
                bc1 = kb.breg(8 * 1024 - 1)
                bc2 = kb.breg(8 * 2048 - 1)
                gg = kb.emit("gq", lambda o, k=k, bct=bct: o.indirect_dma_start(
                    xe_sb[:, k * 1024:(k + 1) * 1024], None,
                    src1nat, IndirectOffsetOnAxis(idx_sb[:, k:k + 1], 0),
                    bounds_check=bct[0], oob_is_err=False),
                    waits=[("sync", ld)] if k == 0 else [], inc=16)
            for f in range(NF):
                vv = kb.emit("vector", lambda o, f=f: o.tensor_scalar(
                    idx_sb[:, 6:7], idx_sb[:, 4:5], f * 128, None, op0=ALU.add),
                    waits=[("sync", ld)], inc=1)
                gg = kb.emit("gq", lambda o, f=f, bc1=bc1: o.indirect_dma_start(
                    w1_sb[:, f * 2048:(f + 1) * 2048], None,
                    w1b, IndirectOffsetOnAxis(idx_sb[:, 6:7], 0),
                    bounds_check=bc1[0], oob_is_err=False),
                    waits=[("vector", vv)], inc=16)
                kb.wait("vector", [("gq", gg)])
            for f in range(16):
                vv = kb.emit("vector", lambda o, f=f: o.tensor_scalar(
                    idx_sb[:, 6:7], idx_sb[:, 5:6], f * 128, None, op0=ALU.add), inc=1)
                gg = kb.emit("gq", lambda o, f=f, bc2=bc2: o.indirect_dma_start(
                    w2_sb[:, f * 1024:(f + 1) * 1024], None,
                    w2b, IndirectOffsetOnAxis(idx_sb[:, 6:7], 0),
                    bounds_check=bc2[0], oob_is_err=False),
                    waits=[("vector", vv)], inc=16)
                kb.wait("vector", [("gq", gg)])
            vtp = 0
            for k in range(4):
                for f in range(NF):
                    tv = kb.emit("tensor", lambda o, k=k, f=f: o.transpose(
                        psT[:, 0:128], xe_sb[:, k * 1024 + f * 128:k * 1024 + (f + 1) * 128],
                        cst.ident_bf),
                        waits=[("gq", gg), ("vector", vtp)] if (k == 0 and f == 0) else
                        [("vector", vtp)], inc=1)
                    vtp = kb.emit("vector", lambda o, k=k, f=f: o.tensor_copy(
                        xeT_sb[:, f * 512 + k * 128:f * 512 + (k + 1) * 128], psT[:, 0:128]),
                        waits=[("tensor", tv)], inc=1)
            slast = {0: 0, 1: 0}
            for f2 in range(16):
                ps = (ps1, ps2)[f2 % 2]
                tv = 0
                for kc in range(NF):
                    tv = kb.emit("tensor", lambda o, f2=f2, kc=kc, ps=ps: o.matmul(
                        ps[:, 0:512],
                        w1_sb[:, kc * 2048 + f2 * 128:kc * 2048 + (f2 + 1) * 128],
                        xeT_sb[:, kc * 512:(kc + 1) * 512],
                        start=(kc == 0), stop=(kc == NF - 1)),
                        waits=[("vector", vtp), ("scalar", slast[f2 % 2])] if kc == 0 else [],
                        inc=1)
                sv = kb.emit("scalar", lambda o, f2=f2, ps=ps: o.activation(
                    h_sb[:, f2 * 512:(f2 + 1) * 512], ps[:, 0:512], AF.Gelu),
                    waits=[("tensor", tv)], inc=1)
                slast[f2 % 2] = sv
            vlast = {0: slast[0], 1: slast[1]}
            for o_ in range(NF):
                ps = (ps1, ps2)[o_ % 2]
                tv = 0
                for kc in range(16):
                    tv = kb.emit("tensor", lambda o, o_=o_, kc=kc, ps=ps: o.matmul(
                        ps[:, 0:512],
                        w2_sb[:, kc * 1024 + o_ * 128:kc * 1024 + (o_ + 1) * 128],
                        h_sb[:, kc * 512:(kc + 1) * 512],
                        start=(kc == 0), stop=(kc == 15)),
                        waits=[("scalar", slast[o_ % 2]), ("vector", vlast[o_ % 2])]
                        if kc == 0 else [], inc=1)
                vv = kb.emit("vector", lambda o, o_=o_, ps=ps: o.tensor_copy(
                    y_sb[:, o_ * 512:(o_ + 1) * 512], ps[:, 0:512]),
                    waits=[("tensor", tv)], inc=1)
                vlast[o_ % 2] = vv
            for k in range(4):
                vtr = 0
                for f in range(NF):
                    tv = kb.emit("tensor", lambda o, k=k, f=f: o.transpose(
                        ps1[:, 0:128], y_sb[:, f * 512 + k * 128:f * 512 + (k + 1) * 128],
                        cst.ident),
                        waits=([("vector", vlast[0]), ("vector", vlast[1])]
                               if (k == 0 and f == 0) else [("vector", vtr)]),
                        inc=1)
                    vtr = kb.emit("vector", lambda o, k=k, f=f: o.tensor_scalar(
                        yn_sb[:, f * 128:(f + 1) * 128], ps1[:, 0:128],
                        wrow_sb.ap().bitcast(F32)[:, k:k + 1], None, op0=ALU.mult),
                        waits=[("tensor", tv)], inc=1)
                gg = kb.emit("gq", lambda o, k=k, bct=bct: o.indirect_dma_start(
                    ysum, IndirectOffsetOnAxis(idx_sb[:, k:k + 1], 0),
                    yn_sb[:], None,
                    bounds_check=bct[0], oob_is_err=False, compute_op=ALU.add),
                    waits=[("vector", vtr)], inc=16)
                kb.wait("vector", [("gq", gg)])
                kb.wait("tensor", [("gq", gg)])
        kb.flush()


def ph_f(kb, nc, cst, xT, s0T, src1T, ysum, n2w, x1T, xn1T, xn1T_bf):
    with contextlib.ExitStack() as ctx:
        yrow = _mk(ctx, nc, "s", "pf_yrow", [128, 4 * 1024], F32)
        ysT = _mk(ctx, nc, "s", "pf_ysT", [128, 8 * 512], F32)
        srcc = _mk(ctx, nc, "s", "pf_src", [128, 8 * 512], F32)
        xc = _mk(ctx, nc, "s", "pf_x", [128, 8 * 512], F32)
        s0c = _mk(ctx, nc, "s", "pf_s0", [128, 8 * 512], F32)
        sq = _mk(ctx, nc, "s", "pf_sq", [128, 512], F32)
        rstd = _mk(ctx, nc, "s", "pf_rstd", [128, 512], F32)
        stat = _mk(ctx, nc, "s", "pf_stat", [128, 512], F32)
        nrow = _mk(ctx, nc, "s", "pf_nrow", [1, D], F32)
        ncol = _mk(ctx, nc, "s", "pf_ncol", [128, NF], F32)
        ev = _mk(ctx, nc, "s", "pf_ev", [128, 512], F32)
        evb = _mk(ctx, nc, "s", "pf_evb", [128, 512], BF16)
        ps1 = _mk(ctx, nc, "p", "pf_ps1", [128, 512], F32)
        ps2 = _mk(ctx, nc, "p", "pf_ps2", [128, 512], F32)
        ld = kb.emit("sync", lambda o: o.dma_start(out=nrow[0:1, :], in_=n2w), inc=16)
        kb.wait("tensor", [("sync", ld)])
        kb.wait("vector", [("sync", ld)])
        col_bias(kb, nc, cst, nrow, 0, ncol, 0, ps1)
        for ch in range(8):
            t0 = ch * 512
            ld = 0
            for k in range(4):
                ld = kb.emit("sync", lambda o, t0=t0, k=k: o.dma_start(
                    out=yrow[:, k * 1024:(k + 1) * 1024],
                    in_=ysum[t0 + k * 128:t0 + (k + 1) * 128, :]),
                    waits=[("tensor", kb.val("tensor")), ("vector", kb.val("vector")),
                           ("sync", kb.val("sync"))]
                    if k == 0 else [], inc=16)
            for f in range(NF):
                ld = kb.emit("sync", lambda o, t0=t0, f=f: o.dma_start(
                    out=srcc[:, f * 512:(f + 1) * 512],
                    in_=src1T[f * 128:(f + 1) * 128, t0:t0 + 512]), inc=16)
                ld = kb.emit("sync", lambda o, t0=t0, f=f: o.dma_start(
                    out=xc[:, f * 512:(f + 1) * 512],
                    in_=xT[f * 128:(f + 1) * 128, t0:t0 + 512]), inc=16)
                ld = kb.emit("sync", lambda o, t0=t0, f=f: o.dma_start(
                    out=s0c[:, f * 512:(f + 1) * 512],
                    in_=s0T[f * 128:(f + 1) * 128, t0:t0 + 512]), inc=16)
            vtr = 0
            for k in range(4):
                for f in range(NF):
                    tv = kb.emit("tensor", lambda o, k=k, f=f: o.transpose(
                        ps1[:, 0:128], yrow[:, k * 1024 + f * 128:k * 1024 + (f + 1) * 128],
                        cst.ident),
                        waits=[("sync", ld), ("vector", vtr)], inc=1)
                    vtr = kb.emit("vector", lambda o, k=k, f=f: o.tensor_copy(
                        ysT[:, f * 512 + k * 128:f * 512 + (k + 1) * 128], ps1[:, 0:128]),
                        waits=[("tensor", tv)], inc=1)
            vv = 0
            for f in range(NF):
                vv = kb.emit("vector", lambda o, f=f: o.tensor_add(
                    srcc[:, f * 512:(f + 1) * 512], srcc[:, f * 512:(f + 1) * 512],
                    ysT[:, f * 512:(f + 1) * 512]), inc=1)
            tvs = 0
            for f in range(NF):
                sv = kb.emit("scalar", lambda o, f=f: o.activation(
                    sq[:], srcc[:, f * 512:(f + 1) * 512], AF.Square),
                    waits=[("vector", vv), ("tensor", tvs)], inc=1)
                tvs = kb.emit("tensor", lambda o, f=f: o.matmul(
                    ps2[:, 0:512], cst.ones128, sq[:],
                    start=(f == 0), stop=(f == NF - 1)),
                    waits=[("scalar", sv)], inc=1)
            vv = kb.emit("vector", lambda o: o.tensor_scalar(
                rstd[:], ps2[:, 0:512], 1.0 / D, 1e-6, op0=ALU.mult, op1=ALU.add),
                waits=[("tensor", tvs)], inc=1)
            sv = kb.emit("scalar", lambda o: o.activation(sq[:], rstd[:], AF.Ln),
                         waits=[("vector", vv)], inc=1)
            sv = kb.emit("scalar", lambda o: o.activation(rstd[:], sq[:], AF.Exp, scale=-0.5),
                         inc=1)
            tvs = 0
            for f in range(NF):
                vv = kb.emit("vector", lambda o, f=f: o.scalar_tensor_tensor(
                    ev[:], srcc[:, f * 512:(f + 1) * 512], 1.0, rstd[:],
                    op0=ALU.mult, op1=ALU.mult), waits=[("scalar", sv)], inc=1)
                vv = kb.emit("vector", lambda o, f=f: o.tensor_scalar(
                    ev[:], ev[:], ncol[:, f:f + 1], None, op0=ALU.mult), inc=1)
                s2 = kb.emit("scalar", lambda o, f=f: o.activation(
                    ev[:], ev[:], AF.Silu), waits=[("vector", vv)], inc=1)
                vv = kb.emit("vector", lambda o, f=f: o.tensor_mul(
                    ev[:], ev[:], s0c[:, f * 512:(f + 1) * 512]),
                    waits=[("scalar", s2)], inc=1)
                vv = kb.emit("vector", lambda o, f=f: o.tensor_add(
                    xc[:, f * 512:(f + 1) * 512], xc[:, f * 512:(f + 1) * 512], ev[:]), inc=1)
                sd = kb.emit("sync", lambda o, f=f, t0=t0: o.dma_start(
                    out=x1T[f * 128:(f + 1) * 128, t0:t0 + 512],
                    in_=xc[:, f * 512:(f + 1) * 512]), waits=[("vector", vv)], inc=16)
            tvs = 0
            for f in range(NF):
                tvs = kb.emit("tensor", lambda o, f=f: o.matmul(
                    ps2[:, 0:512], cst.ones128, xc[:, f * 512:(f + 1) * 512],
                    start=(f == 0), stop=(f == NF - 1)),
                    waits=[("vector", vv)], inc=1)
            vv = kb.emit("vector", lambda o: o.tensor_scalar(
                stat[:], ps2[:, 0:512], 1.0 / D, None, op0=ALU.mult),
                waits=[("tensor", tvs)], inc=1)
            tvs = 0
            for f in range(NF):
                vv2 = kb.emit("vector", lambda o, f=f: o.tensor_tensor(
                    ev[:], xc[:, f * 512:(f + 1) * 512], stat[:], op=ALU.subtract),
                    inc=1)
                sv = kb.emit("scalar", lambda o, f=f: o.activation(
                    sq[:], ev[:], AF.Square), waits=[("vector", vv2), ("tensor", tvs)], inc=1)
                tvs = kb.emit("tensor", lambda o, f=f: o.matmul(
                    ps2[:, 0:512], cst.ones128, sq[:],
                    start=(f == 0), stop=(f == NF - 1)), waits=[("scalar", sv)], inc=1)
            vv = kb.emit("vector", lambda o: o.tensor_scalar(
                rstd[:], ps2[:, 0:512], 1.0 / D, 1e-5, op0=ALU.mult, op1=ALU.add),
                waits=[("tensor", tvs)], inc=1)
            sv = kb.emit("scalar", lambda o: o.activation(sq[:], rstd[:], AF.Ln),
                         waits=[("vector", vv)], inc=1)
            sv = kb.emit("scalar", lambda o: o.activation(rstd[:], sq[:], AF.Exp, scale=-0.5),
                         inc=1)
            for f in range(NF):
                vv = kb.emit("vector", lambda o, f=f: o.tensor_tensor(
                    ev[:], xc[:, f * 512:(f + 1) * 512], stat[:], op=ALU.subtract),
                    waits=[("scalar", sv)], inc=1)
                vv = kb.emit("vector", lambda o, f=f: o.tensor_mul(ev[:], ev[:], rstd[:]), inc=1)
                vv = kb.emit("vector", lambda o, f=f: o.tensor_copy(evb[:], ev[:]), inc=1)
                sd = kb.emit("sync", lambda o, f=f, t0=t0: o.dma_start(
                    out=xn1T[f * 128:(f + 1) * 128, t0:t0 + 512], in_=ev[:]),
                    waits=[("vector", vv)], inc=16)
                sd = kb.emit("sync", lambda o, f=f, t0=t0: o.dma_start(
                    out=xn1T_bf[f * 128:(f + 1) * 128, t0:t0 + 512], in_=evb[:]), inc=16)
                kb.wait("vector", [("sync", sd)])
        kb.flush()


def ph_f2(kb, nc, cst, xn1T, xn1T_bf, wq, wk, wv, bq, bk, bv, waT, wbT, ba, bb,
          convq, convk, convv, qTb, kTb, vTb, lnaT, bgT, lnbT):
    for wi, (w, brow_in, cw, outT, do_l2) in enumerate((
            (wq, bq, convq, qTb, True), (wk, bk, convk, kTb, True),
            (wv, bv, convv, vTb, False))):
        with contextlib.ExitStack() as ctx:
            w_sb = _mk(ctx, nc, "s", f"pf2_w{wi}", [128, 8 * 1024], BF16)
            xn_r = _mk(ctx, nc, "s", f"pf2_xn{wi}", [128, 8 * 2048], BF16)
            brow = _mk(ctx, nc, "s", f"pf2_br{wi}", [1, D], F32)
            bcol = _mk(ctx, nc, "s", f"pf2_bc{wi}", [128, NF], F32)
            ccol = _mk(ctx, nc, "s", f"pf2_cc{wi}", [128, NF * 4], F32)
            zp = _mk(ctx, nc, "s", f"pf2_zp{wi}", [128, 8 * 2052], F32)
            cv = _mk(ctx, nc, "s", f"pf2_cv{wi}", [128, 2048], F32)
            sq = _mk(ctx, nc, "s", f"pf2_sq{wi}", [128, 2048], F32)
            nrm = _mk(ctx, nc, "s", f"pf2_n{wi}", [128, 2048], F32)
            outb = _mk(ctx, nc, "s", f"pf2_o{wi}", [128, 2048], BF16)
            ps1 = _mk(ctx, nc, "p", f"pf2_ps1{wi}", [128, 512], F32)
            ps2 = _mk(ctx, nc, "p", f"pf2_ps2{wi}", [128, 512], F32)
            gw = 0
            for kc in range(NF):
                gw = kb.emit("gq", lambda o, kc=kc, w=w: o.dma_start(
                    out=w_sb[:, kc * 1024:(kc + 1) * 1024],
                    in_=w[kc * 128:(kc + 1) * 128, :]), inc=16)
            ld = kb.emit("sync", lambda o, brow_in=brow_in: o.dma_start(
                out=brow[0:1, :], in_=brow_in), inc=16)
            ldc = kb.emit("sync", lambda o, cw=cw: o.dma_start(
                out=ccol[:], in_=cw), inc=16)
            kb.wait("tensor", [("sync", ld)])
            kb.wait("vector", [("sync", ldc)])
            col_bias(kb, nc, cst, brow, 0, bcol, 0, ps1)
            for b in range(B_PER_CORE):
                tb = b * S
                ldx = 0
                for kc in range(NF):
                    ldx = kb.emit("sync", lambda o, kc=kc, tb=tb: o.dma_start(
                        out=xn_r[:, kc * 2048:(kc + 1) * 2048],
                        in_=xn1T_bf[kc * 128:(kc + 1) * 128, tb:tb + S]),
                        waits=[("tensor", kb.val("tensor")), ("vector", kb.val("vector"))]
                        if kc == 0 else [], inc=16)
                for ft in range(NF):
                    vpad = kb.emit("vector", lambda o, ft=ft: o.memset(
                        zp[:, ft * 2052:ft * 2052 + 2], 0.0), inc=1)
                    vpad = kb.emit("vector", lambda o, ft=ft: o.memset(
                        zp[:, ft * 2052 + 2050:ft * 2052 + 2052], 0.0), inc=1)
                    for chk in range(4):
                        ps = (ps1, ps2)[chk % 2]
                        tv = 0
                        for kc in range(NF):
                            tv = kb.emit("tensor", lambda o, ft=ft, chk=chk, kc=kc, ps=ps: o.matmul(
                                ps[:, 0:512],
                                w_sb[:, kc * 1024 + ft * 128:kc * 1024 + (ft + 1) * 128],
                                xn_r[:, kc * 2048 + chk * 512:kc * 2048 + (chk + 1) * 512],
                                start=(kc == 0), stop=(kc == NF - 1)),
                                waits=[("sync", ldx), ("gq", gw), ("vector", kb.val("vector"))]
                                if kc == 0 else [], inc=1)
                        vv = kb.emit("vector", lambda o, ft=ft, chk=chk, ps=ps: o.tensor_scalar(
                            cv[:, chk * 512:(chk + 1) * 512], ps[:, 0:512],
                            bcol[:, ft:ft + 1], None, op0=ALU.add),
                            waits=[("tensor", tv)], inc=1)
                    sv = kb.emit("scalar", lambda o, ft=ft: o.activation(
                        zp[:, ft * 2052 + 2:ft * 2052 + 2050], cv[:], AF.Silu),
                        waits=[("vector", vv)], inc=1)
                    base = ft * 2052 + 2
                    vv = kb.emit("vector", lambda o, ft=ft, base=base: o.tensor_scalar(
                        cv[:], zp[:, base - 2:base + 2046], ccol[:, ft * 4:ft * 4 + 1],
                        None, op0=ALU.mult), waits=[("scalar", sv)], inc=1)
                    for j in range(1, 4):
                        vv = kb.emit("vector", lambda o, ft=ft, j=j, base=base: o.scalar_tensor_tensor(
                            cv[:], zp[:, base - 2 + j:base - 2 + j + 2048],
                            ccol[:, ft * 4 + j:ft * 4 + j + 1], cv[:],
                            op0=ALU.mult, op1=ALU.add), inc=1)
                    sv = kb.emit("scalar", lambda o, ft=ft: o.activation(
                        cv[:], cv[:], AF.Silu), waits=[("vector", vv)], inc=1)
                    vv = kb.emit("vector", lambda o, ft=ft: o.tensor_copy(
                        zp[:, ft * 2052 + 2:ft * 2052 + 2050], cv[:]),
                        waits=[("scalar", sv)], inc=1)
                if do_l2:
                    tvs = 0
                    for chk in range(4):
                        for f in range(NF):
                            sv = kb.emit("scalar", lambda o, f=f, chk=chk: o.activation(
                                sq[:, 0:512], zp[:, f * 2052 + 2 + chk * 512:f * 2052 + 2 + (chk + 1) * 512],
                                AF.Square), waits=[("vector", kb.val("vector")), ("tensor", tvs)], inc=1)
                            tvs = kb.emit("tensor", lambda o, f=f, chk=chk: o.matmul(
                                ps2[:, 0:512], cst.ones128, sq[:, 0:512],
                                start=(f == 0), stop=(f == NF - 1)),
                                waits=[("scalar", sv)], inc=1)
                        vv = kb.emit("vector", lambda o, chk=chk: o.tensor_copy(
                            sq[:, 0:512], ps2[:, 0:512]), waits=[("tensor", tvs)], inc=1)
                        sv = kb.emit("scalar", lambda o, chk=chk: o.activation(
                            sq[:, 512:1024], sq[:, 0:512], AF.Sqrt),
                            waits=[("vector", vv)], inc=1)
                        vv = kb.emit("vector", lambda o, chk=chk: o.tensor_scalar(
                            sq[:, 512:1024], sq[:, 512:1024], 1e-6, None, op0=ALU.add),
                            waits=[("scalar", sv)], inc=1)
                        vv = kb.emit("vector", lambda o, chk=chk: o.reciprocal(
                            nrm[:, chk * 512:(chk + 1) * 512], sq[:, 512:1024]), inc=1)
                else:
                    vv = kb.emit("vector", lambda o: o.memset(nrm[:], 1.0), inc=1)
                for f in range(NF):
                    vv = kb.emit("vector", lambda o, f=f: o.tensor_tensor(
                        outb[:], zp[:, f * 2052 + 2:f * 2052 + 2050], nrm[:], op=ALU.mult),
                        inc=1)
                    sd = kb.emit("sync", lambda o, f=f, tb=tb, outT=outT: o.dma_start(
                        out=outT[f * 128:(f + 1) * 128, tb:tb + S], in_=outb[:]),
                        waits=[("vector", vv)], inc=16)
                    kb.wait("vector", [("sync", sd)])
            kb.flush()
    with contextlib.ExitStack() as ctx:
        wab = _mk(ctx, nc, "s", "pf2_wab", [128, 2 * NF * 16], F32)
        xn32 = _mk(ctx, nc, "s", "pf2_x32", [128, 8 * 512], F32)
        babb = _mk(ctx, nc, "s", "pf2_babb", [1, 32], F32)
        bcol2 = _mk(ctx, nc, "s", "pf2_bcg", [16, 2], F32)
        gout = _mk(ctx, nc, "s", "pf2_gout", [16, 6 * 512], F32)
        ps1 = _mk(ctx, nc, "p", "pf2_ps1g", [128, 512], F32)
        ld = 0
        for wi, w in enumerate((waT, wbT)):
            for kc in range(NF):
                ld = kb.emit("sync", lambda o, wi=wi, kc=kc, w=w: o.dma_start(
                    out=wab[:, (wi * NF + kc) * 16:(wi * NF + kc + 1) * 16],
                    in_=w[kc * 128:(kc + 1) * 128, :]), inc=16)
        ld = kb.emit("sync", lambda o: o.dma_start(out=babb[0:1, 0:16], in_=ba), inc=16)
        ld = kb.emit("sync", lambda o: o.dma_start(out=babb[0:1, 16:32], in_=bb), inc=16)
        tv = kb.emit("tensor", lambda o: o.transpose(
            ps1[0:16, 0:2], babb[0:1, 0:32].rearrange("a (b c) -> a b c", b=2)
            .transpose([0, 2, 1]).squeeze(0), cst.ident[0:1, 0:1]),
            waits=[("sync", ld)], inc=1) if False else 0
        tv = kb.emit("tensor", lambda o: o.transpose(
            ps1[0:16, 0:1], babb[0:1, 0:16], cst.ident[0:1, 0:1]),
            waits=[("sync", ld)], inc=1)
        vv = kb.emit("vector", lambda o: o.tensor_copy(bcol2[:, 0:1], ps1[0:16, 0:1]),
                     waits=[("tensor", tv)], inc=1)
        tv = kb.emit("tensor", lambda o: o.transpose(
            ps1[0:16, 0:1], babb[0:1, 16:32], cst.ident[0:1, 0:1]),
            waits=[("vector", vv)], inc=1)
        vv = kb.emit("vector", lambda o: o.tensor_copy(bcol2[:, 1:2], ps1[0:16, 0:1]),
                     waits=[("tensor", tv)], inc=1)
        for ch in range(8):
            t0 = ch * 512
            ldx = 0
            for kc in range(NF):
                ldx = kb.emit("sync", lambda o, kc=kc, t0=t0: o.dma_start(
                    out=xn32[:, kc * 512:(kc + 1) * 512],
                    in_=xn1T[kc * 128:(kc + 1) * 128, t0:t0 + 512]),
                    waits=[("tensor", kb.val("tensor")), ("vector", kb.val("vector"))]
                    if kc == 0 else [], inc=16)
            for wi, (dstT, do_ln) in enumerate(((lnaT, True), (lnbT, True))):
                tv = 0
                for kc in range(NF):
                    tv = kb.emit("tensor", lambda o, wi=wi, kc=kc: o.matmul(
                        ps1[0:16, 0:512], wab[:, (wi * NF + kc) * 16:(wi * NF + kc + 1) * 16],
                        xn32[:, kc * 512:(kc + 1) * 512],
                        start=(kc == 0), stop=(kc == NF - 1)),
                        waits=[("sync", ldx), ("vector", kb.val("vector"))] if kc == 0 else [],
                        inc=1)
                sv = kb.emit("scalar", lambda o, wi=wi: o.activation(
                    gout[:, 0:512], ps1[0:16, 0:512], AF.Sigmoid,
                    bias=bcol2[0:16, wi:wi + 1]), waits=[("tensor", tv)], inc=1)
                if wi == 1:
                    sd = kb.emit("sync", lambda o, t0=t0: o.dma_start(
                        out=bgT[:, t0:t0 + 512], in_=gout[0:16, 0:512]),
                        waits=[("scalar", sv)], inc=16)
                vv = kb.emit("vector", lambda o: o.tensor_scalar(
                    gout[:, 512:1024], gout[:, 0:512], 1e-38, None, op0=ALU.max),
                    waits=[("scalar", sv)], inc=1)
                sv = kb.emit("scalar", lambda o: o.activation(
                    gout[:, 1024:1536], gout[:, 512:1024], AF.Ln),
                    waits=[("vector", vv)], inc=1)
                sd = kb.emit("sync", lambda o, wi=wi, t0=t0, dstT=dstT: o.dma_start(
                    out=dstT[:, t0:t0 + 512], in_=gout[0:16, 1024:1536]),
                    waits=[("scalar", sv)], inc=16)
                kb.wait("scalar", [("sync", sd)])
                kb.wait("vector", [("sync", sd)])
        kb.flush()


def ph_g(kb, nc, cst, qTb, kTb, vTb, lnaT, bgT, lnbT, hot16_d, mGA_d, mGAp_d, mGD_d,
         woT, bo, s1T, otmpG):
    CH = 128
    NCH = S // CH
    with contextlib.ExitStack() as ctx:
        hot = _mk(ctx, nc, "s", "pg_hot", [16, 16 * 128], F32)
        mA = _mk(ctx, nc, "s", "pg_mA", [128, 128], F32)
        mAp = _mk(ctx, nc, "s", "pg_mAp", [128, 128], F32)
        mD = _mk(ctx, nc, "s", "pg_mD", [128, 128], F32)
        q_sb = _mk(ctx, nc, "s", "pg_q", [128, 8 * 128], BF16)
        k_sb = _mk(ctx, nc, "s", "pg_k", [128, 8 * 128], BF16)
        v_sb = _mk(ctx, nc, "s", "pg_v", [128, 8 * 128], BF16)
        ga = _mk(ctx, nc, "s", "pg_ga", [16, 6 * 128], F32)
        ellc = _mk(ctx, nc, "s", "pg_ellc", [128, 112], F32)
        ellr = _mk(ctx, nc, "s", "pg_ellr", [1, 16], F32)
        Ssb = _mk(ctx, nc, "s", "pg_S", [64, 32 * 64], F32)
        Stsb = _mk(ctx, nc, "s", "pg_St", [64, 32 * 64], F32)
        Sbf = _mk(ctx, nc, "s", "pg_Sbf", [64, 32 * 64], BF16)
        Stbf = _mk(ctx, nc, "s", "pg_Stbf", [64, 32 * 64], BF16)
        kst = _mk(ctx, nc, "s", "pg_kst", [64, 128], BF16)
        qst = _mk(ctx, nc, "s", "pg_qst", [64, 128], BF16)
        vst2 = _mk(ctx, nc, "s", "pg_vst", [64, 128], BF16)
        knat = _mk(ctx, nc, "s", "pg_knat", [128, 64], BF16)
        vnat = _mk(ctx, nc, "s", "pg_vnat", [128, 64], F32)
        Bbf = _mk(ctx, nc, "s", "pg_B", [128, 128], BF16)
        Bpbf = _mk(ctx, nc, "s", "pg_Bp", [128, 128], BF16)
        Gbf = _mk(ctx, nc, "s", "pg_G", [128, 128], BF16)
        Gpbf = _mk(ctx, nc, "s", "pg_Gp", [128, 128], BF16)
        G2bf = _mk(ctx, nc, "s", "pg_G2", [128, 128], BF16)
        G2pbf = _mk(ctx, nc, "s", "pg_G2p", [128, 128], BF16)
        Ybf = _mk(ctx, nc, "s", "pg_Y", [128, 64], BF16)
        Ytbf = _mk(ctx, nc, "s", "pg_Yt", [64, 128], BF16)
        Ysbf = _mk(ctx, nc, "s", "pg_Ys", [128, 64], BF16)
        P2bf = _mk(ctx, nc, "s", "pg_P2", [128, 128], BF16)
        tmp1 = _mk(ctx, nc, "s", "pg_t1", [128, 128], F32)
        tmp2 = _mk(ctx, nc, "s", "pg_t2", [128, 128], F32)
        tmp3 = _mk(ctx, nc, "s", "pg_t3", [128, 64], F32)
        gcol = _mk(ctx, nc, "s", "pg_gcol", [64, 2], F32)
        och = _mk(ctx, nc, "s", "pg_och", [64, 128], BF16)
        psA = _mk(ctx, nc, "p", "pg_psA", [128, 128], F32)
        psB = _mk(ctx, nc, "p", "pg_psB", [128, 128], F32)
        psC = _mk(ctx, nc, "p", "pg_psC", [128, 128], F32)
        psD = _mk(ctx, nc, "p", "pg_psD", [128, 64], F32)
        psE = _mk(ctx, nc, "p", "pg_psE", [128, 16], F32)
        psT = _mk(ctx, nc, "p", "pg_psT", [128, 128], BF16)
        ld = kb.emit("sync", lambda o: o.dma_start(out=hot[:], in_=hot16_d), inc=16)
        ld2 = kb.emit("sync", lambda o: o.dma_start(out=mA[:], in_=mGA_d), inc=16)
        ld2 = kb.emit("sync", lambda o: o.dma_start(out=mAp[:], in_=mGAp_d), inc=16)
        ld2 = kb.emit("sync", lambda o: o.dma_start(out=mD[:], in_=mGD_d), inc=16)
        vv = kb.emit("vector", lambda o: o.memset(Ssb[:], 0.0), waits=[("sync", ld2)], inc=1)
        vv = kb.emit("vector", lambda o: o.memset(Stsb[:], 0.0), inc=1)
        vv = kb.emit("vector", lambda o: o.memset(Sbf[:], 0.0), inc=1)
        vv = kb.emit("vector", lambda o: o.memset(Stbf[:], 0.0), inc=1)
        kb.wait("tensor", [("vector", vv), ("sync", ld)])
        for b in range(B_PER_CORE):
            for c in range(NCH):
                t0 = b * S + c * CH
                ldx = 0
                for kc in range(NF):
                    ldx = kb.emit("sync", lambda o, kc=kc, t0=t0: o.dma_start(
                        out=q_sb[:, kc * 128:(kc + 1) * 128],
                        in_=qTb[kc * 128:(kc + 1) * 128, t0:t0 + CH]),
                        waits=[("tensor", kb.val("tensor")), ("vector", kb.val("vector")),
                               ("scalar", kb.val("scalar"))] if kc == 0 else [], inc=16)
                    ldx = kb.emit("sync", lambda o, kc=kc, t0=t0: o.dma_start(
                        out=k_sb[:, kc * 128:(kc + 1) * 128],
                        in_=kTb[kc * 128:(kc + 1) * 128, t0:t0 + CH]), inc=16)
                    ldx = kb.emit("sync", lambda o, kc=kc, t0=t0: o.dma_start(
                        out=v_sb[:, kc * 128:(kc + 1) * 128],
                        in_=vTb[kc * 128:(kc + 1) * 128, t0:t0 + CH]), inc=16)
                ldx = kb.emit("sync", lambda o, t0=t0: o.dma_start(
                    out=ga[:, 0:128], in_=lnaT[:, t0:t0 + CH]), inc=16)
                ldx = kb.emit("sync", lambda o, t0=t0: o.dma_start(
                    out=ga[:, 128:256], in_=bgT[:, t0:t0 + CH]), inc=16)
                ldx = kb.emit("sync", lambda o, t0=t0: o.dma_start(
                    out=ga[:, 256:384], in_=lnbT[:, t0:t0 + CH]), inc=16)
                vv = kb.emit("vector", lambda o: o.memset(ga[:, 640:768], 0.0),
                             waits=[("sync", ldx)], inc=1)
                vv = kb.emit("vector", lambda o: o.tensor_tensor_scan(
                    ga[:, 384:512], ga[:, 0:128], ga[:, 640:768], 0.0,
                    op0=ALU.add, op1=ALU.add), inc=1)
                vv = kb.emit("vector", lambda o: o.tensor_add(
                    ga[:, 512:640], ga[:, 384:512], ga[:, 256:384]), inc=1)
                tv = kb.emit("tensor", lambda o: o.transpose(
                    psA[0:128, 0:16], ga[0:16, 384:512], cst.ident[0:16, 0:16]),
                    waits=[("vector", vv)], inc=1)
                vv = kb.emit("vector", lambda o: o.tensor_copy(
                    ellc[:, 0:16], psA[0:128, 0:16]), waits=[("tensor", tv)], inc=1)
                tv = kb.emit("tensor", lambda o: o.transpose(
                    psA[0:128, 0:16], ga[0:16, 512:640], cst.ident[0:16, 0:16]),
                    waits=[("vector", vv)], inc=1)
                vv = kb.emit("vector", lambda o: o.tensor_copy(
                    ellc[:, 16:32], psA[0:128, 0:16]), waits=[("tensor", tv)], inc=1)
                sv = kb.emit("scalar", lambda o: o.activation(
                    ellc[:, 32:48], ellc[:, 0:16], AF.Exp), waits=[("vector", vv)], inc=1)
                vv = kb.emit("vector", lambda o: o.tensor_tensor(
                    ellc[:, 96:112], ellc[:, 16:32], ellc[:, 0:16],
                    op=ALU.subtract), inc=1)
                sv = kb.emit("scalar", lambda o: o.activation(
                    ellc[:, 48:64], ellc[:, 96:112], AF.Exp),
                    waits=[("vector", vv)], inc=1)
                tv = kb.emit("tensor", lambda o: o.transpose(
                    psA[0:1, 0:16], ga[0:16, 511:512], cst.ident[0:16, 0:16]),
                    waits=[("vector", kb.val("vector"))], inc=1)
                vv = kb.emit("vector", lambda o: o.tensor_copy(
                    ellr[0:1, 0:16], psA[0:1, 0:16]), waits=[("tensor", tv)], inc=1)
                tv = kb.emit("tensor", lambda o: o.matmul(
                    psE[0:128, 0:16], cst.ones1[0:1, 0:128], ellr[0:1, 0:16],
                    start=True, stop=True),
                    waits=[("vector", vv), ("scalar", kb.val("scalar"))], inc=1)
                vv = kb.emit("vector", lambda o: o.tensor_tensor(
                    ellc[:, 96:112], psE[0:128, 0:16], ellc[:, 0:16],
                    op=ALU.subtract), waits=[("tensor", tv)], inc=1)
                sv = kb.emit("scalar", lambda o: o.activation(
                    ellc[:, 64:80], ellc[:, 96:112], AF.Exp),
                    waits=[("vector", vv)], inc=1)
                sv = kb.emit("scalar", lambda o: o.activation(
                    ellc[:, 80:96], psE[0:128, 0:16], AF.Exp), inc=1)
                kb.wait("tensor", [("scalar", sv)])
                for h in range(H0):
                    hb = (h % 2) * 64
                    hf = h // 2
                    qsl = q_sb[hb:hb + 64, hf * 128:(hf + 1) * 128]
                    ksl = k_sb[hb:hb + 64, hf * 128:(hf + 1) * 128]
                    vsl = v_sb[hb:hb + 64, hf * 128:(hf + 1) * 128]
                    Scur = Ssb[:, (b * 16 + h) * 64:(b * 16 + h + 1) * 64]
                    Stcur = Stsb[:, (b * 16 + h) * 64:(b * 16 + h + 1) * 64]
                    Scurb = Sbf[:, (b * 16 + h) * 64:(b * 16 + h + 1) * 64]
                    Stcurb = Stbf[:, (b * 16 + h) * 64:(b * 16 + h + 1) * 64]
                    vprev = kb.val("vector")
                    vst = kb.emit("vector", lambda o, ksl=ksl: o.tensor_copy(kst[:], ksl),
                                  waits=[("tensor", kb.val("tensor"))], inc=1)
                    vst = kb.emit("vector", lambda o, qsl=qsl: o.tensor_copy(qst[:], qsl), inc=1)
                    vst = kb.emit("vector", lambda o, vsl=vsl: o.tensor_copy(vst2[:], vsl), inc=1)
                    tv = kb.emit("tensor", lambda o: o.transpose(
                        psT[0:128, 0:64], kst[:], cst.ident_bf[0:64, 0:64]),
                        waits=[("vector", vst)], inc=1)
                    vv = kb.emit("vector", lambda o: o.tensor_copy(knat[:], psT[0:128, 0:64]),
                                 waits=[("tensor", tv)], inc=1)
                    tv = kb.emit("tensor", lambda o: o.transpose(
                        psT[0:128, 0:64], vst2[:], cst.ident_bf[0:64, 0:64]),
                        waits=[("vector", vv)], inc=1)
                    vv = kb.emit("vector", lambda o: o.tensor_copy(
                        vnat[:], psT[0:128, 0:64]), waits=[("tensor", tv)], inc=1)
                    tkk = kb.emit("tensor", lambda o: o.matmul(
                        psA[:, 0:128], kst[:], kst[:], start=True, stop=True),
                        waits=[("vector", vst)], inc=1)
                    trb = kb.emit("tensor", lambda o, h=h: o.matmul(
                        psB[:, 0:128], hot[:, h * 128:(h + 1) * 128], ga[0:16, 384:512],
                        start=True, stop=True), inc=1)
                    trb2 = kb.emit("tensor", lambda o, h=h: o.matmul(
                        psC[:, 0:128], hot[:, h * 128:(h + 1) * 128], ga[0:16, 512:640],
                        start=True, stop=True), inc=1)
                    vv = kb.emit("vector", lambda o, h=h: o.tensor_scalar(
                        tmp1[:], psB[:, 0:128], ellc[:, 16 + h:17 + h], None,
                        op0=ALU.subtract), waits=[("tensor", trb)], inc=1)
                    vv = kb.emit("vector", lambda o: o.tensor_add(tmp1[:], tmp1[:], mA[:]), inc=1)
                    sv = kb.emit("scalar", lambda o: o.activation(
                        tmp1[:], tmp1[:], AF.Exp, scale=-1.0), waits=[("vector", vv)], inc=1)
                    vv = kb.emit("vector", lambda o: o.scalar_tensor_tensor(
                        Bbf[:], psA[:, 0:128], -1.0, tmp1[:], op0=ALU.mult, op1=ALU.mult),
                        waits=[("scalar", sv), ("tensor", tkk)], inc=1)
                    vv = kb.emit("vector", lambda o, h=h: o.tensor_scalar(
                        tmp2[:], psC[:, 0:128], ellc[:, h:h + 1], None,
                        op0=ALU.subtract), waits=[("tensor", trb2)], inc=1)
                    vv = kb.emit("vector", lambda o: o.tensor_add(tmp2[:], tmp2[:], mAp[:]), inc=1)
                    sv = kb.emit("scalar", lambda o: o.activation(
                        tmp2[:], tmp2[:], AF.Exp, scale=-1.0) if False else o.activation(
                        tmp2[:], tmp2[:], AF.Exp, scale=1.0), waits=[("vector", vv)], inc=1)
                    vv = kb.emit("vector", lambda o: o.scalar_tensor_tensor(
                        Bpbf[:], psA[:, 0:128], -1.0, tmp2[:], op0=ALU.mult, op1=ALU.mult),
                        waits=[("scalar", sv)], inc=1)
                    vb_built = vv
                    tks = kb.emit("tensor", lambda o, Scurb=Scurb: o.matmul(
                        psD[:, 0:64], kst[:], Scurb, start=True, stop=True),
                        waits=[("vector", vb_built)], inc=1)
                    vv = kb.emit("vector", lambda o, h=h: o.tensor_scalar(
                        tmp3[:, 64:128] if False else tmp1[:, 0:64], psD[:, 0:64],
                        ellc[:, 32 + h:33 + h], None, op0=ALU.mult),
                        waits=[("tensor", tks)], inc=1)
                    vv = kb.emit("vector", lambda o: o.tensor_tensor(
                        tmp1[:, 0:64], vnat[:], tmp1[:, 0:64], op=ALU.subtract), inc=1)
                    vv = kb.emit("vector", lambda o, h=h: o.tensor_scalar(
                        Ybf[:], tmp1[:, 0:64], ellc[:, 48 + h:49 + h], None,
                        op0=ALU.mult), inc=1)
                    tv = kb.emit("tensor", lambda o: o.matmul(
                        psD[:, 0:64], Bpbf[:], Ybf[:], start=True, stop=True),
                        waits=[("vector", vv)], inc=1)
                    vv = kb.emit("vector", lambda o: o.tensor_tensor(
                        Ybf[:], Ybf[:], psD[:, 0:64], op=ALU.add),
                        waits=[("tensor", tv)], inc=1)
                    tv = kb.emit("tensor", lambda o: o.matmul(
                        psA[:, 0:128], Bpbf[:], Bbf[:], start=True, stop=True),
                        waits=[("vector", vv)], inc=1)
                    vg = kb.emit("vector", lambda o: o.tensor_copy(Gbf[:], psA[:, 0:128]),
                                 waits=[("tensor", tv)], inc=1)
                    tv = kb.emit("tensor", lambda o: o.matmul(
                        psB[:, 0:128], Bbf[:], Bpbf[:], start=True, stop=True), inc=1)
                    vg = kb.emit("vector", lambda o: o.tensor_copy(Gpbf[:], psB[:, 0:128]),
                                 waits=[("tensor", tv)], inc=1)
                    gcur, gpcur, gnxt, gpnxt = Gbf, Gpbf, G2bf, G2pbf
                    for m in range(1, 7):
                        tv = kb.emit("tensor", lambda o, gpcur=gpcur: o.matmul(
                            psD[:, 0:64], gpcur[:], Ybf[:], start=True, stop=True),
                            waits=[("vector", vg)], inc=1)
                        vv = kb.emit("vector", lambda o: o.tensor_tensor(
                            Ybf[:], Ybf[:], psD[:, 0:64], op=ALU.add),
                            waits=[("tensor", tv)], inc=1)
                        if m < 6:
                            tv = kb.emit("tensor", lambda o, gpcur=gpcur, gcur=gcur:
                                o.matmul(psA[:, 0:128], gpcur[:], gcur[:],
                                         start=True, stop=True),
                                waits=[("vector", vv)], inc=1)
                            vg = kb.emit("vector", lambda o, gnxt=gnxt: o.tensor_copy(
                                gnxt[:], psA[:, 0:128]), waits=[("tensor", tv)], inc=1)
                            tv = kb.emit("tensor", lambda o, gcur=gcur, gpcur=gpcur:
                                o.matmul(psB[:, 0:128], gcur[:], gpcur[:],
                                         start=True, stop=True), inc=1)
                            vg = kb.emit("vector", lambda o, gpnxt=gpnxt: o.tensor_copy(
                                gpnxt[:], psB[:, 0:128]), waits=[("tensor", tv)], inc=1)
                            gcur, gpcur, gnxt, gpnxt = gnxt, gpnxt, gcur, gpcur
                        else:
                            vg = vv
                    tv = kb.emit("tensor", lambda o: o.transpose(
                        psT[0:64, 0:128], Ybf[:], cst.ident_bf),
                        waits=[("vector", vg)], inc=1)
                    vv = kb.emit("vector", lambda o: o.tensor_copy(
                        Ytbf[:], psT[0:64, 0:128]), waits=[("tensor", tv)], inc=1)
                    tv = kb.emit("tensor", lambda o: o.matmul(
                        psC[:, 0:128], Ytbf[:], qst[:], start=True, stop=True),
                        waits=[("vector", vv)], inc=1)
                    vv = kb.emit("vector", lambda o, h=h: o.tensor_scalar(
                        tmp1[:], psB[:, 0:128] if False else tmp1[:], 0.0, None, op0=ALU.mult),
                        inc=1) if False else 0
                    trb = kb.emit("tensor", lambda o, h=h: o.matmul(
                        psB[:, 0:128], hot[:, h * 128:(h + 1) * 128], ga[0:16, 384:512],
                        start=True, stop=True), waits=[("vector", vg)], inc=1)
                    vv = kb.emit("vector", lambda o, h=h: o.tensor_scalar(
                        tmp1[:], psB[:, 0:128], ellc[:, h:h + 1], None, op0=ALU.subtract),
                        waits=[("tensor", trb)], inc=1)
                    vv = kb.emit("vector", lambda o: o.tensor_add(tmp1[:], tmp1[:], mD[:]), inc=1)
                    sv = kb.emit("scalar", lambda o: o.activation(
                        tmp1[:], tmp1[:], AF.Exp), waits=[("vector", vv)], inc=1)
                    vv = kb.emit("vector", lambda o: o.tensor_tensor(
                        P2bf[:], tmp1[:], psC[:, 0:128], op=ALU.mult),
                        waits=[("scalar", sv), ("tensor", tv)], inc=1)
                    tv = kb.emit("tensor", lambda o: o.matmul(
                        psD[:, 0:64], P2bf[:], knat[:], start=True, stop=True),
                        waits=[("vector", vv)], inc=1)
                    tv2 = kb.emit("tensor", lambda o, Stcurb=Stcurb: o.matmul(
                        psC[:, 0:64], qst[:], Stcurb, start=True, stop=True),
                        waits=[("vector", vv)], inc=1)
                    vv = kb.emit("vector", lambda o, h=h: o.tensor_scalar(
                        tmp1[:, 0:64], psC[:, 0:64], ellc[:, 32 + h:33 + h], None,
                        op0=ALU.mult), waits=[("tensor", tv2)], inc=1)
                    vv = kb.emit("vector", lambda o: o.tensor_tensor(
                        tmp1[:, 0:64], tmp1[:, 0:64], psD[:, 0:64], op=ALU.add), inc=1)
                    vv = kb.emit("vector", lambda o: o.tensor_copy(
                        tmp2.bitcast(BF16)[:, 0:64], tmp1[:, 0:64]), inc=1)
                    tv = kb.emit("tensor", lambda o: o.transpose(
                        psT[0:64, 0:128], tmp2.bitcast(BF16)[:, 0:64], cst.ident_bf),
                        waits=[("vector", vv)], inc=1)
                    vv = kb.emit("vector", lambda o: o.tensor_copy(och[:], psT[0:64, 0:128]),
                                 waits=[("tensor", tv)], inc=1)
                    sd = kb.emit("sync", lambda o, h=h, t0=t0: o.dma_start(
                        out=otmpG[h * 64:(h + 1) * 64, t0:t0 + CH], in_=och[:]),
                        waits=[("vector", vv)], inc=16)
                    vv = kb.emit("vector", lambda o, h=h: o.tensor_scalar(
                        Ysbf[:], Ybf[:], ellc[:, 64 + h:65 + h], None,
                        op0=ALU.mult), inc=1)
                    tv = kb.emit("tensor", lambda o: o.matmul(
                        psD[0:64, 0:64], knat[:], Ysbf[:], start=True, stop=True),
                        waits=[("vector", vv)], inc=1)
                    tv2 = kb.emit("tensor", lambda o: o.matmul(
                        psC[0:64, 0:64], Ysbf[:], knat[:], start=True, stop=True), inc=1)
                    vv = kb.emit("vector", lambda o, Scur=Scur, h=h: o.scalar_tensor_tensor(
                        Scur, Scur, ellc[0:64, 80 + h:81 + h], psD[0:64, 0:64],
                        op0=ALU.mult, op1=ALU.add), waits=[("tensor", tv)], inc=1)
                    vv = kb.emit("vector", lambda o, Stcur=Stcur, h=h: o.scalar_tensor_tensor(
                        Stcur, Stcur, ellc[0:64, 80 + h:81 + h], psC[0:64, 0:64],
                        op0=ALU.mult, op1=ALU.add), waits=[("tensor", tv2)], inc=1)
                    vv = kb.emit("vector", lambda o, Scur=Scur, Scurb=Scurb: o.tensor_copy(
                        Scurb, Scur), inc=1)
                    vv = kb.emit("vector", lambda o, Stcur=Stcur, Stcurb=Stcurb: o.tensor_copy(
                        Stcurb, Stcur), inc=1)
                    kb.wait("tensor", [("vector", vv), ("sync", sd)])
        kb.flush()
    with contextlib.ExitStack() as ctx:
        wo_sb = _mk(ctx, nc, "s", "pg_wo", [128, 8 * 1024], BF16)
        oin = _mk(ctx, nc, "s", "pg_oin", [128, 8 * 512], BF16)
        borow = _mk(ctx, nc, "s", "pg_borow", [1, D], F32)
        bocol = _mk(ctx, nc, "s", "pg_bocol", [128, NF], F32)
        sev = _mk(ctx, nc, "s", "pg_sev", [128, 512], F32)
        ps4 = _mk(ctx, nc, "p", "pg_ps4", [128, 512], F32)
        ps5 = _mk(ctx, nc, "p", "pg_ps5", [128, 512], F32)
        gw = 0
        for kc in range(NF):
            gw = kb.emit("gq", lambda o, kc=kc: o.dma_start(
                out=wo_sb[:, kc * 1024:(kc + 1) * 1024],
                in_=woT[kc * 128:(kc + 1) * 128, :]), inc=16)
        ld = kb.emit("sync", lambda o: o.dma_start(out=borow[0:1, :], in_=bo), inc=16)
        kb.wait("tensor", [("sync", ld)])
        kb.wait("vector", [("sync", ld)])
        col_bias(kb, nc, cst, borow, 0, bocol, 0, ps4)
        vlast = kb.val("vector")
        for qc in range(8):
            t0 = qc * 512
            ldo = 0
            for kc in range(NF):
                ldo = kb.emit("sync", lambda o, kc=kc, t0=t0: o.dma_start(
                    out=oin[:, kc * 512:(kc + 1) * 512],
                    in_=otmpG[kc * 128:(kc + 1) * 128, t0:t0 + 512]),
                    waits=[("tensor", kb.val("tensor"))] if kc == 0 else [], inc=16)
            for ft in range(NF):
                ps = (ps4, ps5)[ft % 2]
                tv = 0
                for kc in range(NF):
                    tv = kb.emit("tensor", lambda o, ft=ft, kc=kc, ps=ps: o.matmul(
                        ps[:, 0:512],
                        wo_sb[:, kc * 1024 + ft * 128:kc * 1024 + (ft + 1) * 128],
                        oin[:, kc * 512:(kc + 1) * 512],
                        start=(kc == 0), stop=(kc == NF - 1)),
                        waits=([("sync", ldo), ("gq", gw), ("vector", vlast)]
                               if kc == 0 else []), inc=1)
                vv = kb.emit("vector", lambda o, ft=ft, ps=ps: o.tensor_scalar(
                    sev[:], ps[:, 0:512], bocol[:, ft:ft + 1], None, op0=ALU.add),
                    waits=[("tensor", tv)], inc=1)
                sd = kb.emit("sync", lambda o, ft=ft, t0=t0: o.dma_start(
                    out=s1T[ft * 128:(ft + 1) * 128, t0:t0 + 512], in_=sev[:]),
                    waits=[("vector", vv)], inc=16)
                vlast = vv
                kb.wait("vector", [("sync", sd)])
        kb.flush()



def ph_h(kb, nc, cst, xn1T_bf, x1T, s1T, wgT, wuT, wdT, out_nat):
    SC = 1.0 / 127.0
    with contextlib.ExitStack() as ctx:
        wbig = _mk(ctx, nc, "s", "ph_wbig", [128, 8 * 3072], BF16)
        wd_sb = _mk(ctx, nc, "s", "ph_wd", [128, 24 * 1024], BF16)
        xn_r = _mk(ctx, nc, "s", "ph_xn", [128, 8 * 512], BF16)
        h_sb = _mk(ctx, nc, "s", "ph_h", [128, 24 * 512], BF16)
        x1c = _mk(ctx, nc, "s", "ph_x1", [128, 512], F32)
        s1c = _mk(ctx, nc, "s", "ph_s1", [128, 512], F32)
        ev = _mk(ctx, nc, "s", "ph_ev", [128, 512], F32)
        evb = _mk(ctx, nc, "s", "ph_evb", [128, 512], F32)
        ps1 = _mk(ctx, nc, "p", "ph_ps1", [128, 512], F32)
        ps2 = _mk(ctx, nc, "p", "ph_ps2", [128, 512], F32)
        gwd = 0
        for kc in range(24):
            gwd = kb.emit("gq", lambda o, kc=kc: o.dma_start(
                out=wd_sb[:, kc * 1024:(kc + 1) * 1024],
                in_=wdT[kc * 128:(kc + 1) * 128, :]), inc=16)
        for ch in range(8):
            t0 = ch * 512
            ldx = 0
            for kc in range(NF):
                ldx = kb.emit("sync", lambda o, kc=kc, t0=t0: o.dma_start(
                    out=xn_r[:, kc * 512:(kc + 1) * 512],
                    in_=xn1T_bf[kc * 128:(kc + 1) * 128, t0:t0 + 512]),
                    waits=[("tensor", kb.val("tensor")), ("scalar", kb.val("scalar")),
                           ("vector", kb.val("vector"))] if kc == 0 else [], inc=16)
            for phase in (0, 1):
                gw = 0
                for kc in range(NF):
                    gw = kb.emit("gq", lambda o, kc=kc, phase=phase: o.dma_start(
                        out=wbig[:, kc * 3072:(kc + 1) * 3072],
                        in_=(wgT if phase == 0 else wuT)[kc * 128:(kc + 1) * 128, :]),
                        waits=[("tensor", kb.val("tensor"))] if kc == 0 else [], inc=16)
                last = {0: 0, 1: 0}
                for f2 in range(24):
                    ps = (ps1, ps2)[f2 % 2]
                    tv = 0
                    for kc in range(NF):
                        tv = kb.emit("tensor", lambda o, f2=f2, kc=kc, ps=ps: o.matmul(
                            ps[:, 0:512],
                            wbig[:, kc * 3072 + f2 * 128:kc * 3072 + (f2 + 1) * 128],
                            xn_r[:, kc * 512:(kc + 1) * 512],
                            start=(kc == 0), stop=(kc == NF - 1)),
                            waits=[("sync", ldx), ("gq", gw),
                                   ("scalar", last[f2 % 2]) if phase == 0 else
                                   ("vector", last[f2 % 2])] if kc == 0 else [], inc=1)
                    if phase == 0:
                        sv = kb.emit("scalar", lambda o, f2=f2, ps=ps: o.activation(
                            h_sb[:, f2 * 512:(f2 + 1) * 512], ps[:, 0:512], AF.Silu,
                            scale=SC), waits=[("tensor", tv)], inc=1)
                        last[f2 % 2] = sv
                    else:
                        vv = kb.emit("vector", lambda o, f2=f2, ps=ps: o.scalar_tensor_tensor(
                            h_sb[:, f2 * 512:(f2 + 1) * 512], ps[:, 0:512], SC,
                            h_sb[:, f2 * 512:(f2 + 1) * 512], op0=ALU.mult, op1=ALU.mult),
                            waits=[("tensor", tv)], inc=1)
                        last[f2 % 2] = vv
                if phase == 0:
                    kb.wait("vector", [("scalar", max(last[0], last[1]))])
                    last = {0: kb.val("vector"), 1: kb.val("vector")}
            ld2 = kb.emit("sync", lambda o, t0=t0: o.dma_start(
                out=x1c[:], in_=x1T[0:128, t0:t0 + 512]), inc=16) if False else 0
            vlast = {0: kb.val("vector"), 1: kb.val("vector")}
            for ft in range(NF):
                ps = (ps1, ps2)[ft % 2]
                ld2 = kb.emit("sync", lambda o, ft=ft, t0=t0: o.dma_start(
                    out=x1c[:], in_=x1T[ft * 128:(ft + 1) * 128, t0:t0 + 512]),
                    waits=[("vector", vlast[ft % 2])], inc=16)
                ld2 = kb.emit("sync", lambda o, ft=ft, t0=t0: o.dma_start(
                    out=s1c[:], in_=s1T[ft * 128:(ft + 1) * 128, t0:t0 + 512]), inc=16)
                tv = 0
                for kc in range(24):
                    tv = kb.emit("tensor", lambda o, ft=ft, kc=kc, ps=ps: o.matmul(
                        ps[:, 0:512],
                        wd_sb[:, kc * 1024 + ft * 128:kc * 1024 + (ft + 1) * 128],
                        h_sb[:, kc * 512:(kc + 1) * 512],
                        start=(kc == 0), stop=(kc == 23)),
                        waits=[("gq", gwd), ("vector", vlast[ft % 2])] if kc == 0 else [],
                        inc=1)
                sv = kb.emit("scalar", lambda o, ps=ps: o.activation(
                    ev[:], ps[:, 0:512], AF.Silu, scale=SC), waits=[("tensor", tv)], inc=1)
                vv = kb.emit("vector", lambda o: o.tensor_mul(ev[:], ev[:], s1c[:]),
                             waits=[("scalar", sv), ("sync", ld2)], inc=1)
                vv = kb.emit("vector", lambda o: o.tensor_add(evb[:], x1c[:], ev[:]), inc=1)
                for tt in range(4):
                    tvt = kb.emit("tensor", lambda o, tt=tt, ft=ft: o.transpose(
                        ps1[:, 0:128] if ft % 2 else ps2[:, 0:128],
                        evb[:, tt * 128:(tt + 1) * 128], cst.ident),
                        waits=[("vector", vv)], inc=1)
                    vv2 = kb.emit("vector", lambda o, tt=tt, ft=ft: o.tensor_copy(
                        ev[:, 0:128], (ps1 if ft % 2 else ps2)[:, 0:128]),
                        waits=[("tensor", tvt)], inc=1)
                    sd = kb.emit("sync", lambda o, tt=tt, ft=ft, t0=t0: o.dma_start(
                        out=out_nat[t0 + tt * 128:t0 + (tt + 1) * 128,
                                    ft * 128:(ft + 1) * 128],
                        in_=ev[:, 0:128]), waits=[("vector", vv2)], inc=16)
                    kb.wait("tensor", [("sync", sd)])
                    kb.wait("vector", [("sync", sd)])
                vlast[ft % 2] = kb.val("vector")
        kb.flush()


def _t(nc, n, s, dt, k):
    return nc.dram_tensor(n, s, dt, kind=k).ap()


def _host_prep(inputs):
    import ml_dtypes
    z = inputs
    hp = {}
    T = lambda a: np.ascontiguousarray(np.asarray(a).T)
    hp["b_wqT"] = T(z["L0_attn_Wq"]); hp["b_wkT"] = T(z["L0_attn_Wk"])
    hp["b_wvT"] = T(z["L0_attn_Wv"]); hp["b_woT"] = T(z["L0_attn_Wo"])
    for nm in ("bq", "bk", "bv", "bo", "gamma", "beta"):
        hp["b_" + nm] = np.asarray(z["L0_attn_" + nm]).reshape(1, -1)
    hp["b_relb"] = np.asarray(z["L0_attn_rel_bias"])
    hp["b_relbT"] = np.ascontiguousarray(hp["b_relb"].transpose(0, 2, 1))
    pos = np.arange(S, dtype=np.float64) * ((32 - 1) / (S - 1))
    i0 = np.floor(pos).astype(np.int64)
    i1 = np.minimum(i0 + 1, 31)
    fr = pos - i0
    M = np.zeros((S, 32))
    M[np.arange(S), i0] += 1.0 - fr
    np.add.at(M, (np.arange(S), i1), fr)
    M[-1, :] = 0.0
    hp["b_mt"] = np.ascontiguousarray(M.T.astype(np.float32))
    m1 = np.zeros((4, 128, 512), np.float32)
    m2 = np.zeros((4, 128, 512), np.float32)
    qi = np.arange(128)[:, None]; kj = np.arange(512)[None, :]
    for p in range(4):
        m1[p] = np.where(kj <= 128 * p + qi, 0.0, -1e30)
        m2[p] = np.where(kj >= 128 * p + qi, 0.0, -1e30)
    hp["b_mask1"] = m1; hp["b_mask2"] = m2
    hp["c_qw"] = np.asarray(z["L0_ffn_qw"]); hp["c_kw"] = np.asarray(z["L0_ffn_kw"])
    hp["c_vw"] = np.asarray(z["L0_ffn_vw"]); hp["c_ow"] = np.asarray(z["L0_ffn_ow"])
    for nm in ("qb", "kb", "vb", "ob"):
        hp["c_" + nm] = np.asarray(z["L0_ffn_" + nm]).reshape(1, -1)
    hp["c_n1w"] = np.asarray(z["L0_ffn_norm1_w"]).reshape(1, -1)
    hp["f_n2w"] = np.asarray(z["L0_ffn_norm2_w"]).reshape(1, -1)
    hp["d_gw"] = np.asarray(z["L0_ffn_gate_w"])
    hp["e_w1"] = np.asarray(z["L0_ffn_w1"]).reshape(8 * 1024, 2048)
    hp["e_w2"] = np.asarray(z["L0_ffn_w2"]).reshape(8 * 2048, 1024)
    hp["g_wqT"] = T(z["L1_attn_Wq"]); hp["g_wkT"] = T(z["L1_attn_Wk"])
    hp["g_wvT"] = T(z["L1_attn_Wv"]); hp["g_woT"] = T(z["L1_attn_Wo"])
    hp["g_waT"] = T(z["L1_attn_Wa"]); hp["g_wbT"] = T(z["L1_attn_Wb"])
    for nm in ("bq", "bk", "bv", "bo"):
        hp["g_" + nm] = np.asarray(z["L1_attn_" + nm]).reshape(1, -1)
    hp["g_ba"] = np.asarray(z["L1_attn_ba"]).reshape(1, -1)
    hp["g_bb"] = np.asarray(z["L1_attn_bb"]).reshape(1, -1)
    for cn, key in (("cq", "L1_attn_conv_q"), ("ck", "L1_attn_conv_k"),
                    ("cv", "L1_attn_conv_v")):
        cw = np.asarray(z[key])[:, 0, :]
        arr = np.zeros((128, NF * 4), np.float32)
        for f in range(NF):
            arr[:, f * 4:(f + 1) * 4] = cw[f * 128:(f + 1) * 128, :]
        hp["g_" + cn] = arr
    hot = np.zeros((16, 16 * 128), np.float32)
    for h in range(16):
        hot[h, h * 128:(h + 1) * 128] = 1.0
    hp["g_hot16"] = hot.astype(np.float32)
    t_ = np.arange(128)[:, None]; s_ = np.arange(128)[None, :]
    hp["g_mGA"] = np.where(s_ >= t_, 1e30, 0.0).astype(np.float32)
    hp["g_mGAp"] = np.where(s_ <= t_, -1e30, 0.0).astype(np.float32)
    hp["g_mGD"] = np.where(s_ > t_, 0.0, 0.0).astype(np.float32)
    hp["g_mGD"] = np.where(t_ > s_, 0.0, 0.0).astype(np.float32)
    hp["g_mGD"] = np.where(s_ < t_, -1e30, 0.0).astype(np.float32)
    hp["h_wgT"] = T(z["L1_ffn_w_gate"]); hp["h_wuT"] = T(z["L1_ffn_w_up"])
    hp["h_wdT"] = T(z["L1_ffn_w_down"])
    return hp



def kernel(**inputs):
    import ml_dtypes
    hp = _host_prep(inputs)
    ca = make_const_arrays()
    x_full = np.asarray(inputs["L0_attn_x"])
    NCOREL = list(range(N_CORES))

    def run(build_fn, per_core_ins, shared_ins):
        nc = build_fn()
        in_maps = [{**per_core_ins[c], **shared_ins, **ca} for c in range(N_CORES)]
        return run_bass_kernel_spmd(nc, in_maps, core_ids=NCOREL).results

    def mk_nc(scr=16384):
        return bass.Bass("TRN2", target_bir_lowering=False, debug=False,
                         dynamic_dma_scratch_size=scr)

    def bL1():
        nc = mk_nc()
        x = _t(nc, "x_nat", [NTOK, D], F32, "ExternalInput")
        g = _t(nc, "gamma", [1, D], F32, "ExternalInput")
        be = _t(nc, "beta", [1, D], F32, "ExternalInput")
        xT = _t(nc, "xT", [D, NTOK], F32, "ExternalOutput")
        xnT = _t(nc, "xnT", [D, NTOK], F32, "Internal")
        xnT_bf = _t(nc, "xnT_bf", [D, NTOK], BF16, "Internal")
        xn2T_bf = _t(nc, "xn2T_bf", [D, NTOK], BF16, "Internal")
        argsB = [_t(nc, n, [D, D], F32, "ExternalInput") for n in ("wq", "wk", "wv", "wo")]
        bsB = [_t(nc, n, [1, D], F32, "ExternalInput") for n in ("bq", "bk", "bv", "bo")]
        relb = _t(nc, "relb", [16, 32, 32], F32, "ExternalInput")
        relbT = _t(nc, "relbT", [16, 32, 32], F32, "ExternalInput")
        mt = _t(nc, "mt", [32, S], F32, "ExternalInput")
        mask1 = _t(nc, "mask1", [4, 128, 512], F32, "ExternalInput")
        mask2 = _t(nc, "mask2", [4, 128, 512], F32, "ExternalInput")
        s0T = _t(nc, "s0T", [D, NTOK], F32, "ExternalOutput")
        otmp = _t(nc, "otmpB", [D, S], BF16, "Internal")
        rowtmp = _t(nc, "rowtmpB", [16, 128], F32, "Internal")
        qw = _t(nc, "qw", [D, D], F32, "ExternalInput")
        kw = _t(nc, "kw", [D, 256], F32, "ExternalInput")
        vw = _t(nc, "vw", [D, 256], F32, "ExternalInput")
        ow = _t(nc, "ow", [D, D], F32, "ExternalInput")
        qb = _t(nc, "qb", [1, D], F32, "ExternalInput")
        kbi = _t(nc, "kb", [1, 256], F32, "ExternalInput")
        vb = _t(nc, "vb", [1, 256], F32, "ExternalInput")
        ob = _t(nc, "ob", [1, D], F32, "ExternalInput")
        n1w = _t(nc, "n1w", [1, D], F32, "ExternalInput")
        src1T = _t(nc, "src1T", [D, NTOK], F32, "ExternalOutput")
        src1T_bf = _t(nc, "src1T_bf", [D, NTOK], BF16, "Internal")
        src1nat = _t(nc, "src1nat", [NTOK + 128, D], BF16, "ExternalOutput")
        otC = _t(nc, "otmpC", [D, S], BF16, "Internal")
        gw = _t(nc, "gate_w", [D, 8], F32, "ExternalInput")
        cnt = _t(nc, "cnt", [1, 8], F32, "ExternalOutput")
        sel = _t(nc, "sel12", [NTOK, 2], F32, "ExternalOutput")
        wts = _t(nc, "wts12", [NTOK, 2], F32, "ExternalOutput")
        ci = _t(nc, "cincl", [8, NTOK], F32, "ExternalOutput")
        kb = KB(nc); cst = Consts(kb, nc)
        ph_a(kb, nc, cst, x, g, be, xT, xnT, xnT_bf, xn2T_bf)
        ph_b(kb, nc, cst, xn2T_bf, *argsB, *bsB, relb, relbT, mt, mask1, mask2,
             s0T, otmp, rowtmp)
        ph_c(kb, nc, cst, xnT_bf, xnT, qw, kw, vw, ow, qb, kbi, vb, ob, n1w,
             src1T, src1T_bf, src1nat, otC)
        ph_d_count(kb, nc, cst, src1T, gw, cnt, sel, wts, ci)
        kb.drain(); kb.flush()
        return nc
    pc = [{"x_nat": np.ascontiguousarray(x_full[2 * c:2 * c + 2].reshape(NTOK, D))}
          for c in range(N_CORES)]
    r1 = run(bL1, pc, {"gamma": hp["b_gamma"], "beta": hp["b_beta"],
                       "wq": hp["b_wqT"], "wk": hp["b_wkT"], "wv": hp["b_wvT"],
                       "wo": hp["b_woT"], "bq": hp["b_bq"], "bk": hp["b_bk"],
                       "bv": hp["b_bv"], "bo": hp["b_bo"], "relb": hp["b_relb"],
                       "relbT": hp["b_relbT"], "mt": hp["b_mt"],
                       "mask1": hp["b_mask1"], "mask2": hp["b_mask2"],
                       "qw": hp["c_qw"], "kw": hp["c_kw"], "vw": hp["c_vw"],
                       "ow": hp["c_ow"], "qb": hp["c_qb"], "kb": hp["c_kb"],
                       "vb": hp["c_vb"], "ob": hp["c_ob"], "n1w": hp["c_n1w"],
                       "gate_w": hp["d_gw"]})
    allcnt = np.stack([r1[c]["cnt"][0] for c in range(N_CORES)]).astype(np.float32)

    def bL2():
        nc = mk_nc(49152)
        ci = _t(nc, "cincl", [8, NTOK], F32, "ExternalInput")
        sel = _t(nc, "sel12", [NTOK, 2], F32, "ExternalInput")
        wts = _t(nc, "wts12", [NTOK, 2], F32, "ExternalInput")
        ac = _t(nc, "allcnt", [8, 8], F32, "ExternalInput")
        mc = _t(nc, "maskcol", [8, 1], F32, "ExternalInput")
        w1f = _t(nc, "w1f", [8 * 1024, 2048], F32, "ExternalInput")
        w2f = _t(nc, "w2f", [8 * 2048, 1024], F32, "ExternalInput")
        il = _t(nc, "ilist2", [NSLOT * 512], I32, "Internal")
        wl = _t(nc, "wlist2", [NSLOT * 512], I32, "Internal")
        i1 = _t(nc, "idxw1", [NSLOT, 128], I32, "Internal")
        i2 = _t(nc, "idxw2", [NSLOT, 128], I32, "Internal")
        w1b = _t(nc, "w1b", [8 * 1024, 2048], BF16, "Internal")
        w2b = _t(nc, "w2b", [8 * 2048, 1024], BF16, "Internal")
        ys = _t(nc, "ysum", [NTOK + 128, D], F32, "Internal")
        sn = _t(nc, "src1nat", [NTOK + 128, D], BF16, "ExternalInput")
        xT = _t(nc, "xT", [D, NTOK], F32, "ExternalInput")
        s0 = _t(nc, "s0T", [D, NTOK], F32, "ExternalInput")
        s1 = _t(nc, "src1T", [D, NTOK], F32, "ExternalInput")
        n2 = _t(nc, "n2w", [1, D], F32, "ExternalInput")
        x1 = _t(nc, "x1T", [D, NTOK], F32, "Internal")
        xn1 = _t(nc, "xn1T", [D, NTOK], F32, "Internal")
        xn1b = _t(nc, "xn1T_bf", [D, NTOK], BF16, "Internal")
        wq = _t(nc, "wq1", [D, D], F32, "ExternalInput")
        wk = _t(nc, "wk1", [D, D], F32, "ExternalInput")
        wv = _t(nc, "wv1", [D, D], F32, "ExternalInput")
        bq = _t(nc, "bq1", [1, D], F32, "ExternalInput")
        bk2 = _t(nc, "bk1", [1, D], F32, "ExternalInput")
        bv = _t(nc, "bv1", [1, D], F32, "ExternalInput")
        waT = _t(nc, "waT", [D, 16], F32, "ExternalInput")
        wbT = _t(nc, "wbT", [D, 16], F32, "ExternalInput")
        ba = _t(nc, "ba", [1, 16], F32, "ExternalInput")
        bb2 = _t(nc, "bb", [1, 16], F32, "ExternalInput")
        cq = _t(nc, "convq", [128, NF * 4], F32, "ExternalInput")
        ck = _t(nc, "convk", [128, NF * 4], F32, "ExternalInput")
        cv2 = _t(nc, "convv", [128, NF * 4], F32, "ExternalInput")
        qT = _t(nc, "qTb", [D, NTOK], BF16, "Internal")
        kT = _t(nc, "kTb", [D, NTOK], BF16, "Internal")
        vT = _t(nc, "vTb", [D, NTOK], BF16, "Internal")
        la = _t(nc, "lnaT", [16, NTOK], F32, "Internal")
        bg = _t(nc, "bgT", [16, NTOK], F32, "Internal")
        lb = _t(nc, "lnbT", [16, NTOK], F32, "Internal")
        hot = _t(nc, "hot16", [16, 16 * 128], F32, "ExternalInput")
        mA = _t(nc, "mGA", [128, 128], F32, "ExternalInput")
        mAp = _t(nc, "mGAp", [128, 128], F32, "ExternalInput")
        mD = _t(nc, "mGD", [128, 128], F32, "ExternalInput")
        wo = _t(nc, "woT1", [D, D], F32, "ExternalInput")
        bo = _t(nc, "bo1", [1, D], F32, "ExternalInput")
        s1T = _t(nc, "s1T", [D, NTOK], F32, "Internal")
        og = _t(nc, "otmpG", [D, NTOK], BF16, "Internal")
        wg = _t(nc, "wgT", [D, 3072], I32, "ExternalInput")
        wu = _t(nc, "wuT", [D, 3072], I32, "ExternalInput")
        wd = _t(nc, "wdT", [3072, D], I32, "ExternalInput")
        on = _t(nc, "out_nat", [NTOK, D], F32, "ExternalOutput")
        kb = KB(nc); cst = Consts(kb, nc)
        ph_d_route(kb, nc, cst, ci, sel, wts, ac, mc, il, wl, i1, i2)
        ph_e_prep(kb, nc, cst, w1f, w2f, w1b, w2b, ys)
        ph_e(kb, nc, cst, sn, w1b, w2b, il, wl, i1, i2, ys)
        ph_f(kb, nc, cst, xT, s0, s1, ys, n2, x1, xn1, xn1b)
        ph_f2(kb, nc, cst, xn1, xn1b, wq, wk, wv, bq, bk2, bv, waT, wbT, ba, bb2,
              cq, ck, cv2, qT, kT, vT, la, bg, lb)
        ph_g(kb, nc, cst, qT, kT, vT, la, bg, lb, hot, mA, mAp, mD, wo, bo, s1T, og)
        ph_h(kb, nc, cst, xn1b, x1, s1T, wg, wu, wd, on)
        kb.drain(); kb.flush()
        return nc
    pc = []
    for c in range(N_CORES):
        mk = np.zeros((8, 1), np.float32); mk[:c] = 1.0
        pc.append({"cincl": r1[c]["cincl"], "sel12": r1[c]["sel12"],
                   "wts12": r1[c]["wts12"], "maskcol": mk,
                   "src1nat": r1[c]["src1nat"], "xT": r1[c]["xT"],
                   "s0T": r1[c]["s0T"], "src1T": r1[c]["src1T"]})
    r2 = run(bL2, pc, {"allcnt": allcnt, "w1f": hp["e_w1"], "w2f": hp["e_w2"],
                       "n2w": hp["f_n2w"], "wq1": hp["g_wqT"], "wk1": hp["g_wkT"],
                       "wv1": hp["g_wvT"], "bq1": hp["g_bq"], "bk1": hp["g_bk"],
                       "bv1": hp["g_bv"], "waT": hp["g_waT"], "wbT": hp["g_wbT"],
                       "ba": hp["g_ba"], "bb": hp["g_bb"], "convq": hp["g_cq"],
                       "convk": hp["g_ck"], "convv": hp["g_cv"],
                       "hot16": hp["g_hot16"], "mGA": hp["g_mGA"],
                       "mGAp": hp["g_mGAp"], "mGD": hp["g_mGD"],
                       "woT1": hp["g_woT"], "bo1": hp["g_bo"],
                       "wgT": hp["h_wgT"], "wuT": hp["h_wuT"], "wdT": hp["h_wdT"]})

    out = np.concatenate([r2[c]["out_nat"].reshape(2, S, D) for c in range(N_CORES)],
                         axis=0)
    return np.ascontiguousarray(out.astype(np.float32))


# version 04
# speedup vs baseline: 1.1846x; 1.1846x over previous
import contextlib
import dataclasses
import numpy as np
import concourse.bass as bass
import concourse.mybir as mybir
from concourse.bass import IndirectOffsetOnAxis
from concourse.bass_utils import run_bass_kernel_spmd

F32 = mybir.dt.float32
BF16 = mybir.dt.bfloat16
I32 = mybir.dt.int32
U32 = mybir.dt.uint32
AF = mybir.ActivationFunctionType
ALU = mybir.AluOpType
AX = mybir.AxisListType

ENGINES = ("sync", "gpsimd", "tensor", "vector", "scalar")

N_CORES = 8
B_PER_CORE = 2
S = 2048
NTOK = B_PER_CORE * S
D = 1024
NF = D // 128
NT = NTOK // 128
H0 = 16
DH = 64
CAPACITY = 10240
SLOT = 512
NSLOT = 24


class KB:

    _QUEUES = ("sync", "gq")
    _STREAM = {"gq": "gpsimd"}

    def __init__(self, nc):
        self.nc = nc
        self.ops = []
        self.cnt = {e: 0 for e in ENGINES + ("gq",)}
        self.sems = {}
        self.nblk = 0
        self.epoch = {q: 0 for q in self._QUEUES}
        self.ckpt = {q: [] for q in self._QUEUES}
        self._gq_unthrottled = 0
        self.bregs = {}
        self._xwait = {e: {} for e in self._SELF_EDGE}
        self.xser = True

    def _sem(self, e):
        if e not in self.sems:
            self.sems[e] = self.nc.alloc_semaphore(name=f"prog_{e}")
        return self.sems[e]

    def _esem(self, q):
        key = "ep_" + q
        if key not in self.sems:
            self.sems[key] = self.nc.alloc_semaphore(name=key)
        return self.sems[key]

    def _queue_epoch_wait(self, q, v):
        for cv, ep in reversed(self.ckpt[q]):
            if cv >= v:
                return ("ep_" + q, ep)
        cv = self.cnt[q]
        assert cv >= v, (q, cv, v)
        self.epoch[q] += 1
        ep = self.epoch[q]
        esem = self._esem(q)
        qsem = self._sem(q)

        def clo(obj, cv=cv, ep=ep):
            if cv > 0:
                obj.wait_ge(qsem, cv)
            obj.sem_inc(esem, 1)
        self.ops.append((self._STREAM.get(q, q), clo))
        self.ckpt[q].append((cv, ep))
        return ("ep_" + q, ep)

    _SELF_EDGE = ("vector", "scalar", "gpsimd")

    def emit(self, eng, builder, waits=(), inc=0):
        if builder is not None and eng in self._SELF_EDGE:
            if inc == 0:
                inc = 1
            if self.cnt[eng] > 0:
                waits = ((eng, self.cnt[eng]),) + tuple(waits)
            if self.xser:
                extra = []
                for e2 in self._SELF_EDGE:
                    if e2 != eng and self.cnt[e2] > self._xwait[eng].get(e2, 0):
                        extra.append((e2, self.cnt[e2]))
                        self._xwait[eng][e2] = self.cnt[e2]
                waits = tuple(waits) + tuple(extra)
        fixed = []
        for we, wv in waits:
            if wv <= 0:
                continue
            if we in self._QUEUES and we != eng:
                fixed.append(self._queue_epoch_wait(we, wv))
            elif we == eng and we in self._QUEUES:
                fixed.append((we, self.cnt[we]))
            else:
                fixed.append((we, wv))
        waits = tuple(fixed)
        sem = self._sem(eng) if inc else None
        for we, _ in waits:
            self._sem(we)

        def clo(obj, waits=waits, inc=inc, sem=sem, builder=builder):
            for we, wv in waits:
                obj.wait_ge(self.sems[we], wv)
            if builder is None:
                return
            ins = builder(obj)
            if inc:
                ins.then_inc(sem, inc)
        self.ops.append((self._STREAM.get(eng, eng), clo))
        if inc:
            self.cnt[eng] += inc
        if eng == "gq" and inc:
            self._gq_unthrottled += 1
            if self._gq_unthrottled >= 16:
                self._gq_unthrottled = 0
                lag = self.cnt["gq"] - 16 * 16
                if lag > 0:
                    qsem = self._sem("gq")

                    def thr(obj, lag=lag, qsem=qsem):
                        obj.wait_ge(qsem, lag)
                    self.ops.append(("gpsimd", thr))
        return self.cnt[eng]

    def wait(self, eng, waits):
        self.emit(eng, None, waits=waits, inc=0)

    def breg(self, value, eng="gpsimd"):
        key = (eng, value)
        if key not in self.bregs:
            holder = [None]

            def mk(obj, holder=holder, value=value):
                holder[0] = obj.to_reg(value)
                return obj.nop()
            self.emit(eng, mk, inc=1)
            self.bregs[key] = holder
        return self.bregs[key]

    def val(self, eng):
        return self.cnt[eng]

    def barrier(self, engs=ENGINES):
        engs = tuple(engs)
        targets = {}
        for e in engs:
            v = self.cnt[e]
            sem = self._sem(e)
            extra = []
            if e == "gpsimd" and self.cnt["gq"] > 0:
                extra.append((self._sem("gq"), self.cnt["gq"]))

            def clo(obj, v=v, sem=sem, extra=tuple(extra)):
                if v > 0:
                    obj.wait_ge(sem, v)
                for s2, v2 in extra:
                    obj.wait_ge(s2, v2)
                obj.sem_inc(sem, 1)
            self.ops.append((self._STREAM.get(e, e), clo))
            self.cnt[e] += 1
            targets[e] = self.cnt[e]
        for e in engs:
            self.wait(e, [(e2, targets[e2]) for e2 in engs if e2 != e])

    def drain(self):
        self.wait("sync", [(e, self.cnt[e]) for e in ENGINES])

    def fence(self, writer="tensor", waiters=("vector", "scalar", "gpsimd")):
        v = self.cnt[writer]
        for e in waiters:
            self.wait(e, [(writer, v)])

    def flush(self, barrier=True):
        if barrier:
            self.barrier()
        if not self.ops:
            return
        ops, self.ops = self.ops, []
        self.nblk += 1
        with self.nc.Block(f"blk{self.nblk}") as block:
            for eng in ENGINES:
                mine = [c for (e, c) in ops if e == eng]
                if not mine:
                    continue

                def run(obj, mine=mine):
                    for c in mine:
                        c(obj)
                getattr(block, eng)(run)




def _mk(ctx, nc, kind, name, shape, dt):
    f = nc.sbuf_tensor if kind == "s" else nc.psum_tensor
    return ctx.enter_context(f(name, shape, dt))

def cdiv(a, b):
    return (a + b - 1) // b


def fview(dram_ap, f, c0, c1):
    return dram_ap[f * 128:(f + 1) * 128, c0:c1]



def make_const_arrays():
    import ml_dtypes
    c = {}
    c["c_ident"] = np.eye(128, dtype=np.float32)
    c["c_ones1"] = np.ones((1, 128), dtype=np.float32)
    c["c_ones128"] = np.ones((128, 128), dtype=np.float32)
    c["c_identb"] = np.eye(128).astype(ml_dtypes.bfloat16)
    c["c_iota8"] = np.tile(np.arange(8, dtype=np.float32), (128, 1))
    c["c_iota24"] = np.arange(NSLOT, dtype=np.float32).reshape(1, NSLOT)
    c["c_iota128"] = np.arange(128, dtype=np.float32).reshape(128, 1)
    c["c_iota128r"] = np.arange(128, dtype=np.float32).reshape(1, 128)
    return c


class Consts:

    def __init__(self, kb, nc):
        self.kb = kb
        self.nc = nc
        self.tiles = {}
        v = 0
        for name, shape, dt in (("c_ident", [128, 128], F32), ("c_ones1", [1, 128], F32),
                                ("c_ones128", [128, 128], F32), ("c_identb", [128, 128], BF16),
                                ("c_iota8", [128, 8], F32), ("c_iota24", [1, NSLOT], F32),
                                ("c_iota128", [128, 1], F32), ("c_iota128r", [1, 128], F32)):
            ap = nc.dram_tensor(name, shape, dt, kind="ExternalInput").ap()
            t = nc.alloc_sbuf_tensor("s" + name, shape, dt)
            v = kb.emit("sync", lambda o, t=t, ap=ap: o.dma_start(out=t.ap(), in_=ap), inc=16)
            self.tiles[name] = t
        for e in ("tensor", "vector", "scalar", "gpsimd"):
            kb.wait(e, [("sync", v)])

    @property
    def ident(self):
        return self.tiles["c_ident"].ap()

    @property
    def ones1(self):
        return self.tiles["c_ones1"].ap()

    @property
    def ones128(self):
        return self.tiles["c_ones128"].ap()

    @property
    def ident_bf(self):
        return self.tiles["c_identb"].ap()

    @property
    def iota8(self):
        return self.tiles["c_iota8"].ap()

    @property
    def iota24(self):
        return self.tiles["c_iota24"].ap()

    @property
    def iota128(self):
        return self.tiles["c_iota128"].ap()

    @property
    def iota128r(self):
        return self.tiles["c_iota128r"].ap()



def ph_a(kb, nc, cst, x_nat, gamma_row, beta_row, xT, xnT, xnT_bf, xn2T_bf):
    with (
        nc.sbuf_tensor("pa_xt0", [128, D], F32) as xt0,
        nc.sbuf_tensor("pa_xt1", [128, D], F32) as xt1,
        nc.sbuf_tensor("pa_xn", [128, D], F32) as xn_t,
        nc.sbuf_tensor("pa_xn2", [128, D], F32) as xn2_t,
        nc.sbuf_tensor("pa_sq", [128, D], F32) as sq_t,
        nc.sbuf_tensor("pa_stats", [128, 8], F32) as st,
        nc.sbuf_tensor("pa_gb", [128, 2 * D], F32) as gb,
        nc.sbuf_tensor("pa_grow", [1, 2 * D], F32) as grow,
        nc.sbuf_tensor("pa_tp32", [128, D], F32) as tp_f32,
        nc.sbuf_tensor("pa_tpb", [128, D], BF16) as tp_bf,
        nc.psum_tensor("pa_ps_t", [128, 1024], F32) as ps_t,
        nc.psum_tensor("pa_ps_b", [128, 1024], F32) as ps_b,
    ):
        ident, ones1 = cst.ident, cst.ones1
        ld = kb.emit("sync", lambda o: o.dma_start(out=grow[0:1, 0:D], in_=gamma_row), inc=16)
        ld = kb.emit("sync", lambda o: o.dma_start(out=grow[0:1, D:2 * D], in_=beta_row), inc=16)
        tv = 0
        for h in range(2):
            tv = kb.emit("tensor", lambda o, h=h: o.matmul(
                ps_b[:, h * 512:(h + 1) * 512], ones1, grow[0:1, h * 512:(h + 1) * 512],
                start=True, stop=True),
                waits=[("sync", ld)] if h == 0 else [], inc=1)
        vv = kb.emit("vector", lambda o: o.tensor_copy(gb[:, 0:D], ps_b[:, 0:D]),
                     waits=[("tensor", tv)], inc=1)
        for h in range(2):
            tv = kb.emit("tensor", lambda o, h=h: o.matmul(
                ps_b[:, h * 512:(h + 1) * 512], ones1, grow[0:1, D + h * 512:D + (h + 1) * 512],
                start=True, stop=True),
                waits=[("vector", vv)] if h == 0 else [], inc=1)
        vv = kb.emit("vector", lambda o: o.tensor_copy(gb[:, D:2 * D], ps_b[:, 0:D]),
                     waits=[("tensor", tv)], inc=1)

        bufs = [xt0, xt1]
        last_use = [(0, 0), (0, 0)]
        for i in range(NT):
            bi = i % 2
            xt = bufs[bi]
            vprev, tprev = last_use[bi]
            ld = kb.emit("sync", lambda o, i=i, xt=xt: o.dma_start(
                out=xt[:], in_=x_nat[i * 128:(i + 1) * 128, :]),
                waits=[("vector", vprev), ("tensor", tprev)], inc=16)
            vv = kb.emit("vector", lambda o, xt=xt: o.reduce_sum(st[:, 0:1], xt[:], axis=AX.X),
                         waits=[("sync", ld)], inc=1)
            vv = kb.emit("vector", lambda o: o.tensor_scalar(
                st[:, 1:2], st[:, 0:1], 1.0 / D, None, op0=ALU.mult), inc=1)
            vv = kb.emit("vector", lambda o, xt=xt: o.tensor_scalar(
                xn_t[:], xt[:], st[:, 1:2], None, op0=ALU.subtract), inc=1)
            sv = kb.emit("scalar", lambda o: o.activation(
                sq_t[:], xn_t[:], AF.Square, accum_out=st[:, 2:3]),
                waits=[("vector", vv)], inc=1)
            vv = kb.emit("vector", lambda o: o.tensor_scalar(
                st[:, 3:4], st[:, 2:3], 1.0 / D, 1e-5, op0=ALU.mult, op1=ALU.add),
                waits=[("scalar", sv)], inc=1)
            sv = kb.emit("scalar", lambda o: o.activation(
                st[:, 5:6], st[:, 3:4], AF.Sqrt),
                waits=[("vector", vv)], inc=1)
            vv = kb.emit("vector", lambda o: o.reciprocal(st[:, 4:5], st[:, 5:6]),
                         waits=[("scalar", sv)], inc=1)
            vv = kb.emit("vector", lambda o: o.tensor_scalar(
                xn_t[:], xn_t[:], st[:, 4:5], None, op0=ALU.mult), inc=1)
            vxn = vv
            vv = kb.emit("vector", lambda o: o.reduce_sum(st[:, 0:1], xn_t[:], axis=AX.X), inc=1)
            vv = kb.emit("vector", lambda o: o.tensor_scalar(
                st[:, 1:2], st[:, 0:1], 1.0 / D, None, op0=ALU.mult), inc=1)
            vv = kb.emit("vector", lambda o: o.tensor_scalar(
                xn2_t[:], xn_t[:], st[:, 1:2], None, op0=ALU.subtract), inc=1)
            sv = kb.emit("scalar", lambda o: o.activation(
                sq_t[:], xn2_t[:], AF.Square, accum_out=st[:, 2:3]),
                waits=[("vector", vv)], inc=1)
            vv = kb.emit("vector", lambda o: o.tensor_scalar(
                st[:, 3:4], st[:, 2:3], 1.0 / D, 1e-5, op0=ALU.mult, op1=ALU.add),
                waits=[("scalar", sv)], inc=1)
            sv = kb.emit("scalar", lambda o: o.activation(
                st[:, 5:6], st[:, 3:4], AF.Sqrt),
                waits=[("vector", vv)], inc=1)
            vv = kb.emit("vector", lambda o: o.reciprocal(st[:, 4:5], st[:, 5:6]),
                         waits=[("scalar", sv)], inc=1)
            vv = kb.emit("vector", lambda o: o.tensor_scalar(
                xn2_t[:], xn2_t[:], st[:, 4:5], None, op0=ALU.mult), inc=1)
            vv = kb.emit("vector", lambda o: o.scalar_tensor_tensor(
                xn2_t[:], xn2_t[:], 1.0, gb[:, 0:D], op0=ALU.mult, op1=ALU.mult), inc=1)
            vv = kb.emit("vector", lambda o: o.tensor_add(xn2_t[:], xn2_t[:], gb[:, D:2 * D]), inc=1)
            vxn2 = vv

            tv0 = 0
            for f in range(NF):
                tv0 = kb.emit("tensor", lambda o, f=f, xt=xt: o.transpose(
                    ps_t[:, f * 128:(f + 1) * 128], xt[:, f * 128:(f + 1) * 128], ident),
                    waits=[("sync", ld)] if f == 0 else [], inc=1)
            vv = kb.emit("vector", lambda o: o.tensor_copy(tp_f32[:], ps_t[:]),
                         waits=[("tensor", tv0)], inc=1)
            s_x = 0
            for f in range(NF):
                s_x = kb.emit("sync", lambda o, f=f, i=i: o.dma_start(
                    out=fview(xT, f, i * 128, (i + 1) * 128),
                    in_=tp_f32[:, f * 128:(f + 1) * 128]),
                    waits=[("vector", vv)] if f == 0 else [], inc=16)
            tv1 = 0
            for f in range(NF):
                tv1 = kb.emit("tensor", lambda o, f=f: o.transpose(
                    ps_t[:, f * 128:(f + 1) * 128], xn_t[:, f * 128:(f + 1) * 128], ident),
                    waits=[("vector", vxn2), ("sync", s_x)] if f == 0 else [], inc=1)
            vv2 = kb.emit("vector", lambda o: o.tensor_copy(tp_f32[:], ps_t[:]),
                          waits=[("tensor", tv1)], inc=1)
            s_xn = 0
            g_xn = 0
            for f in range(NF):
                s_xn = kb.emit("sync", lambda o, f=f, i=i: o.dma_start(
                    out=fview(xnT, f, i * 128, (i + 1) * 128),
                    in_=tp_f32[:, f * 128:(f + 1) * 128]),
                    waits=[("vector", vv2)] if f == 0 else [], inc=16)
                g_xn = kb.emit("gq", lambda o, f=f, i=i: o.dma_start(
                    out=fview(xnT_bf, f, i * 128, (i + 1) * 128),
                    in_=tp_f32[:, f * 128:(f + 1) * 128]),
                    waits=[("vector", vv2)] if f == 0 else [], inc=16)
            tv2 = 0
            for f in range(NF):
                tv2 = kb.emit("tensor", lambda o, f=f: o.transpose(
                    ps_t[:, f * 128:(f + 1) * 128], xn2_t[:, f * 128:(f + 1) * 128], ident),
                    waits=[("vector", vv2)] if f == 0 else [], inc=1)
            vv3 = kb.emit("vector", lambda o: o.tensor_copy(tp_bf[:], ps_t[:]),
                          waits=[("tensor", tv2), ("sync", s_xn), ("gq", g_xn)], inc=1)
            s_xn2 = 0
            for f in range(NF):
                s_xn2 = kb.emit("sync", lambda o, f=f, i=i: o.dma_start(
                    out=fview(xn2T_bf, f, i * 128, (i + 1) * 128),
                    in_=tp_bf[:, f * 128:(f + 1) * 128]),
                    waits=[("vector", vv3)] if f == 0 else [], inc=16)
            last_use[bi] = (vxn, tv0)
            kb.wait("vector", [("sync", s_xn2)])
        kb.flush()



def col_bias(kb, nc, cst, row_sb, off, out_tile, ocol0, ps128, scale=None):
    for f in range(NF):
        tv = kb.emit("tensor", lambda o, f=f: o.transpose(
            ps128[:, 0:1], row_sb[0:1, off + f * 128:off + (f + 1) * 128],
            cst.ident[0:1, 0:1]), inc=1)
        vv = kb.emit("vector", lambda o, f=f: o.tensor_copy(
            out_tile[:, ocol0 + f:ocol0 + f + 1], ps128[:, 0:1]),
            waits=[("tensor", tv)], inc=1)
        kb.wait("tensor", [("vector", vv)])
    if scale is not None:
        kb.emit("vector", lambda o: o.tensor_scalar(
            out_tile[:, ocol0:ocol0 + NF], out_tile[:, ocol0:ocol0 + NF],
            scale, None, op0=ALU.mult), inc=1)


def row_bcast(kb, nc, cst, row_sb, off, width, ps, pbase=0, prows=128):
    tv = 0
    for c0 in range(0, width, 512):
        w = min(512, width - c0)
        tv = kb.emit("tensor", lambda o, c0=c0, w=w: o.matmul(
            ps[pbase:pbase + prows, c0:c0 + w], cst.ones1[0:1, 0:prows],
            row_sb[0:1, off + c0:off + c0 + w], start=True, stop=True), inc=1)
    return tv



def ph_b(kb, nc, cst, xn2T_bf, wq, wk, wv, wo, bq, bk, bv, bo,
         relb, relbT, mt, mask1, mask2, s0T, otmp, rowtmp):
    ones1 = cst.ones1
    with (
        nc.sbuf_tensor("pb_q", [128, 8 * 2048], BF16) as q_sb,
        nc.sbuf_tensor("pb_k", [128, 8 * 2048], BF16) as k_sb,
        nc.sbuf_tensor("pb_v", [128, 16 * 1024], BF16) as v_sb,
    ):
        for b in range(B_PER_CORE):
            tb = b * S
            kb.wait("tensor", [("vector", kb.val("vector")), ("scalar", kb.val("scalar"))])
            with (
                nc.sbuf_tensor(f"pb_w3{b}", [128, 3 * 8 * 1024], BF16) as w3,
                nc.sbuf_tensor(f"pb_xn{b}", [128, 8 * 2048], BF16) as xn_r,
                nc.sbuf_tensor(f"pb_brow{b}", [1, 3 * D], F32) as brow,
                nc.sbuf_tensor(f"pb_bcol{b}", [128, 2 * NF], F32) as bcol,
                nc.sbuf_tensor(f"pb_bvbc{b}", [128, D], F32) as bvbc,
                nc.psum_tensor(f"pb_ps1{b}", [128, 512], F32) as ps1,
                nc.psum_tensor(f"pb_ps2{b}", [128, 512], F32) as ps2,
            ):
                gw = 0
                for wi, w in enumerate((wq, wk, wv)):
                    for kc in range(NF):
                        gw = kb.emit("gq", lambda o, wi=wi, w=w, kc=kc: o.dma_start(
                            out=w3[:, (wi * NF + kc) * 1024:(wi * NF + kc + 1) * 1024],
                            in_=w[kc * 128:(kc + 1) * 128, :]), inc=16)
                ld = kb.emit("sync", lambda o: o.dma_start(out=brow[0:1, 0:D], in_=bq), inc=16)
                ld = kb.emit("sync", lambda o: o.dma_start(out=brow[0:1, D:2 * D], in_=bk), inc=16)
                ld = kb.emit("sync", lambda o: o.dma_start(out=brow[0:1, 2 * D:3 * D], in_=bv), inc=16)
                kb.wait("tensor", [("sync", ld)])
                kb.wait("vector", [("sync", ld)])
                col_bias(kb, nc, cst, brow, 0, bcol, 0, ps1)
                col_bias(kb, nc, cst, brow, D, bcol, NF, ps1)
                for oc in range(2):
                    tv = kb.emit("tensor", lambda o, oc=oc: o.matmul(
                        ps2[:, 0:512], ones1, brow[0:1, 2 * D + oc * 512:2 * D + (oc + 1) * 512],
                        start=True, stop=True),
                        waits=[("vector", kb.val("vector"))], inc=1)
                    vv = kb.emit("vector", lambda o, oc=oc: o.tensor_copy(
                        bvbc[:, oc * 512:(oc + 1) * 512], ps2[:, 0:512]),
                        waits=[("tensor", tv)], inc=1)
                ldx = 0
                for kc in range(NF):
                    ldx = kb.emit("sync", lambda o, kc=kc: o.dma_start(
                        out=xn_r[:, kc * 2048:(kc + 1) * 2048],
                        in_=xn2T_bf[kc * 128:(kc + 1) * 128, tb:tb + S]), inc=16)
                kb.wait("tensor", [("sync", ldx), ("gq", gw)])
                pv0 = kb.val("vector")
                plast = {0: pv0, 1: pv0}
                for u, (ft, ch, which) in enumerate(
                        [(ft, ch, w_) for ft in range(NF) for ch in range(4) for w_ in (0, 1)]):
                    ps = (ps1, ps2)[u % 2]
                    dst = (q_sb, k_sb)[which]
                    bias_off = (0, NF)[which]
                    tv = 0
                    for kc in range(NF):
                        tv = kb.emit("tensor", lambda o, which=which, ft=ft, ch=ch, kc=kc, ps=ps: o.matmul(
                            ps[:, 0:512],
                            w3[:, (which * NF + kc) * 1024 + ft * 128:
                               (which * NF + kc) * 1024 + (ft + 1) * 128],
                            xn_r[:, kc * 2048 + ch * 512:kc * 2048 + (ch + 1) * 512],
                            start=(kc == 0), stop=(kc == NF - 1)),
                            waits=[("vector", plast[u % 2])] if kc == 0 else [], inc=1)
                    scl = 0.125 if which == 0 else 1.0
                    sv = kb.emit("vector", lambda o, which=which, ft=ft, ch=ch, dst=dst, bias_off=bias_off, ps=ps, scl=scl: o.tensor_scalar(
                        dst[:, ft * 2048 + ch * 512:ft * 2048 + (ch + 1) * 512],
                        ps[:, 0:512], bcol[:, bias_off + ft:bias_off + ft + 1], scl,
                        op0=ALU.add, op1=ALU.mult), waits=[("tensor", tv)], inc=1)
                    plast[u % 2] = sv
                vlast = {0: plast[0], 1: plast[1]}
                for u, (tt, oc) in enumerate([(tt, oc) for tt in range(16) for oc in (0, 1)]):
                    ps = (ps1, ps2)[u % 2]
                    tv = 0
                    for kc in range(NF):
                        tv = kb.emit("tensor", lambda o, tt=tt, oc=oc, kc=kc, ps=ps: o.matmul(
                            ps[:, 0:512],
                            xn_r[:, kc * 2048 + tt * 128:kc * 2048 + (tt + 1) * 128],
                            w3[:, (2 * NF + kc) * 1024 + oc * 512:
                               (2 * NF + kc) * 1024 + (oc + 1) * 512],
                            start=(kc == 0), stop=(kc == NF - 1)),
                            waits=([("vector", max(plast[u % 2], vlast[u % 2]))]
                                   if kc == 0 else []), inc=1)
                    vv = kb.emit("vector", lambda o, tt=tt, oc=oc, ps=ps: o.tensor_tensor(
                        v_sb[:, tt * 1024 + oc * 512:tt * 1024 + (oc + 1) * 512],
                        ps[:, 0:512], bvbc[:, oc * 512:(oc + 1) * 512], op=ALU.add),
                        waits=[("tensor", tv)], inc=1)
                    vlast[u % 2] = vv
                kb.flush()
            with contextlib.ExitStack() as ctx:
                g_sb = _mk(ctx, nc, "s", f"pb_g{b}", [32, 2 * 2048], BF16)
                mt_sb = _mk(ctx, nc, "s", f"pb_mt{b}", [32, 2048], BF16)
                rb_sb = _mk(ctx, nc, "s", f"pb_rb{b}", [32, 64], BF16)
                m1_sb = _mk(ctx, nc, "s", f"pb_m1{b}", [128, 4 * 512], F32)
                m2_sb = _mk(ctx, nc, "s", f"pb_m2{b}", [128, 4 * 512], F32)
                mx_sb = _mk(ctx, nc, "s", f"pb_mx{b}", [128, 4], F32)
                nma = _mk(ctx, nc, "s", f"pb_nma{b}", [128, 16], F32)
                nm16 = _mk(ctx, nc, "s", f"pb_nm16{b}", [16, 128], F32)
                nmrow = _mk(ctx, nc, "s", f"pb_nmrow{b}", [1, 2048], F32)
                qaug = _mk(ctx, nc, "s", f"pb_qaug{b}", [65, 2048], BF16)
                kaug = _mk(ctx, nc, "s", f"pb_kaug{b}", [65, 128], BF16)
                vaug = _mk(ctx, nc, "s", f"pb_vaug{b}", [128, 65], BF16)
                onecol = _mk(ctx, nc, "s", f"pb_onecol{b}", [128, 1], BF16)
                pt_sb = _mk(ctx, nc, "s", f"pb_pt{b}", [128, 2 * 512], BF16)
                ot_sb = _mk(ctx, nc, "s", f"pb_ot{b}", [64, 512], BF16)
                rcp_sb = _mk(ctx, nc, "s", f"pb_rcp{b}", [1, 512], F32)
                ps_sA = _mk(ctx, nc, "p", f"pb_ps_sA{b}", [128, 512], F32)
                ps_sB = _mk(ctx, nc, "p", f"pb_ps_sB{b}", [128, 512], F32)
                ps_o = _mk(ctx, nc, "p", f"pb_ps_o{b}", [65, 2048], F32)
                ps_g = _mk(ctx, nc, "p", f"pb_ps_g{b}", [64, 512], F32)
                ps_bc = _mk(ctx, nc, "p", f"pb_ps_bc{b}", [64, 512], F32)
                ld = kb.emit("gq", lambda o: o.dma_start(out=mt_sb[:], in_=mt), inc=16)
                ldm = 0
                for a in range(4):
                    ldm = kb.emit("sync", lambda o, a=a: o.dma_start(
                        out=m1_sb[:, a * 512:(a + 1) * 512], in_=mask1[a]), inc=16)
                    ldm = kb.emit("sync", lambda o, a=a: o.dma_start(
                        out=m2_sb[:, a * 512:(a + 1) * 512], in_=mask2[a]), inc=16)
                vv = kb.emit("vector", lambda o: o.memset(onecol[:], 1.0), inc=1)
                vv = kb.emit("vector", lambda o: o.memset(kaug[64:65, :], 1.0), inc=1)
                vv = kb.emit("vector", lambda o: o.memset(vaug[:, 64:65], 1.0), inc=1)
                kb.wait("tensor", [("sync", ldm), ("gq", ld), ("vector", vv)])
                kb.wait("scalar", [("vector", vv)])
                for h in range(H0):
                    hb = (h % 2) * 64
                    hf = h // 2
                    qsl = q_sb[hb:hb + 64, hf * 2048:(hf + 1) * 2048]
                    ksl = k_sb[hb:hb + 64, hf * 2048:(hf + 1) * 2048]
                    ldr = kb.emit("gq", lambda o, h=h: o.dma_start(
                        out=rb_sb[:, 0:32], in_=relbT[h]), inc=16)
                    ldr = kb.emit("gq", lambda o, h=h: o.dma_start(
                        out=rb_sb[:, 32:64], in_=relb[h]), inc=16)
                    gvv = 0
                    for gi in range(2):
                        for ch in range(4):
                            tv = kb.emit("tensor", lambda o, gi=gi, ch=ch: o.matmul(
                                ps_g[0:32, 0:512], rb_sb[:, gi * 32:(gi + 1) * 32],
                                mt_sb[:, ch * 512:(ch + 1) * 512], start=True, stop=True),
                                waits=[("gq", ldr), ("vector", gvv)], inc=1)
                            gvv = kb.emit("vector", lambda o, gi=gi, ch=ch: o.tensor_copy(
                                g_sb[:, gi * 2048 + ch * 512:gi * 2048 + (ch + 1) * 512],
                                ps_g[0:32, 0:512]), waits=[("tensor", tv)], inc=1)
                            kb.wait("tensor", [("vector", gvv)])
                    vred = {0: 0, 1: 0}
                    unit = 0
                    for qt in range(16):
                        nch = qt // 4 + 1
                        for c in range(nch):
                            par1 = unit % 2
                            unit += 1
                            ps = (ps_sA, ps_sB)[par1]
                            tv = kb.emit("tensor", lambda o, qt=qt, c=c, ps=ps, qsl=qsl, ksl=ksl: o.matmul(
                                ps[:, 0:512], qsl[:, qt * 128:(qt + 1) * 128],
                                ksl[:, c * 512:(c + 1) * 512], start=True, stop=False),
                                waits=[("vector", vred[par1])], inc=1)
                            tv = kb.emit("tensor", lambda o, qt=qt, c=c, ps=ps: o.matmul(
                                ps[:, 0:512], mt_sb[:, qt * 128:(qt + 1) * 128],
                                g_sb[:, c * 512:(c + 1) * 512],
                                start=False, stop=True), inc=1)
                            vv = 0
                            if c == nch - 1:
                                vv = kb.emit("vector", lambda o, qt=qt, ps=ps: o.tensor_add(
                                    ps[:, 0:512], ps[:, 0:512],
                                    m1_sb[:, (qt % 4) * 512:(qt % 4 + 1) * 512]),
                                    waits=[("tensor", tv)], inc=1)
                            col = 0 if c == 0 else 1
                            vv = kb.emit("vector", lambda o, ps=ps, col=col: o.reduce_max(
                                mx_sb[:, col:col + 1], ps[:, 0:512], axis=AX.X),
                                waits=[("tensor", tv)], inc=1)
                            if c > 0:
                                vv = kb.emit("vector", lambda o: o.tensor_max(
                                    mx_sb[:, 0:1], mx_sb[:, 0:1], mx_sb[:, 1:2]), inc=1)
                            vred[par1] = vv
                        vv = kb.emit("vector", lambda o, qt=qt: o.tensor_scalar(
                            nma[:, qt:qt + 1], mx_sb[:, 0:1], -1.0, None, op0=ALU.mult), inc=1)
                    tv = kb.emit("tensor", lambda o: o.transpose(
                        ps_g[0:16, 0:128], nma[:], cst.ident),
                        waits=[("vector", vv)], inc=1)
                    vv = kb.emit("vector", lambda o: o.tensor_copy(nm16[:], ps_g[0:16, 0:128]),
                                 waits=[("tensor", tv)], inc=1)
                    sd1 = kb.emit("sync", lambda o: o.dma_start(out=rowtmp, in_=nm16[:]),
                                  waits=[("vector", vv)], inc=16)
                    sd = kb.emit("sync", lambda o: o.dma_start(
                        out=nmrow[:], in_=rowtmp.rearrange("a b -> (a b)").unsqueeze(0)),
                        waits=[("sync", sd1)], inc=16)
                    vv = kb.emit("vector", lambda o: o.tensor_copy(qaug[64:65, :], nmrow[:]),
                                 waits=[("sync", sd)], inc=1)
                    sv_exp = {0: 0, 1: 0}
                    tv_pv = {0: 0, 1: 0}
                    vv_kaug = 0
                    tv_last_o = 0
                    for kt in range(16):
                        vv_kaug = kb.emit("vector", lambda o, kt=kt, ksl=ksl: o.tensor_copy(
                            kaug[0:64, :], ksl[:, kt * 128:(kt + 1) * 128]),
                            waits=[("tensor", tv_pv[0]), ("tensor", tv_pv[1])], inc=1)
                        vv_kaug = kb.emit("vector", lambda o, kt=kt, h=h: o.tensor_copy(
                            vaug[:, 0:64], v_sb[:, kt * 1024 + h * 64:kt * 1024 + h * 64 + 64]),
                            inc=1)
                        for qc in range(kt // 4, 4):
                            par = (kt + qc) % 2
                            ps = (ps_sA, ps_sB)[par]
                            tv = kb.emit("tensor", lambda o, kt=kt, qc=qc, ps=ps: o.matmul(
                                ps[:, 0:512], kaug[:, 0:128],
                                qaug[:, qc * 512:(qc + 1) * 512], start=True, stop=False),
                                waits=[("vector", vv_kaug), ("scalar", sv_exp[par])], inc=1)
                            tv = kb.emit("tensor", lambda o, kt=kt, qc=qc, ps=ps: o.matmul(
                                ps[:, 0:512], mt_sb[:, kt * 128:(kt + 1) * 128],
                                g_sb[:, 2048 + qc * 512:2048 + (qc + 1) * 512],
                                start=False, stop=True), inc=1)
                            vv = 0
                            if kt // 4 == qc:
                                vv = kb.emit("vector", lambda o, kt=kt, ps=ps: o.tensor_add(
                                    ps[:, 0:512], ps[:, 0:512],
                                    m2_sb[:, (kt % 4) * 512:(kt % 4 + 1) * 512]),
                                    waits=[("tensor", tv)], inc=1)
                            sv = kb.emit("scalar", lambda o, ps=ps, par=par: o.activation(
                                pt_sb[:, par * 512:(par + 1) * 512], ps[:, 0:512], AF.Exp),
                                waits=[("tensor", tv), ("vector", vv)], inc=1)
                            sv_exp[par] = sv
                            first = (kt == 0)
                            last = (kt == 4 * qc + 3)
                            tv = kb.emit("tensor", lambda o, qc=qc, par=par, first=first, last=last: o.matmul(
                                ps_o[0:65, qc * 512:(qc + 1) * 512], vaug[:],
                                pt_sb[:, par * 512:(par + 1) * 512],
                                start=first, stop=last),
                                waits=[("scalar", sv)], inc=1)
                            tv_pv[par] = tv
                            tv_last_o = tv
                    for qc in range(4):
                        vv = kb.emit("vector", lambda o, qc=qc: o.reciprocal(
                            rcp_sb[:], ps_o[64:65, qc * 512:(qc + 1) * 512]),
                            waits=[("tensor", tv_last_o)], inc=1)
                        tvb = kb.emit("tensor", lambda o: o.matmul(
                            ps_bc[0:64, 0:512], ones1[0:1, 0:64], rcp_sb[:],
                            start=True, stop=True), waits=[("vector", vv)], inc=1)
                        sv = kb.emit("scalar", lambda o, qc=qc: o.activation(
                            ot_sb[:], ps_o[0:64, qc * 512:(qc + 1) * 512], AF.Copy),
                            waits=[("tensor", tvb)], inc=1)
                        vv = kb.emit("vector", lambda o: o.tensor_tensor(
                            ot_sb[:], ot_sb[:], ps_bc[0:64, 0:512], op=ALU.mult),
                            waits=[("scalar", sv), ("tensor", tvb)], inc=1)
                        sd = kb.emit("sync", lambda o, h=h, qc=qc: o.dma_start(
                            out=otmp[h * 64:(h + 1) * 64, qc * 512:(qc + 1) * 512],
                            in_=ot_sb[:]), waits=[("vector", vv)], inc=16)
                        kb.wait("tensor", [("sync", sd)])
                    kb.wait("tensor", [("vector", kb.val("vector")), ("scalar", kb.val("scalar"))])
                    kb.fence()
                kb.flush()
            with (
                nc.sbuf_tensor(f"pb_wo{b}", [128, 8 * 1024], BF16) as wo_sb,
                nc.sbuf_tensor(f"pb_oin{b}", [128, 8 * 512], BF16) as oin,
                nc.sbuf_tensor(f"pb_borow{b}", [1, D], F32) as borow,
                nc.sbuf_tensor(f"pb_bocol{b}", [128, NF], F32) as bocol,
                nc.sbuf_tensor(f"pb_sev{b}", [128, 512], F32) as sev,
                nc.psum_tensor(f"pb_ps4{b}", [128, 512], F32) as ps4,
                nc.psum_tensor(f"pb_ps5{b}", [128, 512], F32) as ps5,
            ):
                gw = 0
                for kc in range(NF):
                    gw = kb.emit("gq", lambda o, kc=kc: o.dma_start(
                        out=wo_sb[:, kc * 1024:(kc + 1) * 1024],
                        in_=wo[kc * 128:(kc + 1) * 128, :]), inc=16)
                ld = kb.emit("sync", lambda o: o.dma_start(out=borow[0:1, :], in_=bo), inc=16)
                kb.wait("tensor", [("sync", ld)])
                kb.wait("vector", [("sync", ld)])
                col_bias(kb, nc, cst, borow, 0, bocol, 0, ps4)
                slast = {0: 0, 1: 0}
                for qc in range(4):
                    ldo = 0
                    for kc in range(NF):
                        ldo = kb.emit("sync", lambda o, kc=kc, qc=qc: o.dma_start(
                            out=oin[:, kc * 512:(kc + 1) * 512],
                            in_=otmp[kc * 128:(kc + 1) * 128, qc * 512:(qc + 1) * 512]),
                            waits=[("tensor", kb.val("tensor"))] if kc == 0 else [], inc=16)
                    for ft in range(NF):
                        ps = (ps4, ps5)[ft % 2]
                        tv = 0
                        for kc in range(NF):
                            tv = kb.emit("tensor", lambda o, ft=ft, kc=kc, ps=ps: o.matmul(
                                ps[:, 0:512],
                                wo_sb[:, kc * 1024 + ft * 128:kc * 1024 + (ft + 1) * 128],
                                oin[:, kc * 512:(kc + 1) * 512],
                                start=(kc == 0), stop=(kc == NF - 1)),
                                waits=([("sync", ldo), ("gq", gw), ("vector", slast[ft % 2])]
                                       if kc == 0 else []), inc=1)
                        sv = kb.emit("vector", lambda o, ft=ft, ps=ps: o.tensor_scalar(
                            sev[:], ps[:, 0:512], bocol[:, ft:ft + 1], None, op0=ALU.add),
                            waits=[("tensor", tv)], inc=1)
                        sd = kb.emit("sync", lambda o, ft=ft, qc=qc, b=b: o.dma_start(
                            out=s0T[ft * 128:(ft + 1) * 128, tb + qc * 512:tb + (qc + 1) * 512],
                            in_=sev[:]), waits=[("vector", sv)], inc=16)
                        slast[ft % 2] = sv
                        kb.wait("vector", [("sync", sd)])
                kb.flush()


def ph_c(kb, nc, cst, xnT_bf, xnT_f32, qw, kw, vw, ow, qb, kbias, vb, ob, n1w,
         src1T, src1T_bf, src1nat, otmp):
    ones1 = cst.ones1
    KV = 4
    with nc.sbuf_tensor("pc_zero", [128, D], BF16) as zt:
        vv = kb.emit("vector", lambda o: o.memset(zt[:], 0.0), inc=1)
        kb.emit("sync", lambda o: o.dma_start(out=src1nat[NTOK:NTOK + 128, :], in_=zt[:]),
                waits=[("vector", vv)], inc=16)
        kb.flush()
    with (
        nc.sbuf_tensor("pc_q", [128, 8 * 2048], BF16) as q_sb,
        nc.sbuf_tensor("pc_k", [128, 2 * 2048], BF16) as k_sb,
        nc.sbuf_tensor("pc_v", [128, 16 * 256], BF16) as v_sb,
    ):
        for b in range(B_PER_CORE):
            tb = b * S
            kb.wait("tensor", [("vector", kb.val("vector")), ("scalar", kb.val("scalar"))])
            with contextlib.ExitStack() as ctx:
                w3 = _mk(ctx, nc, "s", f"pc_w3{b}", [128, 8 * 1024 + 2 * 8 * 256], BF16)
                xn_r = _mk(ctx, nc, "s", f"pc_xn{b}", [128, 8 * 2048], BF16)
                brow = _mk(ctx, nc, "s", f"pc_brow{b}", [1, D + 2 * 256], F32)
                bcol = _mk(ctx, nc, "s", f"pc_bcol{b}", [128, NF], F32)
                bvbc = _mk(ctx, nc, "s", f"pc_bvbc{b}", [128, 256], F32)
                ps1 = _mk(ctx, nc, "p", f"pc_ps1{b}", [128, 512], F32)
                ps2 = _mk(ctx, nc, "p", f"pc_ps2{b}", [128, 512], F32)
                gw = 0
                for kc in range(NF):
                    gw = kb.emit("gq", lambda o, kc=kc: o.dma_start(
                        out=w3[:, kc * 1024:(kc + 1) * 1024],
                        in_=qw[kc * 128:(kc + 1) * 128, :]), inc=16)
                for wi, w in enumerate((kw, vw)):
                    for kc in range(NF):
                        gw = kb.emit("gq", lambda o, wi=wi, w=w, kc=kc: o.dma_start(
                            out=w3[:, 8192 + (wi * NF + kc) * 256:8192 + (wi * NF + kc + 1) * 256],
                            in_=w[kc * 128:(kc + 1) * 128, :]), inc=16)
                ld = kb.emit("sync", lambda o: o.dma_start(out=brow[0:1, 0:D], in_=qb), inc=16)
                ld = kb.emit("sync", lambda o: o.dma_start(out=brow[0:1, D:D + 256], in_=kbias), inc=16)
                ld = kb.emit("sync", lambda o: o.dma_start(out=brow[0:1, D + 256:D + 512], in_=vb), inc=16)
                kb.wait("tensor", [("sync", ld)])
                kb.wait("vector", [("sync", ld)])
                col_bias(kb, nc, cst, brow, 0, bcol, 0, ps1)
                bkcol = _mk(ctx, nc, "s", f"pc_bkcol{b}", [128, 2], F32)
                for f in range(2):
                    tvb = kb.emit("tensor", lambda o, f=f: o.transpose(
                        ps1[:, 0:1], brow[0:1, D + f * 128:D + (f + 1) * 128],
                        cst.ident[0:1, 0:1]), inc=1)
                    vvb = kb.emit("vector", lambda o, f=f: o.tensor_copy(
                        bkcol[:, f:f + 1], ps1[:, 0:1]), waits=[("tensor", tvb)], inc=1)
                    kb.wait("tensor", [("vector", vvb)])
                tvv = kb.emit("tensor", lambda o: o.matmul(
                    ps2[:, 0:256], ones1, brow[0:1, D + 256:D + 512],
                    start=True, stop=True), waits=[("vector", kb.val("vector"))], inc=1)
                vv = kb.emit("vector", lambda o: o.tensor_copy(bvbc[:, 0:256], ps2[:, 0:256]),
                             waits=[("tensor", tvv)], inc=1)
                ldx = 0
                for kc in range(NF):
                    ldx = kb.emit("sync", lambda o, kc=kc: o.dma_start(
                        out=xn_r[:, kc * 2048:(kc + 1) * 2048],
                        in_=xnT_bf[kc * 128:(kc + 1) * 128, tb:tb + S]), inc=16)
                kb.wait("tensor", [("sync", ldx), ("gq", gw)])
                pv0 = kb.val("vector")
                plast = {0: pv0, 1: pv0}
                for u, (ft, ch) in enumerate([(ft, ch) for ft in range(NF) for ch in range(4)]):
                    ps = (ps1, ps2)[u % 2]
                    tv = 0
                    for kc in range(NF):
                        tv = kb.emit("tensor", lambda o, ft=ft, ch=ch, kc=kc, ps=ps: o.matmul(
                            ps[:, 0:512],
                            w3[:, kc * 1024 + ft * 128:kc * 1024 + (ft + 1) * 128],
                            xn_r[:, kc * 2048 + ch * 512:kc * 2048 + (ch + 1) * 512],
                            start=(kc == 0), stop=(kc == NF - 1)),
                            waits=[("vector", plast[u % 2])] if kc == 0 else [], inc=1)
                    sv = kb.emit("vector", lambda o, ft=ft, ch=ch, ps=ps: o.tensor_scalar(
                        q_sb[:, ft * 2048 + ch * 512:ft * 2048 + (ch + 1) * 512],
                        ps[:, 0:512], bcol[:, ft:ft + 1], 0.125,
                        op0=ALU.add, op1=ALU.mult), waits=[("tensor", tv)], inc=1)
                    plast[u % 2] = sv
                for u, (ft, ch) in enumerate([(ft, ch) for ft in range(2) for ch in range(4)]):
                    ps = (ps1, ps2)[u % 2]
                    tv = 0
                    for kc in range(NF):
                        tv = kb.emit("tensor", lambda o, ft=ft, ch=ch, kc=kc, ps=ps: o.matmul(
                            ps[:, 0:512],
                            w3[:, 8192 + kc * 256 + ft * 128:8192 + kc * 256 + (ft + 1) * 128],
                            xn_r[:, kc * 2048 + ch * 512:kc * 2048 + (ch + 1) * 512],
                            start=(kc == 0), stop=(kc == NF - 1)),
                            waits=[("vector", plast[u % 2])] if kc == 0 else [], inc=1)
                    sv = kb.emit("vector", lambda o, ft=ft, ch=ch, ps=ps: o.tensor_scalar(
                        k_sb[:, ft * 2048 + ch * 512:ft * 2048 + (ch + 1) * 512],
                        ps[:, 0:512], bkcol[:, ft:ft + 1], None,
                        op0=ALU.add), waits=[("tensor", tv)], inc=1)
                    plast[u % 2] = sv
                for u, tt in enumerate(range(16)):
                    ps = (ps1, ps2)[u % 2]
                    tv = 0
                    for kc in range(NF):
                        tv = kb.emit("tensor", lambda o, tt=tt, kc=kc, ps=ps: o.matmul(
                            ps[:, 0:256],
                            xn_r[:, kc * 2048 + tt * 128:kc * 2048 + (tt + 1) * 128],
                            w3[:, 8192 + NF * 256 + kc * 256:8192 + NF * 256 + (kc + 1) * 256],
                            start=(kc == 0), stop=(kc == NF - 1)),
                            waits=[("vector", plast[u % 2])] if kc == 0 else [], inc=1)
                    vv = kb.emit("vector", lambda o, tt=tt, ps=ps: o.tensor_tensor(
                        v_sb[:, tt * 256:(tt + 1) * 256],
                        ps[:, 0:256], bvbc[:, 0:256], op=ALU.add),
                        waits=[("tensor", tv)], inc=1)
                    plast[u % 2] = vv
                kb.flush()
            with contextlib.ExitStack() as ctx:
                mx_sb = _mk(ctx, nc, "s", f"pc_mx{b}", [128, 4], F32)
                nma = _mk(ctx, nc, "s", f"pc_nma{b}", [128, 16], F32)
                nm16 = _mk(ctx, nc, "s", f"pc_nm16{b}", [16, 128], F32)
                nmrow = _mk(ctx, nc, "s", f"pc_nmrow{b}", [1, 2048], F32)
                qaug = _mk(ctx, nc, "s", f"pc_qaug{b}", [65, 2048], BF16)
                kaug = _mk(ctx, nc, "s", f"pc_kaug{b}", [65, 128], BF16)
                vaug = _mk(ctx, nc, "s", f"pc_vaug{b}", [128, 65], BF16)
                kstage = _mk(ctx, nc, "s", f"pc_kst{b}", [64, 2048], BF16)
                pt_sb = _mk(ctx, nc, "s", f"pc_pt{b}", [128, 2 * 512], BF16)
                ot_sb = _mk(ctx, nc, "s", f"pc_ot{b}", [64, 512], BF16)
                rcp_sb = _mk(ctx, nc, "s", f"pc_rcp{b}", [1, 512], F32)
                ps_sA = _mk(ctx, nc, "p", f"pc_ps_sA{b}", [128, 512], F32)
                ps_sB = _mk(ctx, nc, "p", f"pc_ps_sB{b}", [128, 512], F32)
                ps_o = _mk(ctx, nc, "p", f"pc_ps_o{b}", [65, 2048], F32)
                ps_bc = _mk(ctx, nc, "p", f"pc_ps_bc{b}", [64, 512], F32)
                rowtmp = nc.dram_tensor(f"pc_rowtmp{b}", [16, 128], F32, kind="Internal").ap()
                vv = kb.emit("vector", lambda o: o.memset(kaug[64:65, :], 1.0), inc=1)
                vv = kb.emit("vector", lambda o: o.memset(vaug[:, 64:65], 1.0), inc=1)
                kb.wait("tensor", [("vector", vv)])
                kb.wait("scalar", [("vector", vv)])
                for h in range(H0):
                    hb = (h % 2) * 64
                    hf = h // 2
                    g = h // 4
                    gb2 = (g % 2) * 64
                    gf = g // 2
                    qsl = q_sb[hb:hb + 64, hf * 2048:(hf + 1) * 2048]
                    ksl = k_sb[gb2:gb2 + 64, gf * 2048:(gf + 1) * 2048]
                    vst = kb.emit("vector", lambda o, qsl=qsl: o.tensor_copy(qaug[0:64, :], qsl),
                                  waits=[("tensor", kb.val("tensor"))], inc=1)
                    if h % 4 == 0:
                        vst = kb.emit("vector", lambda o, ksl=ksl: o.tensor_copy(kstage[:], ksl),
                                      inc=1)
                    vred = {0: vst, 1: vst}
                    unit = 0
                    for qt in range(16):
                        for c in range(4):
                            par1 = unit % 2
                            unit += 1
                            ps = (ps_sA, ps_sB)[par1]
                            tv = kb.emit("tensor", lambda o, qt=qt, c=c, ps=ps: o.matmul(
                                ps[:, 0:512], qaug[0:64, qt * 128:(qt + 1) * 128],
                                kstage[:, c * 512:(c + 1) * 512], start=True, stop=True),
                                waits=[("vector", vred[par1])], inc=1)
                            col = 0 if c == 0 else 1
                            vv = kb.emit("vector", lambda o, ps=ps, col=col: o.reduce_max(
                                mx_sb[:, col:col + 1], ps[:, 0:512], axis=AX.X),
                                waits=[("tensor", tv)], inc=1)
                            if c > 0:
                                vv = kb.emit("vector", lambda o: o.tensor_max(
                                    mx_sb[:, 0:1], mx_sb[:, 0:1], mx_sb[:, 1:2]), inc=1)
                            vred[par1] = vv
                        vv = kb.emit("vector", lambda o, qt=qt: o.tensor_scalar(
                            nma[:, qt:qt + 1], mx_sb[:, 0:1], -1.0, None, op0=ALU.mult), inc=1)
                    tv = kb.emit("tensor", lambda o: o.transpose(
                        ps_bc[0:16, 0:128] if False else ps_sA[0:16, 0:128], nma[:], cst.ident),
                        waits=[("vector", vv)], inc=1)
                    vv = kb.emit("vector", lambda o: o.tensor_copy(nm16[:], ps_sA[0:16, 0:128]),
                                 waits=[("tensor", tv)], inc=1)
                    sd1 = kb.emit("sync", lambda o: o.dma_start(out=rowtmp, in_=nm16[:]),
                                  waits=[("vector", vv)], inc=16)
                    sd = kb.emit("sync", lambda o: o.dma_start(
                        out=nmrow[:], in_=rowtmp.rearrange("a b -> (a b)").unsqueeze(0)),
                        waits=[("sync", sd1)], inc=16)
                    vv = kb.emit("vector", lambda o: o.tensor_copy(qaug[64:65, :], nmrow[:]),
                                 waits=[("sync", sd)], inc=1)
                    sv_exp = {0: 0, 1: 0}
                    tv_pv = {0: 0, 1: 0}
                    tv_last_o = 0
                    for kt in range(16):
                        vv_kaug = kb.emit("vector", lambda o, kt=kt: o.tensor_copy(
                            kaug[0:64, :], kstage[:, kt * 128:(kt + 1) * 128]),
                            waits=[("tensor", tv_pv[0]), ("tensor", tv_pv[1])], inc=1)
                        vv_kaug = kb.emit("vector", lambda o, kt=kt, g=g: o.tensor_copy(
                            vaug[:, 0:64], v_sb[:, kt * 256 + g * 64:kt * 256 + g * 64 + 64]),
                            inc=1)
                        for qc in range(4):
                            par = (kt + qc) % 2
                            ps = (ps_sA, ps_sB)[par]
                            tv = kb.emit("tensor", lambda o, qc=qc, ps=ps: o.matmul(
                                ps[:, 0:512], kaug[:, 0:128],
                                qaug[:, qc * 512:(qc + 1) * 512], start=True, stop=True),
                                waits=[("vector", vv_kaug), ("scalar", sv_exp[par])], inc=1)
                            sv = kb.emit("scalar", lambda o, ps=ps, par=par: o.activation(
                                pt_sb[:, par * 512:(par + 1) * 512], ps[:, 0:512], AF.Exp),
                                waits=[("tensor", tv)], inc=1)
                            sv_exp[par] = sv
                            first = (kt == 0)
                            last = (kt == 15)
                            tv = kb.emit("tensor", lambda o, qc=qc, par=par, first=first, last=last: o.matmul(
                                ps_o[0:65, qc * 512:(qc + 1) * 512], vaug[:],
                                pt_sb[:, par * 512:(par + 1) * 512],
                                start=first, stop=last),
                                waits=[("scalar", sv)], inc=1)
                            tv_pv[par] = tv
                            tv_last_o = tv
                    for qc in range(4):
                        vv = kb.emit("vector", lambda o, qc=qc: o.reciprocal(
                            rcp_sb[:], ps_o[64:65, qc * 512:(qc + 1) * 512]),
                            waits=[("tensor", tv_last_o)], inc=1)
                        tvb = kb.emit("tensor", lambda o: o.matmul(
                            ps_bc[0:64, 0:512], ones1[0:1, 0:64], rcp_sb[:],
                            start=True, stop=True), waits=[("vector", vv)], inc=1)
                        sv = kb.emit("scalar", lambda o, qc=qc: o.activation(
                            ot_sb[:], ps_o[0:64, qc * 512:(qc + 1) * 512], AF.Copy),
                            waits=[("tensor", tvb)], inc=1)
                        vv = kb.emit("vector", lambda o: o.tensor_tensor(
                            ot_sb[:], ot_sb[:], ps_bc[0:64, 0:512], op=ALU.mult),
                            waits=[("scalar", sv)], inc=1)
                        sd = kb.emit("sync", lambda o, h=h, qc=qc: o.dma_start(
                            out=otmp[h * 64:(h + 1) * 64, qc * 512:(qc + 1) * 512],
                            in_=ot_sb[:]), waits=[("vector", vv)], inc=16)
                        kb.wait("tensor", [("sync", sd)])
                    kb.wait("tensor", [("vector", kb.val("vector")), ("scalar", kb.val("scalar"))])
                    kb.fence()
                kb.flush()
            with contextlib.ExitStack() as ctx:
                wo_sb = _mk(ctx, nc, "s", f"pc_wo{b}", [128, 8 * 1024], BF16)
                oin = _mk(ctx, nc, "s", f"pc_oin{b}", [128, 8 * 512], BF16)
                xn32 = _mk(ctx, nc, "s", f"pc_xn32{b}", [128, 8 * 512], F32)
                src_sb = _mk(ctx, nc, "s", f"pc_src{b}", [128, 8 * 512], F32)
                sq_sb = _mk(ctx, nc, "s", f"pc_sq{b}", [128, 512], F32)
                rstd_sb = _mk(ctx, nc, "s", f"pc_rstd{b}", [128, 512], F32)
                borow = _mk(ctx, nc, "s", f"pc_borow{b}", [1, 2 * D], F32)
                bocol = _mk(ctx, nc, "s", f"pc_bocol{b}", [128, 2 * NF], F32)
                sev = _mk(ctx, nc, "s", f"pc_sev{b}", [128, 512], F32)
                sevb = _mk(ctx, nc, "s", f"pc_sevb{b}", [128, 512], BF16)
                ps4 = _mk(ctx, nc, "p", f"pc_ps4{b}", [128, 512], F32)
                ps5 = _mk(ctx, nc, "p", f"pc_ps5{b}", [128, 512], F32)
                ps6 = _mk(ctx, nc, "p", f"pc_ps6{b}", [128, 512], F32)
                gw = 0
                for kc in range(NF):
                    gw = kb.emit("gq", lambda o, kc=kc: o.dma_start(
                        out=wo_sb[:, kc * 1024:(kc + 1) * 1024],
                        in_=ow[kc * 128:(kc + 1) * 128, :]), inc=16)
                ld = kb.emit("sync", lambda o: o.dma_start(out=borow[0:1, 0:D], in_=ob), inc=16)
                ld = kb.emit("sync", lambda o: o.dma_start(out=borow[0:1, D:2 * D], in_=n1w), inc=16)
                kb.wait("tensor", [("sync", ld)])
                kb.wait("vector", [("sync", ld)])
                col_bias(kb, nc, cst, borow, 0, bocol, 0, ps4)
                col_bias(kb, nc, cst, borow, D, bocol, NF, ps4)
                vlast = kb.val("vector")
                for qc in range(4):
                    ldo = 0
                    ldx = 0
                    for kc in range(NF):
                        ldo = kb.emit("sync", lambda o, kc=kc, qc=qc: o.dma_start(
                            out=oin[:, kc * 512:(kc + 1) * 512],
                            in_=otmp[kc * 128:(kc + 1) * 128, qc * 512:(qc + 1) * 512]),
                            waits=[("tensor", kb.val("tensor")), ("vector", vlast)]
                            if kc == 0 else [], inc=16)
                        ldx = kb.emit("sync", lambda o, kc=kc, qc=qc: o.dma_start(
                            out=xn32[:, kc * 512:(kc + 1) * 512],
                            in_=xnT_f32[kc * 128:(kc + 1) * 128, tb + qc * 512:tb + (qc + 1) * 512]),
                            inc=16)
                    for ft in range(NF):
                        ps = (ps4, ps5)[ft % 2]
                        tv = 0
                        for kc in range(NF):
                            tv = kb.emit("tensor", lambda o, ft=ft, kc=kc, ps=ps: o.matmul(
                                ps[:, 0:512],
                                wo_sb[:, kc * 1024 + ft * 128:kc * 1024 + (ft + 1) * 128],
                                oin[:, kc * 512:(kc + 1) * 512],
                                start=(kc == 0), stop=(kc == NF - 1)),
                                waits=([("sync", ldx), ("gq", gw), ("vector", vlast)]
                                       if kc == 0 else []), inc=1)
                        vv = kb.emit("vector", lambda o, ft=ft, ps=ps: o.tensor_scalar(
                            sev[:], ps[:, 0:512], bocol[:, ft:ft + 1], None, op0=ALU.add),
                            waits=[("tensor", tv)], inc=1)
                        vv = kb.emit("vector", lambda o, ft=ft: o.tensor_tensor(
                            src_sb[:, ft * 512:(ft + 1) * 512], sev[:],
                            xn32[:, ft * 512:(ft + 1) * 512], op=ALU.add), inc=1)
                        vlast = vv
                    tvs = 0
                    for ft in range(NF):
                        sv = kb.emit("scalar", lambda o, ft=ft: o.activation(
                            sq_sb[:], src_sb[:, ft * 512:(ft + 1) * 512], AF.Square),
                            waits=[("vector", vlast), ("tensor", tvs)], inc=1)
                        tvs = kb.emit("tensor", lambda o, ft=ft: o.matmul(
                            ps6[:, 0:512], cst.ones128, sq_sb[:],
                            start=(ft == 0), stop=(ft == NF - 1)),
                            waits=[("scalar", sv)], inc=1)
                    vv = kb.emit("vector", lambda o: o.tensor_scalar(
                        rstd_sb[:], ps6[:, 0:512], 1.0 / D, 1e-6, op0=ALU.mult, op1=ALU.add),
                        waits=[("tensor", tvs)], inc=1)
                    sv = kb.emit("scalar", lambda o: o.activation(
                        sq_sb[:], rstd_sb[:], AF.Ln), waits=[("vector", vv)], inc=1)
                    sv = kb.emit("scalar", lambda o: o.activation(
                        rstd_sb[:], sq_sb[:], AF.Exp, scale=-0.5), inc=1)
                    for ft in range(NF):
                        vv = kb.emit("vector", lambda o, ft=ft: o.scalar_tensor_tensor(
                            sev[:], src_sb[:, ft * 512:(ft + 1) * 512],
                            1.0, rstd_sb[:], op0=ALU.mult, op1=ALU.mult),
                            waits=[("scalar", sv)], inc=1)
                        vv = kb.emit("vector", lambda o, ft=ft: o.tensor_scalar(
                            sev[:], sev[:], bocol[:, NF + ft:NF + ft + 1], None, op0=ALU.mult),
                            inc=1)
                        vv = kb.emit("vector", lambda o, ft=ft: o.tensor_copy(sevb[:], sev[:]), inc=1)
                        sd = kb.emit("sync", lambda o, ft=ft, qc=qc: o.dma_start(
                            out=src1T[ft * 128:(ft + 1) * 128, tb + qc * 512:tb + (qc + 1) * 512],
                            in_=sev[:]), waits=[("vector", vv)], inc=16)
                        sd = kb.emit("sync", lambda o, ft=ft, qc=qc: o.dma_start(
                            out=src1T_bf[ft * 128:(ft + 1) * 128, tb + qc * 512:tb + (qc + 1) * 512],
                            in_=sevb[:]), inc=16)
                        for tt4 in range(4):
                            tvt = kb.emit("tensor", lambda o, tt4=tt4: o.transpose(
                                ps4[:, 0:128].bitcast(BF16)[:, 0:128], sevb[:, tt4 * 128:(tt4 + 1) * 128],
                                cst.ident_bf), waits=[("vector", vv)], inc=1)
                            vv2 = kb.emit("vector", lambda o, tt4=tt4: o.tensor_copy(
                                sevb[:, 0:128] if False else sq_sb.bitcast(BF16)[:, 0:128],
                                ps4[:, 0:128].bitcast(BF16)[:, 0:128]),
                                waits=[("tensor", tvt)], inc=1)
                            sd = kb.emit("sync", lambda o, tt4=tt4, qc=qc, ft=ft, b=b: o.dma_start(
                                out=src1nat[tb + qc * 512 + tt4 * 128:tb + qc * 512 + (tt4 + 1) * 128,
                                            ft * 128:(ft + 1) * 128],
                                in_=sq_sb.bitcast(BF16)[:, 0:128]),
                                waits=[("vector", vv2)], inc=16)
                            kb.wait("tensor", [("sync", sd)])
                            kb.wait("vector", [("sync", sd)])
                        vlast = kb.val("vector")
                kb.flush()


def ph_d_count(kb, nc, cst, src1T_f32, gate_w, cnt_o, sel12_o, wts12_o, cincl_o):
    with contextlib.ExitStack() as ctx:
        gw_sb = _mk(ctx, nc, "s", "pd_gw", [128, 8 * 8], F32)
        src_r = _mk(ctx, nc, "s", "pd_src", [128, 8 * 512], F32)
        probs = _mk(ctx, nc, "s", "pd_probs", [8, NTOK], F32)
        ptok = _mk(ctx, nc, "s", "pd_ptok", [128, 8], F32)
        m8 = _mk(ctx, nc, "s", "pd_m8", [128, 8], F32)
        i8 = _mk(ctx, nc, "s", "pd_i8", [128, 8], U32)
        i8f = _mk(ctx, nc, "s", "pd_i8f", [128, 2], F32)
        wcol = _mk(ctx, nc, "s", "pd_w", [128, 4], F32)
        etok = _mk(ctx, nc, "s", "pd_etok", [128, 8], F32)
        esum = _mk(ctx, nc, "s", "pd_e", [8, NTOK], F32)
        cnt8 = _mk(ctx, nc, "s", "pd_cnt8", [8, 1], F32)
        cntrow = _mk(ctx, nc, "s", "pd_cntrow", [1, 8], F32)
        ps1 = _mk(ctx, nc, "p", "pd_ps1", [128, 512], F32)
        ps2 = _mk(ctx, nc, "p", "pd_ps2", [128, 512], F32)
        g = 0
        for kc in range(NF):
            g = kb.emit("sync", lambda o, kc=kc: o.dma_start(
                out=gw_sb[:, kc * 8:(kc + 1) * 8], in_=gate_w[kc * 128:(kc + 1) * 128, :]),
                inc=16)
        for ch in range(NTOK // 512):
            ldx = 0
            for kc in range(NF):
                ldx = kb.emit("sync", lambda o, kc=kc, ch=ch: o.dma_start(
                    out=src_r[:, kc * 512:(kc + 1) * 512],
                    in_=src1T_f32[kc * 128:(kc + 1) * 128, ch * 512:(ch + 1) * 512]),
                    waits=[("tensor", kb.val("tensor"))] if kc == 0 else [], inc=16)
            tv = 0
            for kc in range(NF):
                tv = kb.emit("tensor", lambda o, kc=kc, ch=ch: o.matmul(
                    ps1[0:8, 0:512], gw_sb[:, kc * 8:(kc + 1) * 8],
                    src_r[:, kc * 512:(kc + 1) * 512],
                    start=(kc == 0), stop=(kc == NF - 1)),
                    waits=[("sync", ldx)] if kc == 0 else [], inc=1)
            sv = kb.emit("scalar", lambda o, ch=ch: o.activation(
                probs[:, ch * 512:(ch + 1) * 512], ps1[0:8, 0:512], AF.Sigmoid),
                waits=[("tensor", tv)], inc=1)
            kb.wait("tensor", [("scalar", sv)])
        for i in range(NT):
            tv = kb.emit("tensor", lambda o, i=i: o.transpose(
                ps2[0:128, 0:8], probs[:, i * 128:(i + 1) * 128], cst.ident[0:8, 0:8]),
                waits=[("vector", kb.val("vector")), ("scalar", kb.val("scalar"))], inc=1)
            vv = kb.emit("vector", lambda o: o.tensor_copy(ptok[:], ps2[0:128, 0:8]),
                         waits=[("tensor", tv)], inc=1)
            vv = kb.emit("vector", lambda o: o.max(m8[:], ptok[:]), inc=1)
            vv = kb.emit("vector", lambda o: o.max_index(i8[:], m8[:], ptok[:]), inc=1)
            vv = kb.emit("vector", lambda o: o.tensor_copy(i8f[:, 0:2], i8[:, 0:2]), inc=1)
            vv = kb.emit("vector", lambda o: o.tensor_tensor(
                wcol[:, 0:1], m8[:, 0:1], m8[:, 1:2], op=ALU.add), inc=1)
            vv = kb.emit("vector", lambda o: o.tensor_scalar(
                wcol[:, 0:1], wcol[:, 0:1], 1e-6, None, op0=ALU.add), inc=1)
            vv = kb.emit("vector", lambda o: o.reciprocal(wcol[:, 1:2], wcol[:, 0:1]), inc=1)
            vv = kb.emit("vector", lambda o: o.tensor_scalar(
                wcol[:, 2:3], m8[:, 0:1], wcol[:, 1:2], None, op0=ALU.mult), inc=1)
            vv = kb.emit("vector", lambda o: o.tensor_scalar(
                wcol[:, 3:4], m8[:, 1:2], wcol[:, 1:2], None, op0=ALU.mult), inc=1)
            vv = kb.emit("vector", lambda o: o.tensor_scalar(
                etok[:], cst.iota8, i8f[:, 0:1], None, op0=ALU.is_equal), inc=1)
            vv = kb.emit("vector", lambda o: o.scalar_tensor_tensor(
                etok[:], cst.iota8, i8f[:, 1:2], etok[:], op0=ALU.is_equal, op1=ALU.add), inc=1)
            sd = kb.emit("sync", lambda o, i=i: o.dma_start(
                out=sel12_o[i * 128:(i + 1) * 128, :], in_=i8f[:, 0:2]),
                waits=[("vector", vv)], inc=16)
            sd = kb.emit("sync", lambda o, i=i: o.dma_start(
                out=wts12_o[i * 128:(i + 1) * 128, :], in_=wcol[:, 2:4]), inc=16)
            tv = kb.emit("tensor", lambda o, i=i: o.transpose(
                ps2[0:8, 0:128], etok[:], cst.ident),
                waits=[("vector", vv)], inc=1)
            vv = kb.emit("vector", lambda o, i=i: o.tensor_copy(
                esum[:, i * 128:(i + 1) * 128], ps2[0:8, 0:128]),
                waits=[("tensor", tv)], inc=1)
            kb.wait("tensor", [("vector", vv), ("sync", sd)])
        zs = _mk(ctx, nc, "s", "pd_zs", [8, NTOK], F32)
        vv = kb.emit("vector", lambda o: o.memset(zs[:], 0.0), inc=1)
        vv = kb.emit("vector", lambda o: o.tensor_tensor_scan(
            esum[:], esum[:], zs[:], 0.0, op0=ALU.add, op1=ALU.add), inc=1)
        sd = kb.emit("sync", lambda o: o.dma_start(out=cincl_o, in_=esum[:]),
                     waits=[("vector", vv)], inc=16)
        vv = kb.emit("vector", lambda o: o.tensor_copy(cnt8[:], esum[:, NTOK - 1:NTOK]), inc=1)
        tv = kb.emit("tensor", lambda o: o.transpose(
            ps2[0:1, 0:8], cnt8[:], cst.ident[0:8, 0:8]), waits=[("vector", vv)], inc=1)
        vv = kb.emit("vector", lambda o: o.tensor_copy(cntrow[:], ps2[0:1, 0:8]),
                     waits=[("tensor", tv)], inc=1)
        sd = kb.emit("sync", lambda o: o.dma_start(out=cnt_o, in_=cntrow[:]),
                     waits=[("vector", vv)], inc=16)
        kb.flush()


def ph_d_route(kb, nc, cst, cincl_i, sel12_i, wts12_i, allcnt_i, maskcol_i,
               ilist2, wlist2, idxw1_o, idxw2_o):
    with contextlib.ExitStack() as ctx:
        allc = _mk(ctx, nc, "s", "pr_allc", [8, 8], F32)
        mcol = _mk(ctx, nc, "s", "pr_mcol", [8, 1], F32)
        rows = _mk(ctx, nc, "s", "pr_rows", [1, 8 * 8], F32)
        srow = _mk(ctx, nc, "s", "pr_srow", [1, NSLOT], F32)
        iw = _mk(ctx, nc, "s", "pr_iw", [NSLOT, 2 + 2 * 128], F32)
        iwi = _mk(ctx, nc, "s", "pr_iwi", [NSLOT, 2 * 128], I32)
        prebuf = _mk(ctx, nc, "s", "pr_pre", [128, NSLOT * 4], I32)
        prew = _mk(ctx, nc, "s", "pr_prew", [128, NSLOT * 4], I32)
        cinc = _mk(ctx, nc, "s", "pr_cinc", [8, 128], F32)
        ctok = _mk(ctx, nc, "s", "pr_ctok", [128, 8], F32)
        etok = _mk(ctx, nc, "s", "pr_etok", [128, 8], F32)
        self12 = _mk(ctx, nc, "s", "pr_sel", [128, 2], F32)
        wt12 = _mk(ctx, nc, "s", "pr_wt", [128, 2], F32)
        m1t = _mk(ctx, nc, "s", "pr_m1", [128, 8], F32)
        m2t = _mk(ctx, nc, "s", "pr_m2", [128, 8], F32)
        colv = _mk(ctx, nc, "s", "pr_colv", [128, 12], F32)
        coli = _mk(ctx, nc, "s", "pr_coli", [128, 2], I32)
        sci = _mk(ctx, nc, "s", "pr_sci", [128, 4], I32)
        sct = _mk(ctx, nc, "s", "pr_sct", [128, 4], I32)
        scw = _mk(ctx, nc, "s", "pr_scw", [128, 4], I32)
        obc = _mk(ctx, nc, "s", "pr_obc", [128, 8], F32)
        cbc = _mk(ctx, nc, "s", "pr_cbc", [128, 8], F32)
        ps1 = _mk(ctx, nc, "p", "pr_ps1", [128, 128], F32)
        ld = kb.emit("sync", lambda o: o.dma_start(out=allc[:], in_=allcnt_i), inc=16)
        ld = kb.emit("sync", lambda o: o.dma_start(out=mcol[:], in_=maskcol_i), inc=16)
        tv = kb.emit("tensor", lambda o: o.matmul(ps1[0:1, 0:8], mcol[:], allc[:],
                                                  start=True, stop=True),
                     waits=[("sync", ld)], inc=1)
        vv = kb.emit("vector", lambda o: o.tensor_copy(rows[0:1, 0:8], ps1[0:1, 0:8]),
                     waits=[("tensor", tv)], inc=1)
        ld2 = kb.emit("sync", lambda o: o.dma_start(
            out=cinc[:, 0:1], in_=cincl_i[:, NTOK - 1:NTOK],
            allow_slow_non_contiguous=True), inc=16)
        tv = kb.emit("tensor", lambda o: o.transpose(ps1[0:1, 0:8], cinc[0:8, 0:1],
                                                     cst.ident[0:8, 0:8]),
                     waits=[("sync", ld2), ("vector", vv)], inc=1)
        vv = kb.emit("vector", lambda o: o.tensor_copy(rows[0:1, 8:16], ps1[0:1, 0:8]),
                     waits=[("tensor", tv)], inc=1)
        vv = kb.emit("vector", lambda o: o.tensor_scalar(
            rows[0:1, 16:24], rows[0:1, 0:8], -1.0, float(CAPACITY),
            op0=ALU.mult, op1=ALU.add), inc=1)
        vv = kb.emit("vector", lambda o: o.tensor_scalar(
            rows[0:1, 16:24], rows[0:1, 16:24], 0.0, None, op0=ALU.max), inc=1)
        vv = kb.emit("vector", lambda o: o.tensor_tensor(
            rows[0:1, 16:24], rows[0:1, 16:24], rows[0:1, 8:16], op=ALU.min), inc=1)
        vv = kb.emit("vector", lambda o: o.memset(rows[0:1, 24:32], 0.0), inc=1)
        for k in range(16):
            vv = kb.emit("vector", lambda o, k=k: o.scalar_tensor_tensor(
                rows[0:1, 24:32], rows[0:1, 16:24], float(k * 512), rows[0:1, 24:32],
                op0=ALU.is_gt, op1=ALU.add), inc=1)
        vv = kb.emit("vector", lambda o: o.memset(rows[0:1, 32:40], 0.0), inc=1)
        vv = kb.emit("vector", lambda o: o.tensor_tensor_scan(
            rows[0:1, 40:48], rows[0:1, 24:32], rows[0:1, 32:40], 0.0,
            op0=ALU.add, op1=ALU.add), inc=1)
        vv = kb.emit("vector", lambda o: o.tensor_tensor(
            rows[0:1, 48:56], rows[0:1, 40:48], rows[0:1, 24:32], op=ALU.subtract), inc=1)
        vv = kb.emit("vector", lambda o: o.memset(srow[:], 0.0), inc=1)
        for e in range(8):
            vv = kb.emit("vector", lambda o, e=e: o.scalar_tensor_tensor(
                srow[:], cst.iota24, rows[0:1, 40 + e:41 + e], srow[:],
                op0=ALU.is_ge, op1=ALU.add), inc=1)
        tv = kb.emit("tensor", lambda o: o.transpose(
            ps1[0:NSLOT, 0:1], srow[:], cst.ident[0:1, 0:1]),
            waits=[("vector", vv)], inc=1)
        vv = kb.emit("vector", lambda o: o.tensor_copy(iw[0:NSLOT, 0:1], ps1[0:NSLOT, 0:1]),
                     waits=[("tensor", tv)], inc=1)
        tv = kb.emit("tensor", lambda o: o.matmul(
            ps1[0:NSLOT, 0:128], cst.ones1[0:1, 0:NSLOT], cst.iota128r,
            start=True, stop=True), waits=[("vector", vv)], inc=1)
        for wi, scale in ((0, 1024.0), (1, 2048.0)):
            vv = kb.emit("vector", lambda o, wi=wi, scale=scale: o.tensor_scalar(
                iw[0:NSLOT, 2 + wi * 128:2 + (wi + 1) * 128], ps1[0:NSLOT, 0:128],
                iw[0:NSLOT, 0:1], None, op0=ALU.bypass)
                if False else o.tensor_scalar(
                iw[0:NSLOT, 2 + wi * 128:2 + (wi + 1) * 128], ps1[0:NSLOT, 0:128],
                iw[0:NSLOT, 0:1], None, op0=ALU.add),
                waits=[("tensor", tv)] if wi == 0 else [], inc=1)
        for wi, scale in ((0, 1024.0), (1, 2048.0)):
            vv = kb.emit("vector", lambda o, wi=wi, scale=scale: o.tensor_scalar(
                iw[0:NSLOT, 1:2], iw[0:NSLOT, 0:1], scale, None, op0=ALU.mult), inc=1)
            vv = kb.emit("vector", lambda o, wi=wi: o.tensor_scalar(
                iw[0:NSLOT, 2 + wi * 128:2 + (wi + 1) * 128], ps1[0:NSLOT, 0:128],
                iw[0:NSLOT, 1:2], None, op0=ALU.add), inc=1)
        vv = kb.emit("vector", lambda o: o.tensor_copy(
            iwi[0:NSLOT, 0:256], iw[0:NSLOT, 2:258]), inc=1)
        sd = kb.emit("sync", lambda o: o.dma_start(out=idxw1_o, in_=iwi[0:NSLOT, 0:128]),
                     waits=[("vector", vv)], inc=16)
        sd = kb.emit("sync", lambda o: o.dma_start(out=idxw2_o, in_=iwi[0:NSLOT, 128:256]),
                     inc=16)
        vv = kb.emit("vector", lambda o: o.memset(prebuf[:], NTOK), inc=1)
        vv = kb.emit("vector", lambda o: o.memset(prew[:], 0), inc=1)
        for qstart in range(0, NSLOT * 512, 128 * NSLOT * 4 // 8):
            pass
        sd = kb.emit("sync", lambda o: o.dma_start(
            out=ilist2.rearrange("(a b) -> a b", b=NSLOT * 4), in_=prebuf[:]),
            waits=[("vector", vv)], inc=16)
        sd = kb.emit("sync", lambda o: o.dma_start(
            out=wlist2.rearrange("(a b) -> a b", b=NSLOT * 4), in_=prew[:]), inc=16)
        kb.wait("gq", [("sync", sd)])
        vv = kb.emit("vector", lambda o: o.tensor_scalar(
            rows[0:1, 56:64], rows[0:1, 48:56], 512.0, None, op0=ALU.mult), inc=1)
        tv = kb.emit("tensor", lambda o: o.matmul(
            ps1[0:128, 0:8], cst.ones1, rows[0:1, 0:8], start=True, stop=True),
            waits=[("vector", vv)], inc=1)
        vv = kb.emit("vector", lambda o: o.tensor_copy(obc[:], ps1[0:128, 0:8]),
                     waits=[("tensor", tv)], inc=1)
        tv = kb.emit("tensor", lambda o: o.matmul(
            ps1[0:128, 0:8], cst.ones1, rows[0:1, 56:64], start=True, stop=True),
            waits=[("vector", vv)], inc=1)
        vv = kb.emit("vector", lambda o: o.tensor_copy(cbc[:], ps1[0:128, 0:8]),
                     waits=[("tensor", tv)], inc=1)
        for i in range(NT):
            ld = kb.emit("sync", lambda o, i=i: o.dma_start(
                out=cinc[:], in_=cincl_i[:, i * 128:(i + 1) * 128]),
                waits=[("tensor", kb.val("tensor")), ("vector", kb.val("vector"))], inc=16)
            ld = kb.emit("sync", lambda o, i=i: o.dma_start(
                out=self12[:], in_=sel12_i[i * 128:(i + 1) * 128, :]), inc=16)
            ld = kb.emit("sync", lambda o, i=i: o.dma_start(
                out=wt12[:], in_=wts12_i[i * 128:(i + 1) * 128, :]), inc=16)
            tv = kb.emit("tensor", lambda o: o.transpose(
                ps1[0:128, 0:8], cinc[:], cst.ident[0:8, 0:8]),
                waits=[("sync", ld)], inc=1)
            vv = kb.emit("vector", lambda o: o.tensor_copy(ctok[:], ps1[0:128, 0:8]),
                         waits=[("tensor", tv)], inc=1)
            vv = kb.emit("vector", lambda o: o.tensor_scalar(
                etok[:], cst.iota8, self12[:, 0:1], None, op0=ALU.is_equal), inc=1)
            vv = kb.emit("vector", lambda o: o.scalar_tensor_tensor(
                etok[:], cst.iota8, self12[:, 1:2], etok[:], op0=ALU.is_equal, op1=ALU.add),
                inc=1)
            vv = kb.emit("vector", lambda o: o.tensor_tensor(
                ctok[:], ctok[:], etok[:], op=ALU.subtract), inc=1)
            for j in range(2):
                rb_ = (i * 2 + j) % 4
                vv = kb.emit("vector", lambda o, j=j: o.tensor_scalar(
                    m1t[:], cst.iota8, self12[:, j:j + 1], None, op0=ALU.is_equal), inc=1)
                vv = kb.emit("vector", lambda o: o.tensor_tensor(
                    m2t[:], m1t[:], ctok[:], op=ALU.mult), inc=1)
                vv = kb.emit("vector", lambda o, j=j: o.reduce_sum(
                    colv[:, 0 + j:1 + j], m2t[:], axis=AX.X), inc=1)
                vv = kb.emit("vector", lambda o: o.tensor_tensor(
                    m2t[:], m1t[:], obc[:], op=ALU.mult), inc=1)
                vv = kb.emit("vector", lambda o, j=j: o.reduce_sum(
                    colv[:, 2 + j:3 + j], m2t[:], axis=AX.X), inc=1)
                vv = kb.emit("vector", lambda o: o.tensor_tensor(
                    m2t[:], m1t[:], cbc[:], op=ALU.mult), inc=1)
                vv = kb.emit("vector", lambda o, j=j: o.reduce_sum(
                    colv[:, 4 + j:5 + j], m2t[:], axis=AX.X), inc=1)
                vv = kb.emit("vector", lambda o, j=j: o.tensor_tensor(
                    colv[:, 6 + j:7 + j], colv[:, 0 + j:1 + j], colv[:, 2 + j:3 + j],
                    op=ALU.add), inc=1)
                vv = kb.emit("vector", lambda o, j=j: o.tensor_scalar(
                    colv[:, 8 + j:9 + j], colv[:, 6 + j:7 + j], float(CAPACITY), None,
                    op0=ALU.is_lt), inc=1)
                vv = kb.emit("vector", lambda o, j=j: o.tensor_tensor(
                    colv[:, 10 + j:11 + j], colv[:, 4 + j:5 + j], colv[:, 0 + j:1 + j],
                    op=ALU.add), inc=1)
                vv = kb.emit("vector", lambda o, j=j: o.scalar_tensor_tensor(
                    m2t[:, 0:1], colv[:, 8 + j:9 + j], -1e6, colv[:, 10 + j:11 + j],
                    op0=ALU.mult, op1=ALU.add), inc=1)
                vv = kb.emit("vector", lambda o, j=j: o.tensor_scalar(
                    m2t[:, 1:2], m2t[:, 0:1], 1e6, None, op0=ALU.add), inc=1)
                vv = kb.emit("vector", lambda o, j=j, rb_=rb_: o.tensor_copy(
                    sci[:, rb_:rb_ + 1], m2t[:, 1:2]), inc=1)
                vv = kb.emit("vector", lambda o, i=i: o.tensor_scalar(
                    m2t[:, 2:3], cst.iota128, float(i * 128), None, op0=ALU.add), inc=1)
                vv = kb.emit("vector", lambda o, rb_=rb_: o.tensor_copy(sct[:, rb_:rb_ + 1], m2t[:, 2:3]), inc=1)
                vv = kb.emit("vector", lambda o, j=j, rb_=rb_: o.scalar_tensor_tensor(
                    scw.ap().bitcast(F32)[:, rb_:rb_ + 1], wt12[:, j:j + 1], 1.0, colv[:, 8 + j:9 + j],
                    op0=ALU.mult, op1=ALU.mult), inc=1)
                bc = kb.breg(NSLOT * 512 - 1)
                gsc = kb.emit("gq", lambda o, i=i, j=j, rb_=rb_, bc=bc: o.indirect_dma_start(
                    wlist2.unsqueeze(1), IndirectOffsetOnAxis(sci[:, rb_:rb_ + 1], 0),
                    scw[:, rb_:rb_ + 1], None,
                    bounds_check=bc[0], oob_is_err=False),
                    waits=[("vector", vv)], inc=16)
                gsc = kb.emit("gq", lambda o, i=i, j=j, rb_=rb_, bc=bc: o.indirect_dma_start(
                    ilist2.unsqueeze(1), IndirectOffsetOnAxis(sci[:, rb_:rb_ + 1], 0),
                    sct[:, rb_:rb_ + 1], None,
                    bounds_check=bc[0], oob_is_err=False), inc=16)
                if (i * 2 + j) % 4 == 3:
                    kb.wait("vector", [("gq", gsc)])
        kb.flush()


def ph_e_prep(kb, nc, cst, w1f, w2f, w1b, w2b, ysum):
    g = 0
    for e in range(8):
        g = kb.emit("gq", lambda o, e=e: o.dma_start(
            out=w1b[e * 1024:(e + 1) * 1024, :], in_=w1f[e * 1024:(e + 1) * 1024, :]), inc=16)
        g = kb.emit("gq", lambda o, e=e: o.dma_start(
            out=w2b[e * 2048:(e + 1) * 2048, :], in_=w2f[e * 2048:(e + 1) * 2048, :]), inc=16)
    with contextlib.ExitStack() as ctx:
        zt = _mk(ctx, nc, "s", "pe_zero", [128, D], F32)
        vv = kb.emit("vector", lambda o: o.memset(zt[:], 0.0), inc=1)
        sz = 0
        for i in range(NT + 1):
            sz = kb.emit("sync", lambda o, i=i: o.dma_start(
                out=ysum[i * 128:(i + 1) * 128, :], in_=zt[:]),
                waits=[("vector", vv)] if i == 0 else [], inc=16)
        kb.flush()


def ph_e(kb, nc, cst, src1nat, w1b, w2b, ilist2, wlist2, idxw1, idxw2, ysum,
         slots=range(NSLOT)):
    with contextlib.ExitStack() as ctx:
        w1_sb = _mk(ctx, nc, "s", "pe_w1", [128, 8 * 2048], BF16)
        w2_sb = _mk(ctx, nc, "s", "pe_w2", [128, 16 * 1024], BF16)
        idx_sb = _mk(ctx, nc, "s", "pe_idx", [128, 16], I32)
        wrow_sb = _mk(ctx, nc, "s", "pe_wr", [128, 4], I32)
        xe_sb = _mk(ctx, nc, "s", "pe_xe", [128, 4 * 1024], BF16)
        xeT_sb = _mk(ctx, nc, "s", "pe_xeT", [128, 8 * 512], BF16)
        h_sb = _mk(ctx, nc, "s", "pe_h", [128, 16 * 512], BF16)
        y_sb = _mk(ctx, nc, "s", "pe_y", [128, 8 * 512], F32)
        yn_sb = _mk(ctx, nc, "s", "pe_yn", [128, 1024], F32)
        ps1 = _mk(ctx, nc, "p", "pe_ps1", [128, 512], F32)
        ps2 = _mk(ctx, nc, "p", "pe_ps2", [128, 512], F32)
        psT = _mk(ctx, nc, "p", "pe_psT", [128, 512], BF16)
        for s in slots:
            tprev = kb.val("tensor")
            ld = 0
            for k in range(4):
                ld = kb.emit("sync", lambda o, s=s, k=k: o.dma_start(
                    out=idx_sb[:, k:k + 1],
                    in_=ilist2[s * 512 + k * 128:s * 512 + (k + 1) * 128].unsqueeze(1)),
                    waits=[("gq", kb.val("gq")), ("tensor", tprev),
                           ("vector", kb.val("vector"))] if k == 0 else [], inc=16)
                ld = kb.emit("sync", lambda o, s=s, k=k: o.dma_start(
                    out=wrow_sb[:, k:k + 1],
                    in_=wlist2[s * 512 + k * 128:s * 512 + (k + 1) * 128].unsqueeze(1)),
                    inc=16)
            ld = kb.emit("sync", lambda o, s=s: o.dma_start(
                out=idx_sb[:, 4:5], in_=idxw1[s].unsqueeze(1)), inc=16)
            ld = kb.emit("sync", lambda o, s=s: o.dma_start(
                out=idx_sb[:, 5:6], in_=idxw2[s].unsqueeze(1)), inc=16)
            gg = 0
            for k in range(4):
                bct = kb.breg(NTOK + 127)
                bc1 = kb.breg(8 * 1024 - 1)
                bc2 = kb.breg(8 * 2048 - 1)
                gg = kb.emit("gq", lambda o, k=k, bct=bct: o.indirect_dma_start(
                    xe_sb[:, k * 1024:(k + 1) * 1024], None,
                    src1nat, IndirectOffsetOnAxis(idx_sb[:, k:k + 1], 0),
                    bounds_check=bct[0], oob_is_err=False),
                    waits=[("sync", ld)] if k == 0 else [], inc=16)
            for f in range(NF):
                vv = kb.emit("vector", lambda o, f=f: o.tensor_scalar(
                    idx_sb[:, 6:7], idx_sb[:, 4:5], f * 128, None, op0=ALU.add),
                    waits=[("sync", ld)], inc=1)
                gg = kb.emit("gq", lambda o, f=f, bc1=bc1: o.indirect_dma_start(
                    w1_sb[:, f * 2048:(f + 1) * 2048], None,
                    w1b, IndirectOffsetOnAxis(idx_sb[:, 6:7], 0),
                    bounds_check=bc1[0], oob_is_err=False),
                    waits=[("vector", vv)], inc=16)
                kb.wait("vector", [("gq", gg)])
            for f in range(16):
                vv = kb.emit("vector", lambda o, f=f: o.tensor_scalar(
                    idx_sb[:, 6:7], idx_sb[:, 5:6], f * 128, None, op0=ALU.add), inc=1)
                gg = kb.emit("gq", lambda o, f=f, bc2=bc2: o.indirect_dma_start(
                    w2_sb[:, f * 1024:(f + 1) * 1024], None,
                    w2b, IndirectOffsetOnAxis(idx_sb[:, 6:7], 0),
                    bounds_check=bc2[0], oob_is_err=False),
                    waits=[("vector", vv)], inc=16)
                kb.wait("vector", [("gq", gg)])
            vtp = 0
            for k in range(4):
                for f in range(NF):
                    tv = kb.emit("tensor", lambda o, k=k, f=f: o.transpose(
                        psT[:, 0:128], xe_sb[:, k * 1024 + f * 128:k * 1024 + (f + 1) * 128],
                        cst.ident_bf),
                        waits=[("gq", gg), ("vector", vtp)] if (k == 0 and f == 0) else
                        [("vector", vtp)], inc=1)
                    vtp = kb.emit("vector", lambda o, k=k, f=f: o.tensor_copy(
                        xeT_sb[:, f * 512 + k * 128:f * 512 + (k + 1) * 128], psT[:, 0:128]),
                        waits=[("tensor", tv)], inc=1)
            slast = {0: 0, 1: 0}
            for f2 in range(16):
                ps = (ps1, ps2)[f2 % 2]
                tv = 0
                for kc in range(NF):
                    tv = kb.emit("tensor", lambda o, f2=f2, kc=kc, ps=ps: o.matmul(
                        ps[:, 0:512],
                        w1_sb[:, kc * 2048 + f2 * 128:kc * 2048 + (f2 + 1) * 128],
                        xeT_sb[:, kc * 512:(kc + 1) * 512],
                        start=(kc == 0), stop=(kc == NF - 1)),
                        waits=[("vector", vtp), ("scalar", slast[f2 % 2])] if kc == 0 else [],
                        inc=1)
                sv = kb.emit("scalar", lambda o, f2=f2, ps=ps: o.activation(
                    h_sb[:, f2 * 512:(f2 + 1) * 512], ps[:, 0:512], AF.Gelu),
                    waits=[("tensor", tv)], inc=1)
                slast[f2 % 2] = sv
            vlast = {0: slast[0], 1: slast[1]}
            for o_ in range(NF):
                ps = (ps1, ps2)[o_ % 2]
                tv = 0
                for kc in range(16):
                    tv = kb.emit("tensor", lambda o, o_=o_, kc=kc, ps=ps: o.matmul(
                        ps[:, 0:512],
                        w2_sb[:, kc * 1024 + o_ * 128:kc * 1024 + (o_ + 1) * 128],
                        h_sb[:, kc * 512:(kc + 1) * 512],
                        start=(kc == 0), stop=(kc == 15)),
                        waits=[("scalar", slast[o_ % 2]), ("vector", vlast[o_ % 2])]
                        if kc == 0 else [], inc=1)
                vv = kb.emit("vector", lambda o, o_=o_, ps=ps: o.tensor_copy(
                    y_sb[:, o_ * 512:(o_ + 1) * 512], ps[:, 0:512]),
                    waits=[("tensor", tv)], inc=1)
                vlast[o_ % 2] = vv
            for k in range(4):
                vtr = 0
                for f in range(NF):
                    tv = kb.emit("tensor", lambda o, k=k, f=f: o.transpose(
                        ps1[:, 0:128], y_sb[:, f * 512 + k * 128:f * 512 + (k + 1) * 128],
                        cst.ident),
                        waits=([("vector", vlast[0]), ("vector", vlast[1])]
                               if (k == 0 and f == 0) else [("vector", vtr)]),
                        inc=1)
                    vtr = kb.emit("vector", lambda o, k=k, f=f: o.tensor_scalar(
                        yn_sb[:, f * 128:(f + 1) * 128], ps1[:, 0:128],
                        wrow_sb.ap().bitcast(F32)[:, k:k + 1], None, op0=ALU.mult),
                        waits=[("tensor", tv)], inc=1)
                gg = kb.emit("gq", lambda o, k=k, bct=bct: o.indirect_dma_start(
                    ysum, IndirectOffsetOnAxis(idx_sb[:, k:k + 1], 0),
                    yn_sb[:], None,
                    bounds_check=bct[0], oob_is_err=False, compute_op=ALU.add),
                    waits=[("vector", vtr)], inc=16)
                kb.wait("vector", [("gq", gg)])
                kb.wait("tensor", [("gq", gg)])
        kb.flush()


def ph_f(kb, nc, cst, xT, s0T, src1T, ysum, n2w, x1T, xn1T, xn1T_bf):
    with contextlib.ExitStack() as ctx:
        yrow = _mk(ctx, nc, "s", "pf_yrow", [128, 4 * 1024], F32)
        ysT = _mk(ctx, nc, "s", "pf_ysT", [128, 8 * 512], F32)
        srcc = _mk(ctx, nc, "s", "pf_src", [128, 8 * 512], F32)
        xc = _mk(ctx, nc, "s", "pf_x", [128, 8 * 512], F32)
        s0c = _mk(ctx, nc, "s", "pf_s0", [128, 8 * 512], F32)
        sq = _mk(ctx, nc, "s", "pf_sq", [128, 512], F32)
        rstd = _mk(ctx, nc, "s", "pf_rstd", [128, 512], F32)
        stat = _mk(ctx, nc, "s", "pf_stat", [128, 512], F32)
        nrow = _mk(ctx, nc, "s", "pf_nrow", [1, D], F32)
        ncol = _mk(ctx, nc, "s", "pf_ncol", [128, NF], F32)
        ev = _mk(ctx, nc, "s", "pf_ev", [128, 512], F32)
        evb = _mk(ctx, nc, "s", "pf_evb", [128, 512], BF16)
        ps1 = _mk(ctx, nc, "p", "pf_ps1", [128, 512], F32)
        ps2 = _mk(ctx, nc, "p", "pf_ps2", [128, 512], F32)
        ld = kb.emit("sync", lambda o: o.dma_start(out=nrow[0:1, :], in_=n2w), inc=16)
        kb.wait("tensor", [("sync", ld)])
        kb.wait("vector", [("sync", ld)])
        col_bias(kb, nc, cst, nrow, 0, ncol, 0, ps1)
        for ch in range(8):
            t0 = ch * 512
            ld = 0
            for k in range(4):
                ld = kb.emit("sync", lambda o, t0=t0, k=k: o.dma_start(
                    out=yrow[:, k * 1024:(k + 1) * 1024],
                    in_=ysum[t0 + k * 128:t0 + (k + 1) * 128, :]),
                    waits=[("tensor", kb.val("tensor")), ("vector", kb.val("vector")),
                           ("sync", kb.val("sync"))]
                    if k == 0 else [], inc=16)
            for f in range(NF):
                ld = kb.emit("sync", lambda o, t0=t0, f=f: o.dma_start(
                    out=srcc[:, f * 512:(f + 1) * 512],
                    in_=src1T[f * 128:(f + 1) * 128, t0:t0 + 512]), inc=16)
                ld = kb.emit("sync", lambda o, t0=t0, f=f: o.dma_start(
                    out=xc[:, f * 512:(f + 1) * 512],
                    in_=xT[f * 128:(f + 1) * 128, t0:t0 + 512]), inc=16)
                ld = kb.emit("sync", lambda o, t0=t0, f=f: o.dma_start(
                    out=s0c[:, f * 512:(f + 1) * 512],
                    in_=s0T[f * 128:(f + 1) * 128, t0:t0 + 512]), inc=16)
            vtr = 0
            for k in range(4):
                for f in range(NF):
                    tv = kb.emit("tensor", lambda o, k=k, f=f: o.transpose(
                        ps1[:, 0:128], yrow[:, k * 1024 + f * 128:k * 1024 + (f + 1) * 128],
                        cst.ident),
                        waits=[("sync", ld), ("vector", vtr)], inc=1)
                    vtr = kb.emit("vector", lambda o, k=k, f=f: o.tensor_copy(
                        ysT[:, f * 512 + k * 128:f * 512 + (k + 1) * 128], ps1[:, 0:128]),
                        waits=[("tensor", tv)], inc=1)
            vv = 0
            for f in range(NF):
                vv = kb.emit("vector", lambda o, f=f: o.tensor_add(
                    srcc[:, f * 512:(f + 1) * 512], srcc[:, f * 512:(f + 1) * 512],
                    ysT[:, f * 512:(f + 1) * 512]), inc=1)
            tvs = 0
            for f in range(NF):
                sv = kb.emit("scalar", lambda o, f=f: o.activation(
                    sq[:], srcc[:, f * 512:(f + 1) * 512], AF.Square),
                    waits=[("vector", vv), ("tensor", tvs)], inc=1)
                tvs = kb.emit("tensor", lambda o, f=f: o.matmul(
                    ps2[:, 0:512], cst.ones128, sq[:],
                    start=(f == 0), stop=(f == NF - 1)),
                    waits=[("scalar", sv)], inc=1)
            vv = kb.emit("vector", lambda o: o.tensor_scalar(
                rstd[:], ps2[:, 0:512], 1.0 / D, 1e-6, op0=ALU.mult, op1=ALU.add),
                waits=[("tensor", tvs)], inc=1)
            sv = kb.emit("scalar", lambda o: o.activation(sq[:], rstd[:], AF.Ln),
                         waits=[("vector", vv)], inc=1)
            sv = kb.emit("scalar", lambda o: o.activation(rstd[:], sq[:], AF.Exp, scale=-0.5),
                         inc=1)
            tvs = 0
            for f in range(NF):
                vv = kb.emit("vector", lambda o, f=f: o.scalar_tensor_tensor(
                    ev[:], srcc[:, f * 512:(f + 1) * 512], 1.0, rstd[:],
                    op0=ALU.mult, op1=ALU.mult), waits=[("scalar", sv)], inc=1)
                vv = kb.emit("vector", lambda o, f=f: o.tensor_scalar(
                    ev[:], ev[:], ncol[:, f:f + 1], None, op0=ALU.mult), inc=1)
                s2 = kb.emit("scalar", lambda o, f=f: o.activation(
                    ev[:], ev[:], AF.Silu), waits=[("vector", vv)], inc=1)
                vv = kb.emit("vector", lambda o, f=f: o.tensor_mul(
                    ev[:], ev[:], s0c[:, f * 512:(f + 1) * 512]),
                    waits=[("scalar", s2)], inc=1)
                vv = kb.emit("vector", lambda o, f=f: o.tensor_add(
                    xc[:, f * 512:(f + 1) * 512], xc[:, f * 512:(f + 1) * 512], ev[:]), inc=1)
                sd = kb.emit("sync", lambda o, f=f, t0=t0: o.dma_start(
                    out=x1T[f * 128:(f + 1) * 128, t0:t0 + 512],
                    in_=xc[:, f * 512:(f + 1) * 512]), waits=[("vector", vv)], inc=16)
            tvs = 0
            for f in range(NF):
                tvs = kb.emit("tensor", lambda o, f=f: o.matmul(
                    ps2[:, 0:512], cst.ones128, xc[:, f * 512:(f + 1) * 512],
                    start=(f == 0), stop=(f == NF - 1)),
                    waits=[("vector", vv)], inc=1)
            vv = kb.emit("vector", lambda o: o.tensor_scalar(
                stat[:], ps2[:, 0:512], 1.0 / D, None, op0=ALU.mult),
                waits=[("tensor", tvs)], inc=1)
            tvs = 0
            for f in range(NF):
                vv2 = kb.emit("vector", lambda o, f=f: o.tensor_tensor(
                    ev[:], xc[:, f * 512:(f + 1) * 512], stat[:], op=ALU.subtract),
                    inc=1)
                sv = kb.emit("scalar", lambda o, f=f: o.activation(
                    sq[:], ev[:], AF.Square), waits=[("vector", vv2), ("tensor", tvs)], inc=1)
                tvs = kb.emit("tensor", lambda o, f=f: o.matmul(
                    ps2[:, 0:512], cst.ones128, sq[:],
                    start=(f == 0), stop=(f == NF - 1)), waits=[("scalar", sv)], inc=1)
            vv = kb.emit("vector", lambda o: o.tensor_scalar(
                rstd[:], ps2[:, 0:512], 1.0 / D, 1e-5, op0=ALU.mult, op1=ALU.add),
                waits=[("tensor", tvs)], inc=1)
            sv = kb.emit("scalar", lambda o: o.activation(sq[:], rstd[:], AF.Ln),
                         waits=[("vector", vv)], inc=1)
            sv = kb.emit("scalar", lambda o: o.activation(rstd[:], sq[:], AF.Exp, scale=-0.5),
                         inc=1)
            for f in range(NF):
                vv = kb.emit("vector", lambda o, f=f: o.tensor_tensor(
                    ev[:], xc[:, f * 512:(f + 1) * 512], stat[:], op=ALU.subtract),
                    waits=[("scalar", sv)], inc=1)
                vv = kb.emit("vector", lambda o, f=f: o.tensor_mul(ev[:], ev[:], rstd[:]), inc=1)
                vv = kb.emit("vector", lambda o, f=f: o.tensor_copy(evb[:], ev[:]), inc=1)
                sd = kb.emit("sync", lambda o, f=f, t0=t0: o.dma_start(
                    out=xn1T[f * 128:(f + 1) * 128, t0:t0 + 512], in_=ev[:]),
                    waits=[("vector", vv)], inc=16)
                sd = kb.emit("sync", lambda o, f=f, t0=t0: o.dma_start(
                    out=xn1T_bf[f * 128:(f + 1) * 128, t0:t0 + 512], in_=evb[:]), inc=16)
                kb.wait("vector", [("sync", sd)])
        kb.flush()


def ph_f2(kb, nc, cst, xn1T, xn1T_bf, wq, wk, wv, bq, bk, bv, waT, wbT, ba, bb,
          convq, convk, convv, qTb, kTb, vTb, lnaT, bgT, lnbT):
    for wi, (w, brow_in, cw, outT, do_l2) in enumerate((
            (wq, bq, convq, qTb, True), (wk, bk, convk, kTb, True),
            (wv, bv, convv, vTb, False))):
        with contextlib.ExitStack() as ctx:
            w_sb = _mk(ctx, nc, "s", f"pf2_w{wi}", [128, 8 * 1024], BF16)
            xn_r = _mk(ctx, nc, "s", f"pf2_xn{wi}", [128, 8 * 2048], BF16)
            brow = _mk(ctx, nc, "s", f"pf2_br{wi}", [1, D], F32)
            bcol = _mk(ctx, nc, "s", f"pf2_bc{wi}", [128, NF], F32)
            ccol = _mk(ctx, nc, "s", f"pf2_cc{wi}", [128, NF * 4], F32)
            zp = _mk(ctx, nc, "s", f"pf2_zp{wi}", [128, 8 * 2052], F32)
            cv = _mk(ctx, nc, "s", f"pf2_cv{wi}", [128, 2048], F32)
            sq = _mk(ctx, nc, "s", f"pf2_sq{wi}", [128, 2048], F32)
            nrm = _mk(ctx, nc, "s", f"pf2_n{wi}", [128, 2048], F32)
            outb = _mk(ctx, nc, "s", f"pf2_o{wi}", [128, 2048], BF16)
            ps1 = _mk(ctx, nc, "p", f"pf2_ps1{wi}", [128, 512], F32)
            ps2 = _mk(ctx, nc, "p", f"pf2_ps2{wi}", [128, 512], F32)
            gw = 0
            for kc in range(NF):
                gw = kb.emit("gq", lambda o, kc=kc, w=w: o.dma_start(
                    out=w_sb[:, kc * 1024:(kc + 1) * 1024],
                    in_=w[kc * 128:(kc + 1) * 128, :]), inc=16)
            ld = kb.emit("sync", lambda o, brow_in=brow_in: o.dma_start(
                out=brow[0:1, :], in_=brow_in), inc=16)
            ldc = kb.emit("sync", lambda o, cw=cw: o.dma_start(
                out=ccol[:], in_=cw), inc=16)
            kb.wait("tensor", [("sync", ld)])
            kb.wait("vector", [("sync", ldc)])
            col_bias(kb, nc, cst, brow, 0, bcol, 0, ps1)
            for b in range(B_PER_CORE):
                tb = b * S
                ldx = 0
                for kc in range(NF):
                    ldx = kb.emit("sync", lambda o, kc=kc, tb=tb: o.dma_start(
                        out=xn_r[:, kc * 2048:(kc + 1) * 2048],
                        in_=xn1T_bf[kc * 128:(kc + 1) * 128, tb:tb + S]),
                        waits=[("tensor", kb.val("tensor")), ("vector", kb.val("vector"))]
                        if kc == 0 else [], inc=16)
                for ft in range(NF):
                    vpad = kb.emit("vector", lambda o, ft=ft: o.memset(
                        zp[:, ft * 2052:ft * 2052 + 2], 0.0), inc=1)
                    vpad = kb.emit("vector", lambda o, ft=ft: o.memset(
                        zp[:, ft * 2052 + 2050:ft * 2052 + 2052], 0.0), inc=1)
                    for chk in range(4):
                        ps = (ps1, ps2)[chk % 2]
                        tv = 0
                        for kc in range(NF):
                            tv = kb.emit("tensor", lambda o, ft=ft, chk=chk, kc=kc, ps=ps: o.matmul(
                                ps[:, 0:512],
                                w_sb[:, kc * 1024 + ft * 128:kc * 1024 + (ft + 1) * 128],
                                xn_r[:, kc * 2048 + chk * 512:kc * 2048 + (chk + 1) * 512],
                                start=(kc == 0), stop=(kc == NF - 1)),
                                waits=[("sync", ldx), ("gq", gw), ("vector", kb.val("vector"))]
                                if kc == 0 else [], inc=1)
                        vv = kb.emit("vector", lambda o, ft=ft, chk=chk, ps=ps: o.tensor_scalar(
                            cv[:, chk * 512:(chk + 1) * 512], ps[:, 0:512],
                            bcol[:, ft:ft + 1], None, op0=ALU.add),
                            waits=[("tensor", tv)], inc=1)
                    sv = kb.emit("scalar", lambda o, ft=ft: o.activation(
                        zp[:, ft * 2052 + 2:ft * 2052 + 2050], cv[:], AF.Silu),
                        waits=[("vector", vv)], inc=1)
                    base = ft * 2052 + 2
                    vv = kb.emit("vector", lambda o, ft=ft, base=base: o.tensor_scalar(
                        cv[:], zp[:, base - 2:base + 2046], ccol[:, ft * 4:ft * 4 + 1],
                        None, op0=ALU.mult), waits=[("scalar", sv)], inc=1)
                    for j in range(1, 4):
                        vv = kb.emit("vector", lambda o, ft=ft, j=j, base=base: o.scalar_tensor_tensor(
                            cv[:], zp[:, base - 2 + j:base - 2 + j + 2048],
                            ccol[:, ft * 4 + j:ft * 4 + j + 1], cv[:],
                            op0=ALU.mult, op1=ALU.add), inc=1)
                    sv = kb.emit("scalar", lambda o, ft=ft: o.activation(
                        cv[:], cv[:], AF.Silu), waits=[("vector", vv)], inc=1)
                    vv = kb.emit("vector", lambda o, ft=ft: o.tensor_copy(
                        zp[:, ft * 2052 + 2:ft * 2052 + 2050], cv[:]),
                        waits=[("scalar", sv)], inc=1)
                if do_l2:
                    tvs = 0
                    for chk in range(4):
                        for f in range(NF):
                            sv = kb.emit("scalar", lambda o, f=f, chk=chk: o.activation(
                                sq[:, 0:512], zp[:, f * 2052 + 2 + chk * 512:f * 2052 + 2 + (chk + 1) * 512],
                                AF.Square), waits=[("vector", kb.val("vector")), ("tensor", tvs)], inc=1)
                            tvs = kb.emit("tensor", lambda o, f=f, chk=chk: o.matmul(
                                ps2[:, 0:512], cst.ones128, sq[:, 0:512],
                                start=(f == 0), stop=(f == NF - 1)),
                                waits=[("scalar", sv)], inc=1)
                        vv = kb.emit("vector", lambda o, chk=chk: o.tensor_copy(
                            sq[:, 0:512], ps2[:, 0:512]), waits=[("tensor", tvs)], inc=1)
                        sv = kb.emit("scalar", lambda o, chk=chk: o.activation(
                            sq[:, 512:1024], sq[:, 0:512], AF.Sqrt),
                            waits=[("vector", vv)], inc=1)
                        vv = kb.emit("vector", lambda o, chk=chk: o.tensor_scalar(
                            sq[:, 512:1024], sq[:, 512:1024], 1e-6, None, op0=ALU.add),
                            waits=[("scalar", sv)], inc=1)
                        vv = kb.emit("vector", lambda o, chk=chk: o.reciprocal(
                            nrm[:, chk * 512:(chk + 1) * 512], sq[:, 512:1024]), inc=1)
                else:
                    vv = kb.emit("vector", lambda o: o.memset(nrm[:], 1.0), inc=1)
                for f in range(NF):
                    vv = kb.emit("vector", lambda o, f=f: o.tensor_tensor(
                        outb[:], zp[:, f * 2052 + 2:f * 2052 + 2050], nrm[:], op=ALU.mult),
                        inc=1)
                    sd = kb.emit("sync", lambda o, f=f, tb=tb, outT=outT: o.dma_start(
                        out=outT[f * 128:(f + 1) * 128, tb:tb + S], in_=outb[:]),
                        waits=[("vector", vv)], inc=16)
                    kb.wait("vector", [("sync", sd)])
            kb.flush()
    with contextlib.ExitStack() as ctx:
        wab = _mk(ctx, nc, "s", "pf2_wab", [128, 2 * NF * 16], F32)
        xn32 = _mk(ctx, nc, "s", "pf2_x32", [128, 8 * 512], F32)
        babb = _mk(ctx, nc, "s", "pf2_babb", [1, 32], F32)
        bcol2 = _mk(ctx, nc, "s", "pf2_bcg", [16, 2], F32)
        gout = _mk(ctx, nc, "s", "pf2_gout", [16, 6 * 512], F32)
        ps1 = _mk(ctx, nc, "p", "pf2_ps1g", [128, 512], F32)
        ld = 0
        for wi, w in enumerate((waT, wbT)):
            for kc in range(NF):
                ld = kb.emit("sync", lambda o, wi=wi, kc=kc, w=w: o.dma_start(
                    out=wab[:, (wi * NF + kc) * 16:(wi * NF + kc + 1) * 16],
                    in_=w[kc * 128:(kc + 1) * 128, :]), inc=16)
        ld = kb.emit("sync", lambda o: o.dma_start(out=babb[0:1, 0:16], in_=ba), inc=16)
        ld = kb.emit("sync", lambda o: o.dma_start(out=babb[0:1, 16:32], in_=bb), inc=16)
        tv = kb.emit("tensor", lambda o: o.transpose(
            ps1[0:16, 0:2], babb[0:1, 0:32].rearrange("a (b c) -> a b c", b=2)
            .transpose([0, 2, 1]).squeeze(0), cst.ident[0:1, 0:1]),
            waits=[("sync", ld)], inc=1) if False else 0
        tv = kb.emit("tensor", lambda o: o.transpose(
            ps1[0:16, 0:1], babb[0:1, 0:16], cst.ident[0:1, 0:1]),
            waits=[("sync", ld)], inc=1)
        vv = kb.emit("vector", lambda o: o.tensor_copy(bcol2[:, 0:1], ps1[0:16, 0:1]),
                     waits=[("tensor", tv)], inc=1)
        tv = kb.emit("tensor", lambda o: o.transpose(
            ps1[0:16, 0:1], babb[0:1, 16:32], cst.ident[0:1, 0:1]),
            waits=[("vector", vv)], inc=1)
        vv = kb.emit("vector", lambda o: o.tensor_copy(bcol2[:, 1:2], ps1[0:16, 0:1]),
                     waits=[("tensor", tv)], inc=1)
        for ch in range(8):
            t0 = ch * 512
            ldx = 0
            for kc in range(NF):
                ldx = kb.emit("sync", lambda o, kc=kc, t0=t0: o.dma_start(
                    out=xn32[:, kc * 512:(kc + 1) * 512],
                    in_=xn1T[kc * 128:(kc + 1) * 128, t0:t0 + 512]),
                    waits=[("tensor", kb.val("tensor")), ("vector", kb.val("vector"))]
                    if kc == 0 else [], inc=16)
            for wi, (dstT, do_ln) in enumerate(((lnaT, True), (lnbT, True))):
                tv = 0
                for kc in range(NF):
                    tv = kb.emit("tensor", lambda o, wi=wi, kc=kc: o.matmul(
                        ps1[0:16, 0:512], wab[:, (wi * NF + kc) * 16:(wi * NF + kc + 1) * 16],
                        xn32[:, kc * 512:(kc + 1) * 512],
                        start=(kc == 0), stop=(kc == NF - 1)),
                        waits=[("sync", ldx), ("vector", kb.val("vector"))] if kc == 0 else [],
                        inc=1)
                sv = kb.emit("scalar", lambda o, wi=wi: o.activation(
                    gout[:, 0:512], ps1[0:16, 0:512], AF.Sigmoid,
                    bias=bcol2[0:16, wi:wi + 1]), waits=[("tensor", tv)], inc=1)
                if wi == 1:
                    sd = kb.emit("sync", lambda o, t0=t0: o.dma_start(
                        out=bgT[:, t0:t0 + 512], in_=gout[0:16, 0:512]),
                        waits=[("scalar", sv)], inc=16)
                vv = kb.emit("vector", lambda o: o.tensor_scalar(
                    gout[:, 512:1024], gout[:, 0:512], 1e-38, None, op0=ALU.max),
                    waits=[("scalar", sv)], inc=1)
                sv = kb.emit("scalar", lambda o: o.activation(
                    gout[:, 1024:1536], gout[:, 512:1024], AF.Ln),
                    waits=[("vector", vv)], inc=1)
                sd = kb.emit("sync", lambda o, wi=wi, t0=t0, dstT=dstT: o.dma_start(
                    out=dstT[:, t0:t0 + 512], in_=gout[0:16, 1024:1536]),
                    waits=[("scalar", sv)], inc=16)
                kb.wait("scalar", [("sync", sd)])
                kb.wait("vector", [("sync", sd)])
        kb.flush()


def ph_g(kb, nc, cst, qTb, kTb, vTb, lnaT, bgT, lnbT, hot16_d, mGA_d, mGAp_d, mGD_d,
         woT, bo, s1T, otmpG):
    CH = 128
    NCH = S // CH
    with contextlib.ExitStack() as ctx:
        hot = _mk(ctx, nc, "s", "pg_hot", [16, 16 * 128], F32)
        mA = _mk(ctx, nc, "s", "pg_mA", [128, 128], F32)
        mAp = _mk(ctx, nc, "s", "pg_mAp", [128, 128], F32)
        mD = _mk(ctx, nc, "s", "pg_mD", [128, 128], F32)
        q_sb = _mk(ctx, nc, "s", "pg_q", [128, 8 * 128], BF16)
        k_sb = _mk(ctx, nc, "s", "pg_k", [128, 8 * 128], BF16)
        v_sb = _mk(ctx, nc, "s", "pg_v", [128, 8 * 128], BF16)
        ga = _mk(ctx, nc, "s", "pg_ga", [16, 6 * 128], F32)
        ellc = _mk(ctx, nc, "s", "pg_ellc", [128, 112], F32)
        ellr = _mk(ctx, nc, "s", "pg_ellr", [1, 16], F32)
        Ssb = _mk(ctx, nc, "s", "pg_S", [64, 32 * 64], F32)
        Stsb = _mk(ctx, nc, "s", "pg_St", [64, 32 * 64], F32)
        Sbf = _mk(ctx, nc, "s", "pg_Sbf", [64, 32 * 64], BF16)
        Stbf = _mk(ctx, nc, "s", "pg_Stbf", [64, 32 * 64], BF16)
        kst = _mk(ctx, nc, "s", "pg_kst", [64, 128], BF16)
        qst = _mk(ctx, nc, "s", "pg_qst", [64, 128], BF16)
        vst2 = _mk(ctx, nc, "s", "pg_vst", [64, 128], BF16)
        knat = _mk(ctx, nc, "s", "pg_knat", [128, 64], BF16)
        vnat = _mk(ctx, nc, "s", "pg_vnat", [128, 64], F32)
        Bbf = _mk(ctx, nc, "s", "pg_B", [128, 128], BF16)
        Bpbf = _mk(ctx, nc, "s", "pg_Bp", [128, 128], BF16)
        Gbf = _mk(ctx, nc, "s", "pg_G", [128, 128], BF16)
        Gpbf = _mk(ctx, nc, "s", "pg_Gp", [128, 128], BF16)
        G2bf = _mk(ctx, nc, "s", "pg_G2", [128, 128], BF16)
        G2pbf = _mk(ctx, nc, "s", "pg_G2p", [128, 128], BF16)
        Ybf = _mk(ctx, nc, "s", "pg_Y", [128, 64], BF16)
        Ytbf = _mk(ctx, nc, "s", "pg_Yt", [64, 128], BF16)
        Ysbf = _mk(ctx, nc, "s", "pg_Ys", [128, 64], BF16)
        P2bf = _mk(ctx, nc, "s", "pg_P2", [128, 128], BF16)
        tmp1 = _mk(ctx, nc, "s", "pg_t1", [128, 128], F32)
        tmp2 = _mk(ctx, nc, "s", "pg_t2", [128, 128], F32)
        tmp3 = _mk(ctx, nc, "s", "pg_t3", [128, 64], F32)
        gcol = _mk(ctx, nc, "s", "pg_gcol", [64, 2], F32)
        och = _mk(ctx, nc, "s", "pg_och", [64, 128], BF16)
        psA = _mk(ctx, nc, "p", "pg_psA", [128, 128], F32)
        psB = _mk(ctx, nc, "p", "pg_psB", [128, 128], F32)
        psC = _mk(ctx, nc, "p", "pg_psC", [128, 128], F32)
        psD = _mk(ctx, nc, "p", "pg_psD", [128, 64], F32)
        psE = _mk(ctx, nc, "p", "pg_psE", [128, 16], F32)
        psT = _mk(ctx, nc, "p", "pg_psT", [128, 128], BF16)
        ld = kb.emit("sync", lambda o: o.dma_start(out=hot[:], in_=hot16_d), inc=16)
        ld2 = kb.emit("sync", lambda o: o.dma_start(out=mA[:], in_=mGA_d), inc=16)
        ld2 = kb.emit("sync", lambda o: o.dma_start(out=mAp[:], in_=mGAp_d), inc=16)
        ld2 = kb.emit("sync", lambda o: o.dma_start(out=mD[:], in_=mGD_d), inc=16)
        vv = kb.emit("vector", lambda o: o.memset(Ssb[:], 0.0), waits=[("sync", ld2)], inc=1)
        vv = kb.emit("vector", lambda o: o.memset(Stsb[:], 0.0), inc=1)
        vv = kb.emit("vector", lambda o: o.memset(Sbf[:], 0.0), inc=1)
        vv = kb.emit("vector", lambda o: o.memset(Stbf[:], 0.0), inc=1)
        kb.wait("tensor", [("vector", vv), ("sync", ld)])
        for b in range(B_PER_CORE):
            for c in range(NCH):
                t0 = b * S + c * CH
                ldx = 0
                for kc in range(NF):
                    ldx = kb.emit("sync", lambda o, kc=kc, t0=t0: o.dma_start(
                        out=q_sb[:, kc * 128:(kc + 1) * 128],
                        in_=qTb[kc * 128:(kc + 1) * 128, t0:t0 + CH]),
                        waits=[("tensor", kb.val("tensor")), ("vector", kb.val("vector")),
                               ("scalar", kb.val("scalar"))] if kc == 0 else [], inc=16)
                    ldx = kb.emit("sync", lambda o, kc=kc, t0=t0: o.dma_start(
                        out=k_sb[:, kc * 128:(kc + 1) * 128],
                        in_=kTb[kc * 128:(kc + 1) * 128, t0:t0 + CH]), inc=16)
                    ldx = kb.emit("sync", lambda o, kc=kc, t0=t0: o.dma_start(
                        out=v_sb[:, kc * 128:(kc + 1) * 128],
                        in_=vTb[kc * 128:(kc + 1) * 128, t0:t0 + CH]), inc=16)
                ldx = kb.emit("sync", lambda o, t0=t0: o.dma_start(
                    out=ga[:, 0:128], in_=lnaT[:, t0:t0 + CH]), inc=16)
                ldx = kb.emit("sync", lambda o, t0=t0: o.dma_start(
                    out=ga[:, 128:256], in_=bgT[:, t0:t0 + CH]), inc=16)
                ldx = kb.emit("sync", lambda o, t0=t0: o.dma_start(
                    out=ga[:, 256:384], in_=lnbT[:, t0:t0 + CH]), inc=16)
                vv = kb.emit("vector", lambda o: o.memset(ga[:, 640:768], 0.0),
                             waits=[("sync", ldx)], inc=1)
                vv = kb.emit("vector", lambda o: o.tensor_tensor_scan(
                    ga[:, 384:512], ga[:, 0:128], ga[:, 640:768], 0.0,
                    op0=ALU.add, op1=ALU.add), inc=1)
                vv = kb.emit("vector", lambda o: o.tensor_add(
                    ga[:, 512:640], ga[:, 384:512], ga[:, 256:384]), inc=1)
                tv = kb.emit("tensor", lambda o: o.transpose(
                    psA[0:128, 0:16], ga[0:16, 384:512], cst.ident[0:16, 0:16]),
                    waits=[("vector", vv)], inc=1)
                vv = kb.emit("vector", lambda o: o.tensor_copy(
                    ellc[:, 0:16], psA[0:128, 0:16]), waits=[("tensor", tv)], inc=1)
                tv = kb.emit("tensor", lambda o: o.transpose(
                    psA[0:128, 0:16], ga[0:16, 512:640], cst.ident[0:16, 0:16]),
                    waits=[("vector", vv)], inc=1)
                vv = kb.emit("vector", lambda o: o.tensor_copy(
                    ellc[:, 16:32], psA[0:128, 0:16]), waits=[("tensor", tv)], inc=1)
                sv = kb.emit("scalar", lambda o: o.activation(
                    ellc[:, 32:48], ellc[:, 0:16], AF.Exp), waits=[("vector", vv)], inc=1)
                vv = kb.emit("vector", lambda o: o.tensor_tensor(
                    ellc[:, 96:112], ellc[:, 16:32], ellc[:, 0:16],
                    op=ALU.subtract), inc=1)
                sv = kb.emit("scalar", lambda o: o.activation(
                    ellc[:, 48:64], ellc[:, 96:112], AF.Exp),
                    waits=[("vector", vv)], inc=1)
                tv = kb.emit("tensor", lambda o: o.transpose(
                    psA[0:1, 0:16], ga[0:16, 511:512], cst.ident[0:16, 0:16]),
                    waits=[("vector", kb.val("vector"))], inc=1)
                vv = kb.emit("vector", lambda o: o.tensor_copy(
                    ellr[0:1, 0:16], psA[0:1, 0:16]), waits=[("tensor", tv)], inc=1)
                tv = kb.emit("tensor", lambda o: o.matmul(
                    psE[0:128, 0:16], cst.ones1[0:1, 0:128], ellr[0:1, 0:16],
                    start=True, stop=True),
                    waits=[("vector", vv), ("scalar", kb.val("scalar"))], inc=1)
                vv = kb.emit("vector", lambda o: o.tensor_tensor(
                    ellc[:, 96:112], psE[0:128, 0:16], ellc[:, 0:16],
                    op=ALU.subtract), waits=[("tensor", tv)], inc=1)
                sv = kb.emit("scalar", lambda o: o.activation(
                    ellc[:, 64:80], ellc[:, 96:112], AF.Exp),
                    waits=[("vector", vv)], inc=1)
                sv = kb.emit("scalar", lambda o: o.activation(
                    ellc[:, 80:96], psE[0:128, 0:16], AF.Exp), inc=1)
                kb.wait("tensor", [("scalar", sv)])
                for h in range(H0):
                    hb = (h % 2) * 64
                    hf = h // 2
                    qsl = q_sb[hb:hb + 64, hf * 128:(hf + 1) * 128]
                    ksl = k_sb[hb:hb + 64, hf * 128:(hf + 1) * 128]
                    vsl = v_sb[hb:hb + 64, hf * 128:(hf + 1) * 128]
                    Scur = Ssb[:, (b * 16 + h) * 64:(b * 16 + h + 1) * 64]
                    Stcur = Stsb[:, (b * 16 + h) * 64:(b * 16 + h + 1) * 64]
                    Scurb = Sbf[:, (b * 16 + h) * 64:(b * 16 + h + 1) * 64]
                    Stcurb = Stbf[:, (b * 16 + h) * 64:(b * 16 + h + 1) * 64]
                    vprev = kb.val("vector")
                    vst = kb.emit("vector", lambda o, ksl=ksl: o.tensor_copy(kst[:], ksl),
                                  waits=[("tensor", kb.val("tensor"))], inc=1)
                    vst = kb.emit("vector", lambda o, qsl=qsl: o.tensor_copy(qst[:], qsl), inc=1)
                    vst = kb.emit("vector", lambda o, vsl=vsl: o.tensor_copy(vst2[:], vsl), inc=1)
                    tv = kb.emit("tensor", lambda o: o.transpose(
                        psT[0:128, 0:64], kst[:], cst.ident_bf[0:64, 0:64]),
                        waits=[("vector", vst)], inc=1)
                    vv = kb.emit("vector", lambda o: o.tensor_copy(knat[:], psT[0:128, 0:64]),
                                 waits=[("tensor", tv)], inc=1)
                    tv = kb.emit("tensor", lambda o: o.transpose(
                        psT[0:128, 0:64], vst2[:], cst.ident_bf[0:64, 0:64]),
                        waits=[("vector", vv)], inc=1)
                    vv = kb.emit("vector", lambda o: o.tensor_copy(
                        vnat[:], psT[0:128, 0:64]), waits=[("tensor", tv)], inc=1)
                    tkk = kb.emit("tensor", lambda o: o.matmul(
                        psA[:, 0:128], kst[:], kst[:], start=True, stop=True),
                        waits=[("vector", vst)], inc=1)
                    trb = kb.emit("tensor", lambda o, h=h: o.matmul(
                        psB[:, 0:128], hot[:, h * 128:(h + 1) * 128], ga[0:16, 384:512],
                        start=True, stop=True), inc=1)
                    trb2 = kb.emit("tensor", lambda o, h=h: o.matmul(
                        psC[:, 0:128], hot[:, h * 128:(h + 1) * 128], ga[0:16, 512:640],
                        start=True, stop=True), inc=1)
                    vv = kb.emit("vector", lambda o, h=h: o.tensor_scalar(
                        tmp1[:], psB[:, 0:128], ellc[:, 16 + h:17 + h], None,
                        op0=ALU.subtract), waits=[("tensor", trb)], inc=1)
                    vv = kb.emit("vector", lambda o: o.tensor_add(tmp1[:], tmp1[:], mA[:]), inc=1)
                    sv = kb.emit("scalar", lambda o: o.activation(
                        tmp1[:], tmp1[:], AF.Exp, scale=-1.0), waits=[("vector", vv)], inc=1)
                    vv = kb.emit("vector", lambda o: o.scalar_tensor_tensor(
                        Bbf[:], psA[:, 0:128], -1.0, tmp1[:], op0=ALU.mult, op1=ALU.mult),
                        waits=[("scalar", sv), ("tensor", tkk)], inc=1)
                    vv = kb.emit("vector", lambda o, h=h: o.tensor_scalar(
                        tmp2[:], psC[:, 0:128], ellc[:, h:h + 1], None,
                        op0=ALU.subtract), waits=[("tensor", trb2)], inc=1)
                    vv = kb.emit("vector", lambda o: o.tensor_add(tmp2[:], tmp2[:], mAp[:]), inc=1)
                    sv = kb.emit("scalar", lambda o: o.activation(
                        tmp2[:], tmp2[:], AF.Exp, scale=-1.0) if False else o.activation(
                        tmp2[:], tmp2[:], AF.Exp, scale=1.0), waits=[("vector", vv)], inc=1)
                    vv = kb.emit("vector", lambda o: o.scalar_tensor_tensor(
                        Bpbf[:], psA[:, 0:128], -1.0, tmp2[:], op0=ALU.mult, op1=ALU.mult),
                        waits=[("scalar", sv)], inc=1)
                    vb_built = vv
                    tks = kb.emit("tensor", lambda o, Scurb=Scurb: o.matmul(
                        psD[:, 0:64], kst[:], Scurb, start=True, stop=True),
                        waits=[("vector", vb_built)], inc=1)
                    vv = kb.emit("vector", lambda o, h=h: o.tensor_scalar(
                        tmp3[:, 64:128] if False else tmp1[:, 0:64], psD[:, 0:64],
                        ellc[:, 32 + h:33 + h], None, op0=ALU.mult),
                        waits=[("tensor", tks)], inc=1)
                    vv = kb.emit("vector", lambda o: o.tensor_tensor(
                        tmp1[:, 0:64], vnat[:], tmp1[:, 0:64], op=ALU.subtract), inc=1)
                    vv = kb.emit("vector", lambda o, h=h: o.tensor_scalar(
                        Ybf[:], tmp1[:, 0:64], ellc[:, 48 + h:49 + h], None,
                        op0=ALU.mult), inc=1)
                    tv = kb.emit("tensor", lambda o: o.matmul(
                        psD[:, 0:64], Bpbf[:], Ybf[:], start=True, stop=True),
                        waits=[("vector", vv)], inc=1)
                    vv = kb.emit("vector", lambda o: o.tensor_tensor(
                        Ybf[:], Ybf[:], psD[:, 0:64], op=ALU.add),
                        waits=[("tensor", tv)], inc=1)
                    tv = kb.emit("tensor", lambda o: o.matmul(
                        psA[:, 0:128], Bpbf[:], Bbf[:], start=True, stop=True),
                        waits=[("vector", vv)], inc=1)
                    vg = kb.emit("vector", lambda o: o.tensor_copy(Gbf[:], psA[:, 0:128]),
                                 waits=[("tensor", tv)], inc=1)
                    tv = kb.emit("tensor", lambda o: o.matmul(
                        psB[:, 0:128], Bbf[:], Bpbf[:], start=True, stop=True), inc=1)
                    vg = kb.emit("vector", lambda o: o.tensor_copy(Gpbf[:], psB[:, 0:128]),
                                 waits=[("tensor", tv)], inc=1)
                    gcur, gpcur, gnxt, gpnxt = Gbf, Gpbf, G2bf, G2pbf
                    for m in range(1, 7):
                        tv = kb.emit("tensor", lambda o, gpcur=gpcur: o.matmul(
                            psD[:, 0:64], gpcur[:], Ybf[:], start=True, stop=True),
                            waits=[("vector", vg)], inc=1)
                        vv = kb.emit("vector", lambda o: o.tensor_tensor(
                            Ybf[:], Ybf[:], psD[:, 0:64], op=ALU.add),
                            waits=[("tensor", tv)], inc=1)
                        if m < 6:
                            tv = kb.emit("tensor", lambda o, gpcur=gpcur, gcur=gcur:
                                o.matmul(psA[:, 0:128], gpcur[:], gcur[:],
                                         start=True, stop=True),
                                waits=[("vector", vv)], inc=1)
                            vg = kb.emit("vector", lambda o, gnxt=gnxt: o.tensor_copy(
                                gnxt[:], psA[:, 0:128]), waits=[("tensor", tv)], inc=1)
                            tv = kb.emit("tensor", lambda o, gcur=gcur, gpcur=gpcur:
                                o.matmul(psB[:, 0:128], gcur[:], gpcur[:],
                                         start=True, stop=True), inc=1)
                            vg = kb.emit("vector", lambda o, gpnxt=gpnxt: o.tensor_copy(
                                gpnxt[:], psB[:, 0:128]), waits=[("tensor", tv)], inc=1)
                            gcur, gpcur, gnxt, gpnxt = gnxt, gpnxt, gcur, gpcur
                        else:
                            vg = vv
                    tv = kb.emit("tensor", lambda o: o.transpose(
                        psT[0:64, 0:128], Ybf[:], cst.ident_bf),
                        waits=[("vector", vg)], inc=1)
                    vv = kb.emit("vector", lambda o: o.tensor_copy(
                        Ytbf[:], psT[0:64, 0:128]), waits=[("tensor", tv)], inc=1)
                    tv = kb.emit("tensor", lambda o: o.matmul(
                        psC[:, 0:128], Ytbf[:], qst[:], start=True, stop=True),
                        waits=[("vector", vv)], inc=1)
                    vv = kb.emit("vector", lambda o, h=h: o.tensor_scalar(
                        tmp1[:], psB[:, 0:128] if False else tmp1[:], 0.0, None, op0=ALU.mult),
                        inc=1) if False else 0
                    trb = kb.emit("tensor", lambda o, h=h: o.matmul(
                        psB[:, 0:128], hot[:, h * 128:(h + 1) * 128], ga[0:16, 384:512],
                        start=True, stop=True), waits=[("vector", vg)], inc=1)
                    vv = kb.emit("vector", lambda o, h=h: o.tensor_scalar(
                        tmp1[:], psB[:, 0:128], ellc[:, h:h + 1], None, op0=ALU.subtract),
                        waits=[("tensor", trb)], inc=1)
                    vv = kb.emit("vector", lambda o: o.tensor_add(tmp1[:], tmp1[:], mD[:]), inc=1)
                    sv = kb.emit("scalar", lambda o: o.activation(
                        tmp1[:], tmp1[:], AF.Exp), waits=[("vector", vv)], inc=1)
                    vv = kb.emit("vector", lambda o: o.tensor_tensor(
                        P2bf[:], tmp1[:], psC[:, 0:128], op=ALU.mult),
                        waits=[("scalar", sv), ("tensor", tv)], inc=1)
                    tv = kb.emit("tensor", lambda o: o.matmul(
                        psD[:, 0:64], P2bf[:], knat[:], start=True, stop=True),
                        waits=[("vector", vv)], inc=1)
                    tv2 = kb.emit("tensor", lambda o, Stcurb=Stcurb: o.matmul(
                        psC[:, 0:64], qst[:], Stcurb, start=True, stop=True),
                        waits=[("vector", vv)], inc=1)
                    vv = kb.emit("vector", lambda o, h=h: o.tensor_scalar(
                        tmp1[:, 0:64], psC[:, 0:64], ellc[:, 32 + h:33 + h], None,
                        op0=ALU.mult), waits=[("tensor", tv2)], inc=1)
                    vv = kb.emit("vector", lambda o: o.tensor_tensor(
                        tmp1[:, 0:64], tmp1[:, 0:64], psD[:, 0:64], op=ALU.add), inc=1)
                    vv = kb.emit("vector", lambda o: o.tensor_copy(
                        tmp2.bitcast(BF16)[:, 0:64], tmp1[:, 0:64]), inc=1)
                    tv = kb.emit("tensor", lambda o: o.transpose(
                        psT[0:64, 0:128], tmp2.bitcast(BF16)[:, 0:64], cst.ident_bf),
                        waits=[("vector", vv)], inc=1)
                    vv = kb.emit("vector", lambda o: o.tensor_copy(och[:], psT[0:64, 0:128]),
                                 waits=[("tensor", tv)], inc=1)
                    sd = kb.emit("sync", lambda o, h=h, t0=t0: o.dma_start(
                        out=otmpG[h * 64:(h + 1) * 64, t0:t0 + CH], in_=och[:]),
                        waits=[("vector", vv)], inc=16)
                    vv = kb.emit("vector", lambda o, h=h: o.tensor_scalar(
                        Ysbf[:], Ybf[:], ellc[:, 64 + h:65 + h], None,
                        op0=ALU.mult), inc=1)
                    tv = kb.emit("tensor", lambda o: o.matmul(
                        psD[0:64, 0:64], knat[:], Ysbf[:], start=True, stop=True),
                        waits=[("vector", vv)], inc=1)
                    tv2 = kb.emit("tensor", lambda o: o.matmul(
                        psC[0:64, 0:64], Ysbf[:], knat[:], start=True, stop=True), inc=1)
                    vv = kb.emit("vector", lambda o, Scur=Scur, h=h: o.scalar_tensor_tensor(
                        Scur, Scur, ellc[0:64, 80 + h:81 + h], psD[0:64, 0:64],
                        op0=ALU.mult, op1=ALU.add), waits=[("tensor", tv)], inc=1)
                    vv = kb.emit("vector", lambda o, Stcur=Stcur, h=h: o.scalar_tensor_tensor(
                        Stcur, Stcur, ellc[0:64, 80 + h:81 + h], psC[0:64, 0:64],
                        op0=ALU.mult, op1=ALU.add), waits=[("tensor", tv2)], inc=1)
                    vv = kb.emit("vector", lambda o, Scur=Scur, Scurb=Scurb: o.tensor_copy(
                        Scurb, Scur), inc=1)
                    vv = kb.emit("vector", lambda o, Stcur=Stcur, Stcurb=Stcurb: o.tensor_copy(
                        Stcurb, Stcur), inc=1)
                    kb.wait("tensor", [("vector", vv), ("sync", sd)])
        kb.flush()
    with contextlib.ExitStack() as ctx:
        wo_sb = _mk(ctx, nc, "s", "pg_wo", [128, 8 * 1024], BF16)
        oin = _mk(ctx, nc, "s", "pg_oin", [128, 8 * 512], BF16)
        borow = _mk(ctx, nc, "s", "pg_borow", [1, D], F32)
        bocol = _mk(ctx, nc, "s", "pg_bocol", [128, NF], F32)
        sev = _mk(ctx, nc, "s", "pg_sev", [128, 512], F32)
        ps4 = _mk(ctx, nc, "p", "pg_ps4", [128, 512], F32)
        ps5 = _mk(ctx, nc, "p", "pg_ps5", [128, 512], F32)
        gw = 0
        for kc in range(NF):
            gw = kb.emit("gq", lambda o, kc=kc: o.dma_start(
                out=wo_sb[:, kc * 1024:(kc + 1) * 1024],
                in_=woT[kc * 128:(kc + 1) * 128, :]), inc=16)
        ld = kb.emit("sync", lambda o: o.dma_start(out=borow[0:1, :], in_=bo), inc=16)
        kb.wait("tensor", [("sync", ld)])
        kb.wait("vector", [("sync", ld)])
        col_bias(kb, nc, cst, borow, 0, bocol, 0, ps4)
        vlast = kb.val("vector")
        for qc in range(8):
            t0 = qc * 512
            ldo = 0
            for kc in range(NF):
                ldo = kb.emit("sync", lambda o, kc=kc, t0=t0: o.dma_start(
                    out=oin[:, kc * 512:(kc + 1) * 512],
                    in_=otmpG[kc * 128:(kc + 1) * 128, t0:t0 + 512]),
                    waits=[("tensor", kb.val("tensor"))] if kc == 0 else [], inc=16)
            for ft in range(NF):
                ps = (ps4, ps5)[ft % 2]
                tv = 0
                for kc in range(NF):
                    tv = kb.emit("tensor", lambda o, ft=ft, kc=kc, ps=ps: o.matmul(
                        ps[:, 0:512],
                        wo_sb[:, kc * 1024 + ft * 128:kc * 1024 + (ft + 1) * 128],
                        oin[:, kc * 512:(kc + 1) * 512],
                        start=(kc == 0), stop=(kc == NF - 1)),
                        waits=([("sync", ldo), ("gq", gw), ("vector", vlast)]
                               if kc == 0 else []), inc=1)
                vv = kb.emit("vector", lambda o, ft=ft, ps=ps: o.tensor_scalar(
                    sev[:], ps[:, 0:512], bocol[:, ft:ft + 1], None, op0=ALU.add),
                    waits=[("tensor", tv)], inc=1)
                sd = kb.emit("sync", lambda o, ft=ft, t0=t0: o.dma_start(
                    out=s1T[ft * 128:(ft + 1) * 128, t0:t0 + 512], in_=sev[:]),
                    waits=[("vector", vv)], inc=16)
                vlast = vv
                kb.wait("vector", [("sync", sd)])
        kb.flush()



def ph_h(kb, nc, cst, xn1T_bf, x1T, s1T, wgT, wuT, wdT, out_nat):
    SC = 1.0 / 127.0
    with contextlib.ExitStack() as ctx:
        wbig = _mk(ctx, nc, "s", "ph_wbig", [128, 8 * 3072], BF16)
        wd_sb = _mk(ctx, nc, "s", "ph_wd", [128, 24 * 1024], BF16)
        xn_r = _mk(ctx, nc, "s", "ph_xn", [128, 8 * 512], BF16)
        h_sb = _mk(ctx, nc, "s", "ph_h", [128, 24 * 512], BF16)
        x1c = _mk(ctx, nc, "s", "ph_x1", [128, 512], F32)
        s1c = _mk(ctx, nc, "s", "ph_s1", [128, 512], F32)
        ev = _mk(ctx, nc, "s", "ph_ev", [128, 512], F32)
        evb = _mk(ctx, nc, "s", "ph_evb", [128, 512], F32)
        ps1 = _mk(ctx, nc, "p", "ph_ps1", [128, 512], F32)
        ps2 = _mk(ctx, nc, "p", "ph_ps2", [128, 512], F32)
        gwd = 0
        for kc in range(24):
            gwd = kb.emit("gq", lambda o, kc=kc: o.dma_start(
                out=wd_sb[:, kc * 1024:(kc + 1) * 1024],
                in_=wdT[kc * 128:(kc + 1) * 128, :]), inc=16)
        for ch in range(8):
            t0 = ch * 512
            ldx = 0
            for kc in range(NF):
                ldx = kb.emit("sync", lambda o, kc=kc, t0=t0: o.dma_start(
                    out=xn_r[:, kc * 512:(kc + 1) * 512],
                    in_=xn1T_bf[kc * 128:(kc + 1) * 128, t0:t0 + 512]),
                    waits=[("tensor", kb.val("tensor")), ("scalar", kb.val("scalar")),
                           ("vector", kb.val("vector"))] if kc == 0 else [], inc=16)
            for phase in (0, 1):
                gw = 0
                for kc in range(NF):
                    gw = kb.emit("gq", lambda o, kc=kc, phase=phase: o.dma_start(
                        out=wbig[:, kc * 3072:(kc + 1) * 3072],
                        in_=(wgT if phase == 0 else wuT)[kc * 128:(kc + 1) * 128, :]),
                        waits=[("tensor", kb.val("tensor"))] if kc == 0 else [], inc=16)
                last = {0: 0, 1: 0}
                for f2 in range(24):
                    ps = (ps1, ps2)[f2 % 2]
                    tv = 0
                    for kc in range(NF):
                        tv = kb.emit("tensor", lambda o, f2=f2, kc=kc, ps=ps: o.matmul(
                            ps[:, 0:512],
                            wbig[:, kc * 3072 + f2 * 128:kc * 3072 + (f2 + 1) * 128],
                            xn_r[:, kc * 512:(kc + 1) * 512],
                            start=(kc == 0), stop=(kc == NF - 1)),
                            waits=[("sync", ldx), ("gq", gw),
                                   ("scalar", last[f2 % 2]) if phase == 0 else
                                   ("vector", last[f2 % 2])] if kc == 0 else [], inc=1)
                    if phase == 0:
                        sv = kb.emit("scalar", lambda o, f2=f2, ps=ps: o.activation(
                            h_sb[:, f2 * 512:(f2 + 1) * 512], ps[:, 0:512], AF.Silu,
                            scale=SC), waits=[("tensor", tv)], inc=1)
                        last[f2 % 2] = sv
                    else:
                        vv = kb.emit("vector", lambda o, f2=f2, ps=ps: o.scalar_tensor_tensor(
                            h_sb[:, f2 * 512:(f2 + 1) * 512], ps[:, 0:512], SC,
                            h_sb[:, f2 * 512:(f2 + 1) * 512], op0=ALU.mult, op1=ALU.mult),
                            waits=[("tensor", tv)], inc=1)
                        last[f2 % 2] = vv
                if phase == 0:
                    kb.wait("vector", [("scalar", max(last[0], last[1]))])
                    last = {0: kb.val("vector"), 1: kb.val("vector")}
            ld2 = kb.emit("sync", lambda o, t0=t0: o.dma_start(
                out=x1c[:], in_=x1T[0:128, t0:t0 + 512]), inc=16) if False else 0
            vlast = {0: kb.val("vector"), 1: kb.val("vector")}
            for ft in range(NF):
                ps = (ps1, ps2)[ft % 2]
                ld2 = kb.emit("sync", lambda o, ft=ft, t0=t0: o.dma_start(
                    out=x1c[:], in_=x1T[ft * 128:(ft + 1) * 128, t0:t0 + 512]),
                    waits=[("vector", vlast[ft % 2])], inc=16)
                ld2 = kb.emit("sync", lambda o, ft=ft, t0=t0: o.dma_start(
                    out=s1c[:], in_=s1T[ft * 128:(ft + 1) * 128, t0:t0 + 512]), inc=16)
                tv = 0
                for kc in range(24):
                    tv = kb.emit("tensor", lambda o, ft=ft, kc=kc, ps=ps: o.matmul(
                        ps[:, 0:512],
                        wd_sb[:, kc * 1024 + ft * 128:kc * 1024 + (ft + 1) * 128],
                        h_sb[:, kc * 512:(kc + 1) * 512],
                        start=(kc == 0), stop=(kc == 23)),
                        waits=[("gq", gwd), ("vector", vlast[ft % 2])] if kc == 0 else [],
                        inc=1)
                sv = kb.emit("scalar", lambda o, ps=ps: o.activation(
                    ev[:], ps[:, 0:512], AF.Silu, scale=SC), waits=[("tensor", tv)], inc=1)
                vv = kb.emit("vector", lambda o: o.tensor_mul(ev[:], ev[:], s1c[:]),
                             waits=[("scalar", sv), ("sync", ld2)], inc=1)
                vv = kb.emit("vector", lambda o: o.tensor_add(evb[:], x1c[:], ev[:]), inc=1)
                for tt in range(4):
                    tvt = kb.emit("tensor", lambda o, tt=tt, ft=ft: o.transpose(
                        ps1[:, 0:128] if ft % 2 else ps2[:, 0:128],
                        evb[:, tt * 128:(tt + 1) * 128], cst.ident),
                        waits=[("vector", vv)], inc=1)
                    vv2 = kb.emit("vector", lambda o, tt=tt, ft=ft: o.tensor_copy(
                        ev[:, 0:128], (ps1 if ft % 2 else ps2)[:, 0:128]),
                        waits=[("tensor", tvt)], inc=1)
                    sd = kb.emit("sync", lambda o, tt=tt, ft=ft, t0=t0: o.dma_start(
                        out=out_nat[t0 + tt * 128:t0 + (tt + 1) * 128,
                                    ft * 128:(ft + 1) * 128],
                        in_=ev[:, 0:128]), waits=[("vector", vv2)], inc=16)
                    kb.wait("tensor", [("sync", sd)])
                    kb.wait("vector", [("sync", sd)])
                vlast[ft % 2] = kb.val("vector")
        kb.flush()


def _t(nc, n, s, dt, k):
    return nc.dram_tensor(n, s, dt, kind=k).ap()


def _host_prep(inputs):
    import ml_dtypes
    z = inputs
    hp = {}
    T = lambda a: np.ascontiguousarray(np.asarray(a).T)
    hp["b_wqT"] = T(z["L0_attn_Wq"]); hp["b_wkT"] = T(z["L0_attn_Wk"])
    hp["b_wvT"] = T(z["L0_attn_Wv"]); hp["b_woT"] = T(z["L0_attn_Wo"])
    for nm in ("bq", "bk", "bv", "bo", "gamma", "beta"):
        hp["b_" + nm] = np.asarray(z["L0_attn_" + nm]).reshape(1, -1)
    hp["b_relb"] = np.asarray(z["L0_attn_rel_bias"])
    hp["b_relbT"] = np.ascontiguousarray(hp["b_relb"].transpose(0, 2, 1))
    pos = np.arange(S, dtype=np.float64) * ((32 - 1) / (S - 1))
    i0 = np.floor(pos).astype(np.int64)
    i1 = np.minimum(i0 + 1, 31)
    fr = pos - i0
    M = np.zeros((S, 32))
    M[np.arange(S), i0] += 1.0 - fr
    np.add.at(M, (np.arange(S), i1), fr)
    M[-1, :] = 0.0
    hp["b_mt"] = np.ascontiguousarray(M.T.astype(np.float32))
    m1 = np.zeros((4, 128, 512), np.float32)
    m2 = np.zeros((4, 128, 512), np.float32)
    qi = np.arange(128)[:, None]; kj = np.arange(512)[None, :]
    for p in range(4):
        m1[p] = np.where(kj <= 128 * p + qi, 0.0, -1e30)
        m2[p] = np.where(kj >= 128 * p + qi, 0.0, -1e30)
    hp["b_mask1"] = m1; hp["b_mask2"] = m2
    hp["c_qw"] = np.asarray(z["L0_ffn_qw"]); hp["c_kw"] = np.asarray(z["L0_ffn_kw"])
    hp["c_vw"] = np.asarray(z["L0_ffn_vw"]); hp["c_ow"] = np.asarray(z["L0_ffn_ow"])
    for nm in ("qb", "kb", "vb", "ob"):
        hp["c_" + nm] = np.asarray(z["L0_ffn_" + nm]).reshape(1, -1)
    hp["c_n1w"] = np.asarray(z["L0_ffn_norm1_w"]).reshape(1, -1)
    hp["f_n2w"] = np.asarray(z["L0_ffn_norm2_w"]).reshape(1, -1)
    hp["d_gw"] = np.asarray(z["L0_ffn_gate_w"])
    hp["e_w1"] = np.asarray(z["L0_ffn_w1"]).reshape(8 * 1024, 2048)
    hp["e_w2"] = np.asarray(z["L0_ffn_w2"]).reshape(8 * 2048, 1024)
    hp["g_wqT"] = T(z["L1_attn_Wq"]); hp["g_wkT"] = T(z["L1_attn_Wk"])
    hp["g_wvT"] = T(z["L1_attn_Wv"]); hp["g_woT"] = T(z["L1_attn_Wo"])
    hp["g_waT"] = T(z["L1_attn_Wa"]); hp["g_wbT"] = T(z["L1_attn_Wb"])
    for nm in ("bq", "bk", "bv", "bo"):
        hp["g_" + nm] = np.asarray(z["L1_attn_" + nm]).reshape(1, -1)
    hp["g_ba"] = np.asarray(z["L1_attn_ba"]).reshape(1, -1)
    hp["g_bb"] = np.asarray(z["L1_attn_bb"]).reshape(1, -1)
    for cn, key in (("cq", "L1_attn_conv_q"), ("ck", "L1_attn_conv_k"),
                    ("cv", "L1_attn_conv_v")):
        cw = np.asarray(z[key])[:, 0, :]
        arr = np.zeros((128, NF * 4), np.float32)
        for f in range(NF):
            arr[:, f * 4:(f + 1) * 4] = cw[f * 128:(f + 1) * 128, :]
        hp["g_" + cn] = arr
    hot = np.zeros((16, 16 * 128), np.float32)
    for h in range(16):
        hot[h, h * 128:(h + 1) * 128] = 1.0
    hp["g_hot16"] = hot.astype(np.float32)
    t_ = np.arange(128)[:, None]; s_ = np.arange(128)[None, :]
    hp["g_mGA"] = np.where(s_ >= t_, 1e30, 0.0).astype(np.float32)
    hp["g_mGAp"] = np.where(s_ <= t_, -1e30, 0.0).astype(np.float32)
    hp["g_mGD"] = np.where(s_ > t_, 0.0, 0.0).astype(np.float32)
    hp["g_mGD"] = np.where(t_ > s_, 0.0, 0.0).astype(np.float32)
    hp["g_mGD"] = np.where(s_ < t_, -1e30, 0.0).astype(np.float32)
    hp["h_wgT"] = T(z["L1_ffn_w_gate"]); hp["h_wuT"] = T(z["L1_ffn_w_up"])
    hp["h_wdT"] = T(z["L1_ffn_w_down"])
    return hp



def kernel(**inputs):
    import ml_dtypes
    hp = _host_prep(inputs)
    ca = make_const_arrays()
    x_full = np.asarray(inputs["L0_attn_x"])
    NCOREL = list(range(N_CORES))

    def run(build_fn, per_core_ins, shared_ins):
        nc = build_fn()
        in_maps = [{**per_core_ins[c], **shared_ins, **ca} for c in range(N_CORES)]
        return run_bass_kernel_spmd(nc, in_maps, core_ids=NCOREL).results

    def mk_nc(scr=16384):
        return bass.Bass("TRN2", target_bir_lowering=False, debug=False,
                         dynamic_dma_scratch_size=scr)

    def bL1():
        nc = mk_nc()
        x = _t(nc, "x_nat", [NTOK, D], F32, "ExternalInput")
        g = _t(nc, "gamma", [1, D], F32, "ExternalInput")
        be = _t(nc, "beta", [1, D], F32, "ExternalInput")
        xT = _t(nc, "xT", [D, NTOK], F32, "ExternalOutput")
        xnT = _t(nc, "xnT", [D, NTOK], F32, "Internal")
        xnT_bf = _t(nc, "xnT_bf", [D, NTOK], BF16, "Internal")
        xn2T_bf = _t(nc, "xn2T_bf", [D, NTOK], BF16, "Internal")
        argsB = [_t(nc, n, [D, D], F32, "ExternalInput") for n in ("wq", "wk", "wv", "wo")]
        bsB = [_t(nc, n, [1, D], F32, "ExternalInput") for n in ("bq", "bk", "bv", "bo")]
        relb = _t(nc, "relb", [16, 32, 32], F32, "ExternalInput")
        relbT = _t(nc, "relbT", [16, 32, 32], F32, "ExternalInput")
        mt = _t(nc, "mt", [32, S], F32, "ExternalInput")
        mask1 = _t(nc, "mask1", [4, 128, 512], F32, "ExternalInput")
        mask2 = _t(nc, "mask2", [4, 128, 512], F32, "ExternalInput")
        s0T = _t(nc, "s0T", [D, NTOK], F32, "ExternalOutput")
        otmp = _t(nc, "otmpB", [D, S], BF16, "Internal")
        rowtmp = _t(nc, "rowtmpB", [16, 128], F32, "Internal")
        qw = _t(nc, "qw", [D, D], F32, "ExternalInput")
        kw = _t(nc, "kw", [D, 256], F32, "ExternalInput")
        vw = _t(nc, "vw", [D, 256], F32, "ExternalInput")
        ow = _t(nc, "ow", [D, D], F32, "ExternalInput")
        qb = _t(nc, "qb", [1, D], F32, "ExternalInput")
        kbi = _t(nc, "kb", [1, 256], F32, "ExternalInput")
        vb = _t(nc, "vb", [1, 256], F32, "ExternalInput")
        ob = _t(nc, "ob", [1, D], F32, "ExternalInput")
        n1w = _t(nc, "n1w", [1, D], F32, "ExternalInput")
        src1T = _t(nc, "src1T", [D, NTOK], F32, "ExternalOutput")
        src1T_bf = _t(nc, "src1T_bf", [D, NTOK], BF16, "Internal")
        src1nat = _t(nc, "src1nat", [NTOK + 128, D], BF16, "ExternalOutput")
        otC = _t(nc, "otmpC", [D, S], BF16, "Internal")
        gw = _t(nc, "gate_w", [D, 8], F32, "ExternalInput")
        cnt = _t(nc, "cnt", [1, 8], F32, "ExternalOutput")
        sel = _t(nc, "sel12", [NTOK, 2], F32, "ExternalOutput")
        wts = _t(nc, "wts12", [NTOK, 2], F32, "ExternalOutput")
        ci = _t(nc, "cincl", [8, NTOK], F32, "ExternalOutput")
        kb = KB(nc); cst = Consts(kb, nc)
        kb.xser = False
        ph_a(kb, nc, cst, x, g, be, xT, xnT, xnT_bf, xn2T_bf)
        ph_b(kb, nc, cst, xn2T_bf, *argsB, *bsB, relb, relbT, mt, mask1, mask2,
             s0T, otmp, rowtmp)
        ph_c(kb, nc, cst, xnT_bf, xnT, qw, kw, vw, ow, qb, kbi, vb, ob, n1w,
             src1T, src1T_bf, src1nat, otC)
        ph_d_count(kb, nc, cst, src1T, gw, cnt, sel, wts, ci)
        kb.drain(); kb.flush()
        return nc
    pc = [{"x_nat": np.ascontiguousarray(x_full[2 * c:2 * c + 2].reshape(NTOK, D))}
          for c in range(N_CORES)]
    r1 = run(bL1, pc, {"gamma": hp["b_gamma"], "beta": hp["b_beta"],
                       "wq": hp["b_wqT"], "wk": hp["b_wkT"], "wv": hp["b_wvT"],
                       "wo": hp["b_woT"], "bq": hp["b_bq"], "bk": hp["b_bk"],
                       "bv": hp["b_bv"], "bo": hp["b_bo"], "relb": hp["b_relb"],
                       "relbT": hp["b_relbT"], "mt": hp["b_mt"],
                       "mask1": hp["b_mask1"], "mask2": hp["b_mask2"],
                       "qw": hp["c_qw"], "kw": hp["c_kw"], "vw": hp["c_vw"],
                       "ow": hp["c_ow"], "qb": hp["c_qb"], "kb": hp["c_kb"],
                       "vb": hp["c_vb"], "ob": hp["c_ob"], "n1w": hp["c_n1w"],
                       "gate_w": hp["d_gw"]})
    allcnt = np.stack([r1[c]["cnt"][0] for c in range(N_CORES)]).astype(np.float32)

    def bL2():
        nc = mk_nc(49152)
        ci = _t(nc, "cincl", [8, NTOK], F32, "ExternalInput")
        sel = _t(nc, "sel12", [NTOK, 2], F32, "ExternalInput")
        wts = _t(nc, "wts12", [NTOK, 2], F32, "ExternalInput")
        ac = _t(nc, "allcnt", [8, 8], F32, "ExternalInput")
        mc = _t(nc, "maskcol", [8, 1], F32, "ExternalInput")
        w1f = _t(nc, "w1f", [8 * 1024, 2048], F32, "ExternalInput")
        w2f = _t(nc, "w2f", [8 * 2048, 1024], F32, "ExternalInput")
        il = _t(nc, "ilist2", [NSLOT * 512], I32, "Internal")
        wl = _t(nc, "wlist2", [NSLOT * 512], I32, "Internal")
        i1 = _t(nc, "idxw1", [NSLOT, 128], I32, "Internal")
        i2 = _t(nc, "idxw2", [NSLOT, 128], I32, "Internal")
        w1b = _t(nc, "w1b", [8 * 1024, 2048], BF16, "Internal")
        w2b = _t(nc, "w2b", [8 * 2048, 1024], BF16, "Internal")
        ys = _t(nc, "ysum", [NTOK + 128, D], F32, "Internal")
        sn = _t(nc, "src1nat", [NTOK + 128, D], BF16, "ExternalInput")
        xT = _t(nc, "xT", [D, NTOK], F32, "ExternalInput")
        s0 = _t(nc, "s0T", [D, NTOK], F32, "ExternalInput")
        s1 = _t(nc, "src1T", [D, NTOK], F32, "ExternalInput")
        n2 = _t(nc, "n2w", [1, D], F32, "ExternalInput")
        x1 = _t(nc, "x1T", [D, NTOK], F32, "Internal")
        xn1 = _t(nc, "xn1T", [D, NTOK], F32, "Internal")
        xn1b = _t(nc, "xn1T_bf", [D, NTOK], BF16, "Internal")
        wq = _t(nc, "wq1", [D, D], F32, "ExternalInput")
        wk = _t(nc, "wk1", [D, D], F32, "ExternalInput")
        wv = _t(nc, "wv1", [D, D], F32, "ExternalInput")
        bq = _t(nc, "bq1", [1, D], F32, "ExternalInput")
        bk2 = _t(nc, "bk1", [1, D], F32, "ExternalInput")
        bv = _t(nc, "bv1", [1, D], F32, "ExternalInput")
        waT = _t(nc, "waT", [D, 16], F32, "ExternalInput")
        wbT = _t(nc, "wbT", [D, 16], F32, "ExternalInput")
        ba = _t(nc, "ba", [1, 16], F32, "ExternalInput")
        bb2 = _t(nc, "bb", [1, 16], F32, "ExternalInput")
        cq = _t(nc, "convq", [128, NF * 4], F32, "ExternalInput")
        ck = _t(nc, "convk", [128, NF * 4], F32, "ExternalInput")
        cv2 = _t(nc, "convv", [128, NF * 4], F32, "ExternalInput")
        qT = _t(nc, "qTb", [D, NTOK], BF16, "Internal")
        kT = _t(nc, "kTb", [D, NTOK], BF16, "Internal")
        vT = _t(nc, "vTb", [D, NTOK], BF16, "Internal")
        la = _t(nc, "lnaT", [16, NTOK], F32, "Internal")
        bg = _t(nc, "bgT", [16, NTOK], F32, "Internal")
        lb = _t(nc, "lnbT", [16, NTOK], F32, "Internal")
        hot = _t(nc, "hot16", [16, 16 * 128], F32, "ExternalInput")
        mA = _t(nc, "mGA", [128, 128], F32, "ExternalInput")
        mAp = _t(nc, "mGAp", [128, 128], F32, "ExternalInput")
        mD = _t(nc, "mGD", [128, 128], F32, "ExternalInput")
        wo = _t(nc, "woT1", [D, D], F32, "ExternalInput")
        bo = _t(nc, "bo1", [1, D], F32, "ExternalInput")
        s1T = _t(nc, "s1T", [D, NTOK], F32, "Internal")
        og = _t(nc, "otmpG", [D, NTOK], BF16, "Internal")
        wg = _t(nc, "wgT", [D, 3072], I32, "ExternalInput")
        wu = _t(nc, "wuT", [D, 3072], I32, "ExternalInput")
        wd = _t(nc, "wdT", [3072, D], I32, "ExternalInput")
        on = _t(nc, "out_nat", [NTOK, D], F32, "ExternalOutput")
        kb = KB(nc); cst = Consts(kb, nc)
        kb.xser = False
        ph_d_route(kb, nc, cst, ci, sel, wts, ac, mc, il, wl, i1, i2)
        ph_e_prep(kb, nc, cst, w1f, w2f, w1b, w2b, ys)
        ph_e(kb, nc, cst, sn, w1b, w2b, il, wl, i1, i2, ys)
        kb.xser = True
        ph_f(kb, nc, cst, xT, s0, s1, ys, n2, x1, xn1, xn1b)
        ph_f2(kb, nc, cst, xn1, xn1b, wq, wk, wv, bq, bk2, bv, waT, wbT, ba, bb2,
              cq, ck, cv2, qT, kT, vT, la, bg, lb)
        ph_g(kb, nc, cst, qT, kT, vT, la, bg, lb, hot, mA, mAp, mD, wo, bo, s1T, og)
        ph_h(kb, nc, cst, xn1b, x1, s1T, wg, wu, wd, on)
        kb.drain(); kb.flush()
        return nc
    pc = []
    for c in range(N_CORES):
        mk = np.zeros((8, 1), np.float32); mk[:c] = 1.0
        pc.append({"cincl": r1[c]["cincl"], "sel12": r1[c]["sel12"],
                   "wts12": r1[c]["wts12"], "maskcol": mk,
                   "src1nat": r1[c]["src1nat"], "xT": r1[c]["xT"],
                   "s0T": r1[c]["s0T"], "src1T": r1[c]["src1T"]})
    r2 = run(bL2, pc, {"allcnt": allcnt, "w1f": hp["e_w1"], "w2f": hp["e_w2"],
                       "n2w": hp["f_n2w"], "wq1": hp["g_wqT"], "wk1": hp["g_wkT"],
                       "wv1": hp["g_wvT"], "bq1": hp["g_bq"], "bk1": hp["g_bk"],
                       "bv1": hp["g_bv"], "waT": hp["g_waT"], "wbT": hp["g_wbT"],
                       "ba": hp["g_ba"], "bb": hp["g_bb"], "convq": hp["g_cq"],
                       "convk": hp["g_ck"], "convv": hp["g_cv"],
                       "hot16": hp["g_hot16"], "mGA": hp["g_mGA"],
                       "mGAp": hp["g_mGAp"], "mGD": hp["g_mGD"],
                       "woT1": hp["g_woT"], "bo1": hp["g_bo"],
                       "wgT": hp["h_wgT"], "wuT": hp["h_wuT"], "wdT": hp["h_wdT"]})

    out = np.concatenate([r2[c]["out_nat"].reshape(2, S, D) for c in range(N_CORES)],
                         axis=0)
    return np.ascontiguousarray(out.astype(np.float32))
